# version 28
# baseline (speedup 1.0000x reference)
import numpy as np
import ml_dtypes
from contextlib import ExitStack

import concourse.bass as bass
import concourse.mybir as mybir
from concourse.bass_utils import run_bass_kernel_spmd

F32 = mybir.dt.float32
BF16 = mybir.dt.bfloat16
AF = mybir.ActivationFunctionType
ALU = mybir.AluOpType
AX = mybir.AxisListType

BIG = 1.0e30


class Cfg:
    def __init__(self, D=1024, DI=512, NE=16, L=2048, TT=512, NTAP=12, NST=64,
                 TOPK=4, EPS=1e-5):
        self.D, self.DI, self.NE, self.L, self.TT = D, DI, NE, L, TT
        self.NTAP, self.NST, self.TOPK, self.EPS = NTAP, NST, TOPK, EPS
        self.dbg = False
        self.KC = D // 128
        self.IC = DI // 128
        self.NT = L // TT
        self.LC = L // 128
        assert D % 128 == 0 and DI % 128 == 0 and L % TT == 0 and TT % 128 == 0
        assert self.KC % 2 == 0 and self.LC % 4 == 0


class B:
    def __init__(self, nc, cfg, ctx):
        self.nc = nc
        self.c = cfg
        names = ["dq", "dwq", "oq", "gc",
                 "tst", "tgu", "tw2", "tsc",
                 "vln", "vrow", "vau", "vev", "vmi", "vsc", "vtp", "vst",
                 "srow", "sst", "sact",
                 "gmi", "gact"]
        self.sem = {n: ctx.enter_context(nc.semaphore(n)) for n in names}
        self.cnt = {n: 0 for n in names}
        self.rel = {}

    def inc(self, instr, name, k=1):
        instr.then_inc(self.sem[name], k)
        self.cnt[name] += k
        return self.cnt[name]

    def val(self, name):
        return self.cnt[name]

    def wait(self, eng, name, v=None):
        v = self.cnt[name] if v is None else v
        if v > 0:
            eng.wait_ge(self.sem[name], v)

    def use(self, eng, res):
        for (n, v) in self.rel.get(res, []):
            if v > 0:
                eng.wait_ge(self.sem[n], v)

    def release(self, res, name, v):
        self.rel[res] = [(name, v)]

    def barrier_clear(self):
        self.rel = {}


def build_nc(c: Cfg):
    nc = bass.Bass("TRN2", target_bir_lowering=False)
    D, DI, NE, L, TT = c.D, c.DI, c.NE, c.L, c.TT
    KC, IC, NT, LC = c.KC, c.IC, c.NT, c.LC
    NTAP, NST = c.NTAP, c.NST
    HALF = D // 2

    xT = nc.dram_tensor("xT", [D, L], F32, kind="ExternalInput").ap()
    w1d = nc.dram_tensor("w1", [NE, D, DI], F32, kind="ExternalInput").ap()
    w3d = nc.dram_tensor("w3", [NE, D, DI], F32, kind="ExternalInput").ap()
    w2d = nc.dram_tensor("w2", [NE, DI, D], F32, kind="ExternalInput").ap()
    gateT = nc.dram_tensor("gateT", [D, NE], F32, kind="ExternalInput").ap()
    winT = nc.dram_tensor("winT", [D, D], F32, kind="ExternalInput").ap()
    bin_c = nc.dram_tensor("bin_c", [128, KC], F32, kind="ExternalInput").ap()
    gam_c = nc.dram_tensor("gam_c", [128, KC], F32, kind="ExternalInput").ap()
    bet_c = nc.dram_tensor("bet_c", [128, KC], F32, kind="ExternalInput").ap()
    dre_c = nc.dram_tensor("dre_c", [128, KC], F32, kind="ExternalInput").ap()
    abcold = nc.dram_tensor("abcol", [NST, 4], F32, kind="ExternalInput").ap()
    ctred = nc.dram_tensor("ctre", [NST, D], F32, kind="ExternalInput").ap()
    ctimd = nc.dram_tensor("ctim", [NST, D], F32, kind="ExternalInput").ap()
    onecd = nc.dram_tensor("onec_bf", [128, 1], BF16, kind="ExternalInput").ap()
    onerd = nc.dram_tensor("oner_bf", [1, 128], BF16, kind="ExternalInput").ap()
    onerfd = nc.dram_tensor("oner_f", [1, 128], F32, kind="ExternalInput").ap()
    identd = nc.dram_tensor("identd", [128, 128], BF16, kind="ExternalInput").ap()
    outd = nc.dram_tensor("out", [D, L], F32, kind="ExternalOutput").ap()
    if c.dbg:
        d_xhat = nc.dram_tensor("d_xhat", [128, KC, L], BF16, kind="ExternalOutput").ap()
        d_sc = nc.dram_tensor("d_sc", [NE, L], BF16, kind="ExternalOutput").ap()
        d_R1 = nc.dram_tensor("d_R1", [128, KC, L], F32, kind="ExternalOutput").ap()
        d_x3 = nc.dram_tensor("d_x3", [128, KC, L], BF16, kind="ExternalOutput").ap()
        d_R2 = nc.dram_tensor("d_R2", [128, KC, L], F32, kind="ExternalOutput").ap()
        d_k0 = nc.dram_tensor("d_k0", [128, KC, NTAP], F32, kind="ExternalOutput").ap()
        d_x1 = nc.dram_tensor("d_x1", [128, KC, TT], BF16, kind="ExternalOutput").ap()
        d_x2 = nc.dram_tensor("d_x2", [128, KC, TT], F32, kind="ExternalOutput").ap()

    top = ExitStack()
    with top:
        S = B(nc, c, top)
        sb = lambda name, shp, dt: top.enter_context(nc.sbuf_tensor(name, shp, dt))

        R = sb("R", [128, KC, L], F32)
        WSZ = 2 * KC * DI + IC * D
        OW1, OW3, OW2 = 0, KC * DI, 2 * KC * DI
        LSZ = max(KC * DI, IC * D, KC * HALF)
        gatebf = sb("gatebf", [128, KC, NE], BF16)
        prm = sb("prm", [128, 4 * KC], F32)
        acol = sb("acol", [NST, 4], F32)
        k0sb = sb("k0sb", [128, KC, NTAP], F32)
        onec = sb("onec", [128, 1], BF16)
        oner2 = sb("oner2", [1, 128], BF16)
        onerf = sb("onerf", [1, 128], F32)
        idn = sb("idn", [128, 128], BF16)
        xbf_s = sb("xbf_s", [128, 2, TT], BF16)
        xsq_s = sb("xsq_s", [128, 2, TT], BF16)
        lnt = sb("lnt", [128, 2, TT], F32)
        start_mem = ExitStack()
        ssb = lambda name, shp, dt: start_mem.enter_context(
            nc.sbuf_tensor(name, shp, dt))
        gland = ssb("gland", [128, KC, NE], F32)
        ctbf = ssb("ctbf", [NST, 2 * D], BF16)
        ctl = ssb("ctl", [NST, D], F32)
        ajb = ssb("ajb", [NST, 2 * NTAP], F32)
        ajbbf = ssb("ajbbf", [NST, 2 * NTAP], BF16)
        scr2 = ssb("scr2", [NST, 2], F32)

        with nc.Block() as blk:
            @blk.sync
            def _(s):
                for kc in range(KC):
                    S.inc(s.dma_start(R[:, kc, :],
                                      xT.rearrange("(c p) l -> c p l", p=128)[kc]),
                          "dq", 16)
                S.inc(s.dma_start(gland[:, :, :],
                                  gateT.rearrange("(c p) e -> p c e", p=128)), "dq", 16)
                S.inc(s.dma_start(prm[:, 0 * KC:1 * KC], bin_c), "dq", 16)
                S.inc(s.dma_start(prm[:, 1 * KC:2 * KC], gam_c), "dq", 16)
                S.inc(s.dma_start(prm[:, 2 * KC:3 * KC], bet_c), "dq", 16)
                S.inc(s.dma_start(prm[:, 3 * KC:4 * KC], dre_c), "dq", 16)
                S.inc(s.dma_start(acol[:, :], abcold), "dq", 16)
                S.inc(s.dma_start(onec[:, :], onecd), "dq", 16)
                S.inc(s.dma_start(oner2[:, :], onerd), "dq", 16)
                S.inc(s.dma_start(onerf[:, :], onerfd), "dq", 16)
                S.inc(s.dma_start(idn[:, :], identd), "dq", 16)
                S.inc(s.dma_start(ctl[:, :], ctred), "dq", 16)

            @blk.scalar
            def _(sc):
                S.wait(sc, "dq")
                S.inc(sc.activation(gatebf[:, :, :], gland[:, :, :], AF.Copy), "gc")
                S.inc(sc.activation(ctbf[:, 0:D], ctl[:, :], AF.Copy), "gc")


        with nc.Block() as blk:
            @blk.sync
            def _(s):
                S.inc(s.dma_start(ctl[:, :], ctimd), "dq", 16)

            @blk.scalar
            def _(sc):
                S.wait(sc, "dq")
                S.inc(sc.activation(ctbf[:, D:2 * D], ctl[:, :], AF.Copy), "gc")

        with nc.Block() as blk:
            @blk.vector
            def _(v):
                are, aim = acol[:, 0:1], acol[:, 1:2]
                v.tensor_copy(ajb[:, 0:1], acol[:, 2:3])
                vv = S.inc(v.tensor_copy(ajb[:, NTAP:NTAP + 1], acol[:, 3:4]), "vmi")
                for j in range(1, NTAP):
                    S.wait(v, "vmi", vv)
                    pr, pi = ajb[:, j - 1:j], ajb[:, NTAP + j - 1:NTAP + j]
                    cr, ci_ = ajb[:, j:j + 1], ajb[:, NTAP + j:NTAP + j + 1]
                    v.tensor_tensor(scr2[:, 0:1], pi, aim, ALU.mult)
                    vv = S.inc(v.tensor_tensor(scr2[:, 1:2], pr, aim, ALU.mult), "vmi")
                    S.wait(v, "vmi", vv)
                    v.scalar_tensor_tensor(cr, pr, are, scr2[:, 0:1], ALU.mult,
                                           ALU.subtract)
                    vv = S.inc(v.scalar_tensor_tensor(ci_, pi, are, scr2[:, 1:2],
                                                      ALU.mult, ALU.add), "vmi")
                S.wait(v, "vmi", vv)
                v.tensor_copy(ajbbf[:, 0:NTAP], ajb[:, 0:NTAP])
                S.inc(v.tensor_scalar(ajbbf[:, NTAP:2 * NTAP], ajb[:, NTAP:2 * NTAP],
                                      -1.0, None, ALU.mult), "vmi")

        with ExitStack() as ps0:
            k0ps = ps0.enter_context(nc.psum_tensor("k0ps", [128, 512], F32))
            with nc.Block() as blk:
                for kc in range(KC):
                    @blk.tensor
                    def _(t, kc=kc):
                        if kc == 0:
                            S.wait(t, "gc")
                            S.wait(t, "vmi")
                        S.use(t, "k0ps")
                        t.matmul(k0ps[:, 0:NTAP], ctbf[:, kc * 128:(kc + 1) * 128],
                                 ajbbf[:, 0:NTAP], start=True, stop=False)
                        S.inc(t.matmul(k0ps[:, 0:NTAP],
                                       ctbf[:, D + kc * 128:D + (kc + 1) * 128],
                                       ajbbf[:, NTAP:2 * NTAP], start=False,
                                       stop=True), "tst")

                    @blk.vector
                    def _(v, kc=kc):
                        S.wait(v, "tst")
                        vv = S.inc(v.tensor_copy(k0sb[:, kc, :], k0ps[:, 0:NTAP]), "vmi")
                        S.release("k0ps", "vmi", vv)
        with nc.Block() as blk:
            @blk.vector
            def _(v):
                S.wait(v, "vmi")
        start_mem.close()
        S.barrier_clear()

        def ln_stats_part1(blk, src_chunk, st0, st1, skey, bslot,
                           src_tok=None, bufs=None):
            xb, xq = bufs if bufs is not None else (xbf_s, xsq_s)
            sst_base = S.val("vst")
            tst_base = S.val("tst")

            @blk.vector
            def _(vq):
                for cc in range(KC):
                    bi = cc % 2 + 2 * bslot * 0
                    if src_tok is not None:
                        for (n_, v_) in src_tok(cc):
                            vq.wait_ge(S.sem[n_], v_)
                    if cc >= 2:
                        S.wait(vq, "tst", tst_base + cc - 1)
                    else:
                        S.use(vq, ("stat", bslot, bi))
                    vq.tensor_copy(xb[:, bi, :], src_chunk(cc))
                    S.inc(vq.tensor_tensor(xq[:, bi, :], src_chunk(cc),
                                           src_chunk(cc), ALU.mult), "vst")

            @blk.tensor
            def _(t):
                S.use(t, (skey, 0))
                S.use(t, (skey, 1))
                for cc in range(KC):
                    bi = cc % 2
                    S.wait(t, "vst", sst_base + cc + 1)
                    t.matmul(st0[:, :], onec[:, :], xb[:, bi, :],
                             start=(cc == 0), stop=(cc == KC - 1))
                    v = S.inc(t.matmul(st1[:, :], onec[:, :], xq[:, bi, :],
                                       start=(cc == 0), stop=(cc == KC - 1)), "tst")
                    S.release(("stat", bslot, bi), "tst", v)
            return S.val("tst")

        def ln_stats_part2(blk, rows, st0, st1, skey, mm_tok):
            @blk.vector
            def _(v):
                S.wait(v, "tst", mm_tok)
                S.use(v, ("rows",))
                v.tensor_copy(rows[0:1, 0, :], st0[0:1, :])
                vv = S.inc(v.tensor_copy(rows[0:1, 1, :], st1[0:1, :]), "vrow")
                S.release((skey, 0), "vrow", vv)
                S.release((skey, 1), "vrow", vv)

        def ln_stats_tile(blk, src_chunk, rows, st0, st1, src_tok=None,
                          skey="stp", bslot=0):
            tok = ln_stats_part1(blk, src_chunk, st0, st1, skey, bslot, src_tok)
            ln_stats_part2(blk, rows, st0, st1, skey, tok)

        def ln_rows(blk, rows):
            def rr(i):
                return rows[0:1, i, :]

            @blk.vector
            def _(v):
                S.wait(v, "vrow")
                v.tensor_scalar(rr(2), rr(0), 1.0 / D, None, ALU.mult)
                vv = S.inc(v.tensor_scalar(rr(3), rr(1), 1.0 / D, None, ALU.mult),
                           "vrow")
                S.wait(v, "vrow", vv)
                vv = S.inc(v.tensor_tensor(rr(0), rr(2), rr(2), ALU.mult), "vrow")
                S.wait(v, "vrow", vv)
                vv = S.inc(v.scalar_tensor_tensor(rr(3), rr(0), -1.0, rr(3),
                                                  ALU.mult, ALU.add), "vrow")
                S.wait(v, "vrow", vv)
                vv = S.inc(v.tensor_scalar(rr(0), rr(3), float(c.EPS), None,
                                           ALU.add), "vrow")

            @blk.scalar
            def _(sc):
                S.wait(sc, "vrow")
                vv = S.inc(sc.activation(rr(3), rr(0), AF.Ln), "srow")
                S.wait(sc, "srow", vv)
                S.inc(sc.activation(rr(0), rr(3), AF.Exp, scale=-0.5), "srow")

            @blk.vector
            def _(v):
                S.wait(v, "srow")
                S.inc(v.tensor_tensor(rr(1), rr(2), rr(0), ALU.mult), "vrow")

        def ln_bcast_tile(blk, rows, bc0, bc1):
            @blk.tensor
            def _(t):
                S.wait(t, "vrow")
                S.wait(t, "srow")
                S.use(t, "bc0")
                S.use(t, "bc1")
                t.matmul(bc0[:, :], onerf[:, :], rows[0:1, 0, :],
                         start=True, stop=True)
                tok = S.inc(t.matmul(bc1[:, :], onerf[:, :], rows[0:1, 1, :],
                                     start=True, stop=True), "tst")
                S.release(("rows",), "tst", tok)

        def ln_apply_tile(blk, src_chunk, dst_chunk, bc0, bc1, gb=False):
            @blk.vector
            def _(v):
                S.wait(v, "tst")
                toks = {}
                last = 0

                def emit_b(pc):
                    nonlocal last
                    S.wait(v, "vln", toks[pc])
                    if not gb:
                        ib = v.tensor_tensor(dst_chunk(pc), lnt[:, pc % 2, :],
                                             bc1[:, :], ALU.subtract)
                        last = S.inc(ib, "vln")
                    else:
                        ib = v.tensor_tensor(lnt[:, pc % 2, :], lnt[:, pc % 2, :],
                                             bc1[:, :], ALU.subtract)
                        vv2 = S.inc(ib, "vln")
                        S.wait(v, "vln", vv2)
                        gcol = prm[:, KC + pc:KC + pc + 1]
                        bcol = prm[:, 2 * KC + pc:2 * KC + pc + 1]
                        bb = bass.AP(bcol.tensor, bcol.offset,
                                     [list(bcol.ap[0]), [0, TT]])
                        ic2 = v.scalar_tensor_tensor(dst_chunk(pc),
                                                     lnt[:, pc % 2, :], gcol, bb,
                                                     ALU.mult, ALU.add)
                        last = S.inc(ic2, "vln")
                    S.release(("lnt", pc % 2), "vln", last)

                for cc in range(KC):
                    S.use(v, ("lnt", cc % 2))
                    ia = v.tensor_tensor(lnt[:, cc % 2, :], src_chunk(cc), bc0[:, :],
                                         ALU.mult)
                    toks[cc] = S.inc(ia, "vln")
                    if cc >= 1:
                        emit_b(cc - 1)
                emit_b(KC - 1)
                S.release("bc0", "vln", last)
                S.release("bc1", "vln", last)

        def emit_ffn(call):
            with ExitStack() as mem, nc.Block() as blk:
                psn = lambda name, shp, dt=F32: mem.enter_context(
                    nc.psum_tensor(f"{name}_f{call}", shp, dt))
                sbf = lambda name, shp, dt: mem.enter_context(
                    nc.sbuf_tensor(f"{name}_f{call}", shp, dt))
                wbf = [sbf(f"wbf{i}", [128, WSZ], BF16) for i in range(2)]
                wland = sbf("wland", [128, LSZ // 2], F32)
                xhat = sbf("xhat", [128, KC, L], BF16)
                scbc = sbf("scbc", [128, NT, TT], BF16)
                rows = sbf("rows", [1, 4, TT], F32)
                silu_s = sbf("silu_s", [128, IC, TT], BF16)
                tmp_bf = sbf("tmp_bf", [128, NT * IC, TT], BF16)
                scoresF = sbf("scoresF", [NE, L], BF16)
                scores0 = sbf("scores0", [1, L], BF16)
                logf_bf = sbf("logf_bf", [NE, TT], BF16)
                ltok = sbf("ltok", [128, LC, NE], F32)
                wrk = sbf("wrk", [128, LC, NE], F32)
                msk = sbf("msk", [128, LC, NE], F32)
                mcol = sbf("mcol", [128, LC, 1], F32)
                rden = sbf("rden", [128, LC, 1], F32)
                stok_bf = sbf("stok_bf", [128, LC, NE], BF16)

                def w_t(e, part, k, m):
                    off = [OW1, OW3, OW2][part] + k * ([DI, DI, D][part]) + m * 128
                    return wbf[e % 2][:, off:off + 128]
                with ExitStack() as pro:
                    ppsn = lambda name, shp, dt=F32: pro.enter_context(
                        nc.psum_tensor(f"{name}_p{call}", shp, dt))
                    st0 = ppsn("st0", [1, TT])
                    st1 = ppsn("st1", [1, TT])
                    bc0 = ppsn("bc0", [128, TT])
                    bc1 = ppsn("bc1", [128, TT])
                    lgf = ppsn("lgf", [NE, TT])
                    rtp = ppsn("rtp", [128, 4 * 256], BF16)
                    stp0 = ppsn("stp0", [NE, 1024], BF16)
                    stp1 = ppsn("stp1", [NE, 1024], BF16)
                    stp = [stp0, stp1]

                    for tl in range(NT):
                        ln_stats_tile(blk, (lambda cc, tl=tl:
                                            R[:, cc, tl * TT:(tl + 1) * TT]),
                                      rows, st0, st1)
                        ln_rows(blk, rows)
                        ln_bcast_tile(blk, rows, bc0, bc1)
                        ln_apply_tile(blk, (lambda cc, tl=tl:
                                            R[:, cc, tl * TT:(tl + 1) * TT]),
                                      (lambda cc, tl=tl:
                                       xhat[:, cc, tl * TT:(tl + 1) * TT]),
                                      bc0, bc1)

                    TC = TT // 128
                    for tl in range(NT):
                        t0, t1 = tl * TT, (tl + 1) * TT

                        @blk.tensor
                        def _(t, t0=t0, t1=t1):
                            S.wait(t, "vln")
                            S.use(t, "lgf")
                            for k in range(KC):
                                i = t.matmul(lgf[:, :], gatebf[:, k, :],
                                             xhat[:, k, t0:t1],
                                             start=(k == 0), stop=(k == KC - 1))
                            S.inc(i, "tst")

                        @blk.vector
                        def _(v):
                            S.wait(v, "tst")
                            S.use(v, "logf")
                            vv = S.inc(v.tensor_copy(logf_bf[:, :], lgf[:, :]),
                                       "vmi")
                            S.release("lgf", "vmi", vv)

                        @blk.tensor
                        def _(t, tl=tl):
                            S.wait(t, "vmi")
                            S.use(t, "rtp")
                            for i in range(TC):
                                ii = t.transpose(rtp[:, i * 256:i * 256 + NE],
                                                 logf_bf[:, i * 128:(i + 1) * 128],
                                                 idn[0:NE, 0:NE])
                            tok = S.inc(ii, "tst")
                            S.release("logf", "tst", tok)

                        @blk.vector
                        def _(v, tl=tl):
                            S.wait(v, "tst")
                            dst = bass.AP(ltok, tl * TC * NE,
                                          [[LC * NE, 128], [NE, TC], [1, NE]])
                            srcp = bass.AP(rtp, 0,
                                           [[4 * 256, 128], [256, TC], [1, NE]])
                            vv = S.inc(v.tensor_copy(dst, srcp), "vmi")
                            S.release("rtp", "vmi", vv)

                    @blk.vector
                    def _(v):
                        S.wait(v, "vmi")
                        full = ltok[:, :, :]
                        mb = bass.AP(mcol, 0, [[LC, 128], [1, LC], [0, NE]])
                        vv = S.inc(v.reduce_max(mcol[:, :, :], full, axis=AX.X),
                                   "vmi")
                        S.wait(v, "vmi", vv)
                        S.inc(v.tensor_tensor(wrk[:, :, :], full, mb, ALU.subtract),
                              "vmi")

                    @blk.scalar
                    def _(sc):
                        S.wait(sc, "vmi")
                        S.inc(sc.activation(ltok[:, :, :], wrk[:, :, :], AF.Exp),
                              "sst")

                    @blk.vector
                    def _(v):
                        S.wait(v, "sst")
                        full = ltok[:, :, :]
                        mb = bass.AP(mcol, 0, [[LC, 128], [1, LC], [0, NE]])
                        vv = S.inc(v.tensor_copy(wrk[:, :, :], full), "vmi")
                        for it in range(c.TOPK - 1):
                            S.wait(v, "vmi", vv)
                            vv = S.inc(v.reduce_max(mcol[:, :, :], wrk[:, :, :],
                                                    axis=AX.X), "vmi")
                            S.wait(v, "vmi", vv)
                            vv = S.inc(v.tensor_tensor(msk[:, :, :], wrk[:, :, :],
                                                       mb, ALU.is_ge), "vmi")
                            S.wait(v, "vmi", vv)
                            vv = S.inc(v.scalar_tensor_tensor(wrk[:, :, :],
                                                              msk[:, :, :], -BIG,
                                                              wrk[:, :, :],
                                                              ALU.mult, ALU.add),
                                       "vmi")
                        S.wait(v, "vmi", vv)
                        vv = S.inc(v.reduce_max(mcol[:, :, :], wrk[:, :, :],
                                                axis=AX.X), "vmi")
                        S.wait(v, "vmi", vv)
                        vv = S.inc(v.tensor_tensor(msk[:, :, :], full, mb,
                                                   ALU.is_ge), "vmi")
                        S.wait(v, "vmi", vv)
                        vv = S.inc(v.tensor_tensor(wrk[:, :, :], full, msk[:, :, :],
                                                   ALU.mult), "vmi")
                        S.wait(v, "vmi", vv)
                        vv = S.inc(v.reduce_sum(mcol[:, :, :], wrk[:, :, :],
                                                axis=AX.X), "vmi")
                        S.wait(v, "vmi", vv)
                        vv = S.inc(v.reciprocal(rden[:, :, :], mcol[:, :, :]), "vmi")
                        S.wait(v, "vmi", vv)
                        rb = bass.AP(rden, 0, [[LC, 128], [1, LC], [0, NE]])
                        vv = S.inc(v.tensor_tensor(stok_bf[:, :, :], wrk[:, :, :],
                                                   rb, ALU.mult), "vmi")

                    for lc in range(LC):
                        @blk.tensor
                        def _(t, lc=lc):
                            if lc == 0:
                                S.wait(t, "vmi")
                            S.use(t, ("stp", lc % 2))
                            S.inc(t.transpose(
                                stp[lc % 2][:, 0:128],
                                stok_bf[:, lc, :], idn[:, :]), "tst")

                        @blk.vector
                        def _(v, lc=lc):
                            S.wait(v, "tst")
                            vv = S.inc(v.tensor_copy(
                                scoresF[:, lc * 128:(lc + 1) * 128],
                                stp[lc % 2][:, 0:128]), "vtp")
                            S.release(("stp", lc % 2), "vtp", vv)


                A = [psn(f"A{i}", [128, TT]) for i in range(4)]
                C = [psn(f"C{i}", [128, TT]) for i in range(4)]

                @blk.tensor
                def _(t):
                    S.wait(t, "tst", S.val("tst"))
                    S.wait(t, "vln", S.val("vln"))
                    S.wait(t, "vtp", S.val("vtp"))
                    S.wait(t, "vmi", S.val("vmi"))
                    S.wait(t, "vrow", S.val("vrow"))

                def emit_wdma(e):
                    gc_now = S.val("gc")

                    @blk.sync
                    def _(s, e=e, gc_now=gc_now):
                        hcnt = 0
                        for p, (srcd, wid) in enumerate([(w1d, KC * DI),
                                                         (w3d, KC * DI),
                                                         (w2d, IC * D)]):
                            nch = srcd.shape[1] // 128
                            hch = nch // 2
                            wper = wid // nch
                            for hh in range(2):
                                S.wait(s, "gc", gc_now + hcnt)
                                S.use(s, ("wbuf", e % 2))
                                ap = srcd[e][hh * hch * 128:(hh + 1) * hch * 128, :] \
                                    .rearrange("(q p) n -> p q n", p=128)
                                dst = bass.AP(wland, 0,
                                              [[LSZ // 2, 128], [wper, hch],
                                               [1, wper]])
                                S.inc(s.dma_start(dst, ap), "dwq", 16)
                                hcnt += 1

                WPARTS = [(OW1, KC * DI), (OW3, KC * DI), (OW2, IC * D)]
                CAST_SLICES = {0: [0, 1], 1: [2, 3], 2: [4], 3: [5]}

                def emit_one_cast(v, e, hcnt):
                    p, hh = hcnt // 2, hcnt % 2
                    off, wid = WPARTS[p]
                    dwq_need = DWQ_BASE[e] + 16 * (hcnt + 1)
                    v.wait_ge(S.sem["dwq"], dwq_need)
                    S.inc(v.tensor_copy(
                        wbf[e % 2][:, off + hh * (wid // 2):
                                   off + (hh + 1) * (wid // 2)],
                        wland[:, 0:wid // 2]), "gc")

                def emit_wcast_slice(e, tl, gcb):
                    pass

                DWQ_BASE = {}

                def emit_wdma_tracked(e):
                    DWQ_BASE[e] = S.val("dwq")
                    emit_wdma(e)

                def emit_wcasts_now(e):
                    @blk.vector
                    def _(v, e=e):
                        for hcnt in range(6):
                            emit_one_cast(v, e, hcnt)

                def emit_w2_full(ctx):
                    (e, tl, t0, t1, g_tok2, gc_done) = ctx
                    nq = KC // 2

                    def w2_mms(t, q, ks):
                        pair = q % 2
                        ms = (2 * q, 2 * q + 1)
                        ks = list(ks)
                        S.wait(t, "gact", g_tok2[ks[-1]])
                        mm = None
                        for k in ks:
                            for m in ms:
                                mm = t.matmul(C[2 * pair + (m % 2)][:, :],
                                              w_t(e, 2, k, m),
                                              tmp_bf[:, tl * IC + k, :],
                                              start=(k == 0),
                                              stop=(k == IC - 1))
                        return mm

                    def w2_fin(q, tok):
                        if q == nq - 1:
                            S.release(("wbuf", e % 2), "tw2", tok)
                            for ci in range(IC):
                                S.release(("act", ci), "tw2", tok)
                            if tl == NT - 1:
                                S.release(("scbc",), "tw2", tok)
                        S.rel[("q", q)] = [("tw2", tok)]

                    def emit_rupd(q):
                        pair = q % 2

                        @blk.vector
                        def _(v, q=q, pair=pair):
                            S.use(v, ("q", q))
                            vv = 0
                            for m in (2 * q, 2 * q + 1):
                                S.use(v, ("Rs", m, tl))
                                ii = v.scalar_tensor_tensor(
                                    R[:, m, t0:t1], C[2 * pair + (m % 2)][:, :],
                                    0.5, R[:, m, t0:t1], ALU.mult, ALU.add)
                                vv = S.inc(ii, "vev")
                                S.release(("Rs", m, tl), "vev", vv)
                            S.release(("Cp", pair), "vev", vv)

                    if nq >= 2:
                        @blk.tensor
                        def _(t):
                            if tl == 0:
                                S.wait(t, "gc", gc_done)
                            S.use(t, ("Cp", 0))
                            w2_mms(t, 0, range(IC - 1))
                            S.use(t, ("Cp", 1))
                            w2_mms(t, 1, range(IC - 1))
                            tok0 = S.inc(w2_mms(t, 0, [IC - 1]), "tw2")
                            w2_fin(0, tok0)
                            tok1 = S.inc(w2_mms(t, 1, [IC - 1]), "tw2")
                            w2_fin(1, tok1)
                        emit_rupd(0)
                        emit_rupd(1)
                        for q in range(2, nq):
                            @blk.tensor
                            def _(t, q=q):
                                S.use(t, ("Cp", q % 2))
                                tok = S.inc(w2_mms(t, q, range(IC)), "tw2")
                                w2_fin(q, tok)
                            emit_rupd(q)
                    else:
                        @blk.tensor
                        def _(t):
                            if tl == 0:
                                S.wait(t, "gc", gc_done)
                            S.use(t, ("Cp", 0))
                            tok = S.inc(w2_mms(t, 0, range(IC)), "tw2")
                            w2_fin(0, tok)
                        emit_rupd(0)

                gc_base = {0: S.val("gc")}
                emit_wdma_tracked(0)
                emit_wcasts_now(0)
                gcd = {0: S.val("gc")}
                pend = None
                for e in range(NE):
                    @blk.sync
                    def _(s, e=e):
                        if e == 0:
                            S.wait(s, "vtp", S.val("vtp"))
                        S.use(s, ("sc0",))
                        S.inc(s.dma_start(scores0[0:1, :], scoresF[e:e + 1, :]),
                              "dq", 16)
                    sc0_tok = S.val("dq")

                    if e + 1 < NE:
                        gc_base[e + 1] = S.val("gc")
                        emit_wdma_tracked(e + 1)
                        gcd[e + 1] = gc_base[e + 1] + 6
                    gc_done = gcd[e]
                    sc_ev_tok = {}

                    for tl in range(NT):
                        t0, t1 = tl * TT, (tl + 1) * TT
                        gu_tok = {}
                        vau_b = S.val("vau")

                        @blk.tensor
                        def _(t, e=e, tl=tl, t0=t0, t1=t1, gu_tok=gu_tok,
                              vau_b=vau_b, gc_done=gc_done):
                            if tl == 0:
                                S.wait(t, "gc", gc_done - 2)
                                if e == 0:
                                    S.wait(t, "vln", S.val("vln"))
                                    S.wait(t, "vev", S.val("vev"))
                            S.wait(t, "vau", vau_b)
                            for ci in range(IC):
                                pair = ci % 2
                                if ci >= 2:
                                    S.wait(t, "vau", vau_b + ci - 1)
                                for k in range(KC):
                                    mm = t.matmul(A[2 * pair][:, :], w_t(e, 0, k, ci),
                                                  xhat[:, k, t0:t1],
                                                  start=(k == 0), stop=(k == KC - 1))
                                g_tok = S.inc(mm, "tgu")
                                for k in range(KC):
                                    mm = t.matmul(A[2 * pair + 1][:, :],
                                                  w_t(e, 1, k, ci),
                                                  xhat[:, k, t0:t1],
                                                  start=(k == 0), stop=(k == KC - 1))
                                u_tok = S.inc(mm, "tgu")
                                gu_tok[ci] = (g_tok, u_tok)

                        s_tok = {}

                        @blk.scalar
                        def _(sc, gu_tok=gu_tok, s_tok=s_tok, vau_b=vau_b):
                            S.wait(sc, "vau", vau_b)
                            for ci in range(IC):
                                S.wait(sc, "tgu", gu_tok[ci][0])
                                ii = sc.activation(silu_s[:, ci, :],
                                                   A[2 * (ci % 2)][:, :], AF.Silu)
                                s_tok[ci] = S.inc(ii, "sact")

                        v_tok = {}
                        gact_b = S.val("gact")

                        @blk.vector
                        def _(v, e=e, tl=tl, gu_tok=gu_tok, s_tok=s_tok,
                              v_tok=v_tok, gact_b=gact_b):
                            S.wait(v, "gact", gact_b)
                            for ci in range(IC):
                                S.wait(v, "sact", s_tok[ci])
                                S.wait(v, "tgu", gu_tok[ci][1])
                                ii = v.tensor_tensor(tmp_bf[:, tl * IC + ci, :],
                                                     silu_s[:, ci, :],
                                                     A[2 * (ci % 2) + 1][:, :],
                                                     ALU.mult)
                                v_tok[ci] = S.inc(ii, "vau")

                        if tl == 0:
                            if pend is not None:
                                emit_w2_full(pend)
                                pend = None
                            for stl in range(NT):
                                @blk.tensor
                                def _(t, e=e, stl=stl, sc0_tok=sc0_tok):
                                    if stl < 2:
                                        S.use(t, ("Cp", 0))
                                    else:
                                        S.use(t, ("Cb", stl % 2))
                                    if stl == 0:
                                        S.wait(t, "dq", sc0_tok)
                                    ii = t.matmul(
                                        C[stl % 2][:, :], oner2[:, :],
                                        scores0[0:1, stl * TT:(stl + 1) * TT],
                                        start=True, stop=True)
                                    tok = S.inc(ii, "tsc")
                                    if stl == NT - 1:
                                        S.release(("sc0",), "tsc", tok)

                                @blk.vector
                                def _(v, e=e, stl=stl):
                                    S.wait(v, "tsc", S.val("tsc"))
                                    S.use(v, ("scbc",))
                                    vv = S.inc(v.tensor_copy(scbc[:, stl, :],
                                                             C[stl % 2][:, :]),
                                               "vev")
                                    sc_ev_tok[stl] = vv
                                    S.release(("Cb", stl % 2), "vev", vv)
                                    if stl == NT - 1:
                                        S.release(("Cp", 0), "vev", vv)
                        else:
                            emit_w2_full(pend)
                            pend = None

                        @blk.vector
                        def _(v, e=e, tl=tl):
                            if e + 1 < NE:
                                for hcnt in CAST_SLICES[tl % 4] if NT == 4 \
                                        else ([0, 1, 2] if tl % 2 == 0 else [3, 4, 5]):
                                    emit_one_cast(v, e + 1, hcnt)

                        g_tok2 = {}

                        @blk.gpsimd
                        def _(g, e=e, tl=tl, v_tok=v_tok, g_tok2=g_tok2,
                              sc_ev_tok=sc_ev_tok):
                            for ci in range(IC):
                                S.use(g, ("act", ci))
                                if ci == 0:
                                    g.wait_ge(S.sem["vev"], sc_ev_tok[tl])
                                g.wait_ge(S.sem["vau"], v_tok[ci])
                                ii = g.tensor_tensor(
                                    tmp_bf[:, tl * IC + ci, :],
                                    tmp_bf[:, tl * IC + ci, :],
                                    scbc[:, tl, :], ALU.mult)
                                g_tok2[ci] = S.inc(ii, "gact")

                        pend = (e, tl, t0, t1, g_tok2, gc_done)

                emit_w2_full(pend)
                pend = None
            S.barrier_clear()

        def emit_attn():
            with ExitStack() as mem, nc.Block() as blk:
                psn = lambda name, shp, dt=F32: mem.enter_context(
                    nc.psum_tensor(f"{name}_at", shp, dt))
                st0 = psn("st0", [1, TT])
                st1 = psn("st1", [1, TT])
                st2 = psn("st2", [1, TT])
                st3 = psn("st3", [1, TT])
                bc0 = psn("bc0", [128, TT])
                bc1 = psn("bc1", [128, TT])
                x2ps = [psn(f"x2ps{i}", [128, TT]) for i in range(2)]

                sba = lambda name, shp, dt: mem.enter_context(
                    nc.sbuf_tensor(name, shp, dt))
                winbf = sba("winbf", [128, KC, D], BF16)
                wland = sba("wland_at", [128, LSZ // 2], F32)
                rows = sba("rows_at", [1, 4, TT], F32)
                x1t = sba("x1t", [128, 2, KC, TT], BF16)
                x2t = sba("x2t", [128, KC, TT], F32)
                x3t = sba("x3t", [128, KC, TT], F32)
                PAD = 16
                assert NTAP <= PAD
                x3bf = sba("x3bf", [128, KC, PAD + TT], BF16)
                cst = sba("cst", [128, TT], F32)
                xbf_q = sba("xbf_q", [128, 2, TT], BF16)
                xsq_q = sba("xsq_q", [128, 2, TT], BF16)
                carry = sba("carry", [128, KC], F32)
                diagall = sba("diagall", [128, KC, NTAP, 128], BF16)
                ones_t = sba("ones_t", [128, TT], F32)

                QN = max(1, D // (LSZ // 2 // KC)) if KC else 1
                qw = D // QN
                gc_now = S.val("gc")
                dwq_now = S.val("dwq")

                @blk.sync
                def _(s, gc_now=gc_now):
                    for qt in range(QN):
                        S.wait(s, "gc", gc_now + qt)
                        ap = winT[:, qt * qw:(qt + 1) * qw] \
                            .rearrange("(q p) n -> p q n", p=128)
                        dst = bass.AP(wland, 0,
                                      [[LSZ // 2, 128], [qw, KC], [1, qw]])
                        S.inc(s.dma_start(dst, ap), "dwq", 16)

                @blk.scalar
                def _(sc, dwq_now=dwq_now):
                    for qt in range(QN):
                        sc.wait_ge(S.sem["dwq"], dwq_now + 16 * (qt + 1))
                        srcw = bass.AP(wland, 0,
                                       [[LSZ // 2, 128], [qw, KC], [1, qw]])
                        S.inc(sc.activation(winbf[:, :, qt * qw:(qt + 1) * qw],
                                            srcw, AF.Copy), "gc")
                gc_win = S.val("gc")

                @blk.vector
                def _(v):
                    v.memset(ones_t[:, :], 1.0)
                    S.inc(v.memset(carry[:, :], 0.0), "vsc")
                    S.inc(v.memset(x3bf[:, :, 0:PAD], 0.0), "vsc")
                    prevv = None
                    for cc in range(KC):
                        for j in range(NTAP):
                            kcol = k0sb[:, cc, j:j + 1]
                            kbc = bass.AP(k0sb, kcol.offset,
                                          [list(kcol.ap[0]), [0, 128]])
                            prevv = v.tensor_tensor(diagall[:, cc, j, :],
                                                    idn[:, :], kbc, ALU.mult)
                    S.inc(prevv, "vtp")
                dg_tok = S.val("vtp")

                ln1_tok = {0: ln_stats_part1(blk, (lambda cc:
                                                   R[:, cc, 0:TT]),
                                             st2, st3, "stq", 1,
                                             bufs=(xbf_q, xsq_q))}
                apply_tok = {}

                def emit_ln1(tl):
                    ln_stats_part2(blk, rows, st2, st3, "stq", ln1_tok[tl])
                    ln_rows(blk, rows)
                    ln_bcast_tile(blk, rows, bc0, bc1)

                    @blk.vector
                    def _(v, tl=tl):
                        S.use(v, ("x1t", tl % 2))

                    ln_apply_tile(blk, (lambda cc, tl=tl:
                                        R[:, cc, tl * TT:(tl + 1) * TT]),
                                  (lambda cc, tl=tl: x1t[:, tl % 2, cc, :]),
                                  bc0, bc1)
                    apply_tok[tl] = S.val("vln")

                emit_ln1(0)
                for tl in range(NT):
                    t0, t1 = tl * TT, (tl + 1) * TT
                    vln_tile = apply_tok[tl]

                    x2tok = {}
                    for m in range(KC):
                        @blk.tensor
                        def _(t, tl=tl, m=m, vln_tile=vln_tile):
                            if m == 0:
                                S.wait(t, "vln", vln_tile)
                                if tl == 0:
                                    S.wait(t, "gc", gc_win)
                            S.use(t, ("x2ps", m % 2))
                            mm = None
                            for k in range(KC):
                                mm = t.matmul(x2ps[m % 2][:, :],
                                              winbf[:, k, m * 128:(m + 1) * 128],
                                              x1t[:, tl % 2, k, :],
                                              start=(k == 0), stop=(k == KC - 1))
                            tok = S.inc(mm, "tst")
                            if m == KC - 1:
                                S.release(("x1t", tl % 2), "tst", tok)
                            S.rel[("x2m", m)] = [("tst", tok)]

                        @blk.vector
                        def _(v, m=m, x2tok=x2tok):
                            S.use(v, ("x2m", m))
                            S.use(v, ("x2t", m))
                            ii = v.tensor_scalar(x2t[:, m, :], x2ps[m % 2][:, :],
                                                 prm[:, m:m + 1], None, ALU.add)
                            vv = S.inc(ii, "vmi")
                            x2tok[m] = vv
                            S.release(("x2ps", m % 2), "vmi", vv)

                    ln_stats_tile(blk, (lambda cc: x2t[:, cc, :]),
                                  rows, st0, st1,
                                  src_tok=(lambda cc, x2tok=x2tok:
                                           [("vmi", x2tok[cc])]))
                    ln_rows(blk, rows)
                    if tl + 1 < NT:
                        ln1_tok[tl + 1] = ln_stats_part1(
                            blk, (lambda cc, tl=tl:
                                  R[:, cc, (tl + 1) * TT:(tl + 2) * TT]),
                            st2, st3, "stq", 1, bufs=(xbf_q, xsq_q))
                    ln_bcast_tile(blk, rows, bc0, bc1)

                    @blk.vector
                    def _(v):
                        S.use(v, ("x3t",))

                    ln_apply_tile(blk, (lambda cc: x2t[:, cc, :]),
                                  (lambda cc: x3t[:, cc, :]),
                                  bc0, bc1, gb=True)
                    vln_x3 = S.val("vln")
                    for m in range(KC):
                        S.release(("x2t", m), "vln", vln_x3)

                    gmi_b = S.val("gmi")

                    @blk.gpsimd
                    def _(g, tl=tl, vln_x3=vln_x3, gmi_b=gmi_b):
                        g.wait_ge(S.sem["vln"], vln_x3)
                        if tl > 0:
                            g.wait_ge(S.sem["gmi"], gmi_b)
                        for cc in range(KC):
                            S.inc(g.tensor_copy(x3bf[:, cc, PAD:PAD + TT],
                                                x3t[:, cc, :]), "gmi")

                    if tl + 1 < NT:
                        emit_ln1(tl + 1)

                    x3rel = []
                    tap_tok = {}

                    def emit_evac(v, ec, tl=tl, t0=t0, t1=t1):
                        S.use(v, ("ytq", ec, tl))
                        S.use(v, ("Rs2", ec, tl))
                        ii = v.tensor_tensor(R[:, ec, t0:t1], R[:, ec, t0:t1],
                                             x2ps[ec % 2][:, :], ALU.add)
                        rv = S.inc(ii, "vev")
                        S.release(("x2ps", ec % 2), "vev", rv)
                        S.release(("Rs2", ec, tl), "vev", rv)

                    for cc in range(KC):
                        @blk.vector
                        def _(v, cc=cc, tl=tl, t0=t0, t1=t1, vln_x3=vln_x3,
                              x3rel=x3rel):
                            if cc == 0:
                                S.wait(v, "vln", vln_x3)
                            S.use(v, ("cst",))
                            S.use(v, ("carry", cc))
                            ii = v.tensor_tensor_scan(cst[:, :],
                                                      ones_t[:, :],
                                                      x3t[:, cc, :],
                                                      carry[:, cc:cc + 1],
                                                      ALU.mult, ALU.add)
                            vv = S.inc(ii, "vsc")
                            S.wait(v, "vsc", vv)
                            ii = v.tensor_copy(carry[:, cc:cc + 1],
                                               cst[:, TT - 1:TT])
                            cv = S.inc(ii, "vsc")
                            S.release(("carry", cc), "vsc", cv)
                            dcol = prm[:, 3 * KC + cc:3 * KC + cc + 1]
                            S.use(v, ("Rs2", cc, tl))
                            ii = v.scalar_tensor_tensor(R[:, cc, t0:t1],
                                                        cst[:, :], dcol,
                                                        R[:, cc, t0:t1],
                                                        ALU.mult, ALU.add)
                            rv = S.inc(ii, "vsc")
                            S.release(("cst",), "vsc", rv)
                            S.wait(v, "vsc", rv)
                            ii = v.tensor_tensor(R[:, cc, t0:t1], R[:, cc, t0:t1],
                                                 x3t[:, cc, :], ALU.add)
                            rv2 = S.inc(ii, "vsc")
                            S.release(("Rs2", cc, tl), "vsc", rv2)
                            x3rel.append(("vsc", rv2))
                            if cc >= 2:
                                emit_evac(v, cc - 2)

                        @blk.tensor
                        def _(t, cc=cc, tl=tl, t0=t0, t1=t1, gmi_b=gmi_b,
                              tap_tok=tap_tok):
                            if tl == 0 and cc == 0:
                                S.wait(t, "vtp", dg_tok)
                            S.wait(t, "gmi", gmi_b + cc + 1)
                            S.use(t, ("x2ps", cc % 2))
                            mm = None
                            for j in range(NTAP):
                                o0 = max(t0, j)
                                mm = t.matmul(x2ps[cc % 2][:, o0 - t0:TT],
                                              diagall[:, cc, j, :],
                                              x3bf[:, cc, PAD + o0 - t0 - j:
                                                   PAD + TT - j],
                                              start=(j == 0),
                                              stop=(j == NTAP - 1))
                            tok = S.inc(mm, "tw2")
                            tap_tok[cc] = tok
                            S.rel[("ytq", cc, tl)] = [("tw2", tok)]

                    @blk.vector
                    def _(v, tl=tl, t0=t0, t1=t1):
                        emit_evac(v, KC - 2)
                        emit_evac(v, KC - 1)

                    S.rel[("x3t",)] = x3rel + [("gmi", gmi_b + KC)]

                    if tl + 1 < NT:
                        @blk.gpsimd
                        def _(g, tap_tok=tap_tok):
                            g.wait_ge(S.sem["tw2"], tap_tok[KC - 1])
                            S.inc(g.tensor_copy(x3bf[:, :, 0:PAD],
                                                x3bf[:, :, TT:TT + PAD]), "gmi")
            S.barrier_clear()

        def dump(pairs):
            with nc.Block() as blk:
                @blk.sync
                def _(s):
                    for dst, srcb in pairs:
                        S.inc(s.dma_start(dst, srcb), "dq", 16)
                    S.wait(s, "dq")
            S.barrier_clear()

        emit_ffn(0)
        if c.dbg:
            dump([(d_R1, R[:, :, :]), (d_k0, k0sb[:, :, :])])
        emit_attn()
        if c.dbg:
            dump([(d_R2, R[:, :, :])])
        emit_ffn(1)

        with nc.Block() as blk:
            @blk.sync
            def _(s):
                for kc in range(KC):
                    S.inc(s.dma_start(
                        outd.rearrange("(c p) l -> c p l", p=128)[kc],
                        R[:, kc, :]), "oq", 16)
                S.wait(s, "oq")

    return nc


def _col_layout(v, KC):
    return np.ascontiguousarray(np.asarray(v, dtype=np.float32).reshape(KC, 128).T)


def make_core_inputs(c, x_b, W_in, b_in, gamma, beta, A_re, A_im, B_re, B_im,
                     C_re, C_im, D_re, gate_w, w1, w3, w2):
    KC = c.KC
    f32 = lambda a: np.ascontiguousarray(np.asarray(a, dtype=np.float32))
    return {
        "xT": f32(x_b.T),
        "w1": f32(w1), "w3": f32(w3), "w2": f32(w2),
        "gateT": f32(np.asarray(gate_w).T),
        "winT": f32(np.asarray(W_in).T),
        "bin_c": _col_layout(b_in, KC),
        "gam_c": _col_layout(gamma, KC),
        "bet_c": _col_layout(beta, KC),
        "dre_c": _col_layout(D_re, KC),
        "abcol": f32(np.stack([np.asarray(A_re), np.asarray(A_im),
                               np.asarray(B_re), np.asarray(B_im)], axis=1)),
        "ctre": f32(np.asarray(C_re).T),
        "ctim": f32(np.asarray(C_im).T),
        "onec_bf": np.ones((128, 1), dtype=ml_dtypes.bfloat16),
        "oner_bf": np.ones((1, 128), dtype=ml_dtypes.bfloat16),
        "oner_f": np.ones((1, 128), dtype=np.float32),
        "identd": np.eye(128, dtype=ml_dtypes.bfloat16),
    }


_NC_CACHE = {}


def kernel(L0_attn_x, L0_attn_W_in, L0_attn_b_in, L0_attn_gamma, L0_attn_beta,
           L0_attn_A_re, L0_attn_A_im, L0_attn_B_re, L0_attn_B_im,
           L0_attn_C_re, L0_attn_C_im, L0_attn_D_re, L0_attn_D_im,
           L0_ffn_gate_w, L0_ffn_w1, L0_ffn_w3, L0_ffn_w2):
    c = Cfg()
    if "nc" not in _NC_CACHE:
        _NC_CACHE["nc"] = build_nc(c)
    nc = _NC_CACHE["nc"]
    x = np.asarray(L0_attn_x, dtype=np.float32)
    nb = x.shape[0]
    base = make_core_inputs(c, x[0], L0_attn_W_in, L0_attn_b_in, L0_attn_gamma,
                            L0_attn_beta, L0_attn_A_re, L0_attn_A_im,
                            L0_attn_B_re, L0_attn_B_im, L0_attn_C_re,
                            L0_attn_C_im, L0_attn_D_re, L0_ffn_gate_w,
                            L0_ffn_w1, L0_ffn_w3, L0_ffn_w2)
    in_maps = []
    for b in range(nb):
        m = dict(base)
        m["xT"] = np.ascontiguousarray(x[b].T)
        in_maps.append(m)
    res = run_bass_kernel_spmd(nc, in_maps, core_ids=list(range(nb)))
    out = np.stack([np.asarray(res.results[b]["out"], dtype=np.float32).T
                    for b in range(nb)], axis=0)
    return np.ascontiguousarray(out)


# version 30
# speedup vs baseline: 1.0078x; 1.0078x over previous
import numpy as np
import ml_dtypes
from contextlib import ExitStack

import concourse.bass as bass
import concourse.mybir as mybir
from concourse.bass_utils import run_bass_kernel_spmd

F32 = mybir.dt.float32
BF16 = mybir.dt.bfloat16
AF = mybir.ActivationFunctionType
ALU = mybir.AluOpType
AX = mybir.AxisListType

BIG = 1.0e30


class Cfg:
    def __init__(self, D=1024, DI=512, NE=16, L=2048, TT=512, NTAP=12, NST=64,
                 TOPK=4, EPS=1e-5):
        self.D, self.DI, self.NE, self.L, self.TT = D, DI, NE, L, TT
        self.NTAP, self.NST, self.TOPK, self.EPS = NTAP, NST, TOPK, EPS
        self.dbg = False
        self.KC = D // 128
        self.IC = DI // 128
        self.NT = L // TT
        self.LC = L // 128
        assert D % 128 == 0 and DI % 128 == 0 and L % TT == 0 and TT % 128 == 0
        assert self.KC % 2 == 0 and self.LC % 4 == 0


class B:
    def __init__(self, nc, cfg, ctx):
        self.nc = nc
        self.c = cfg
        names = ["dq", "dwq", "oq", "gc",
                 "tst", "tgu", "tw2", "tsc",
                 "vln", "vrow", "vau", "vev", "vmi", "vsc", "vtp", "vst",
                 "srow", "sst", "sact",
                 "gmi", "gact"]
        self.sem = {n: ctx.enter_context(nc.semaphore(n)) for n in names}
        self.cnt = {n: 0 for n in names}
        self.rel = {}

    def inc(self, instr, name, k=1):
        instr.then_inc(self.sem[name], k)
        self.cnt[name] += k
        return self.cnt[name]

    def val(self, name):
        return self.cnt[name]

    def wait(self, eng, name, v=None):
        v = self.cnt[name] if v is None else v
        if v > 0:
            eng.wait_ge(self.sem[name], v)

    def use(self, eng, res):
        for (n, v) in self.rel.get(res, []):
            if v > 0:
                eng.wait_ge(self.sem[n], v)

    def release(self, res, name, v):
        self.rel[res] = [(name, v)]

    def barrier_clear(self):
        self.rel = {}


def build_nc(c: Cfg):
    nc = bass.Bass("TRN2", target_bir_lowering=False)
    D, DI, NE, L, TT = c.D, c.DI, c.NE, c.L, c.TT
    KC, IC, NT, LC = c.KC, c.IC, c.NT, c.LC
    NTAP, NST = c.NTAP, c.NST
    HALF = D // 2

    xT = nc.dram_tensor("xT", [D, L], F32, kind="ExternalInput").ap()
    w1d = nc.dram_tensor("w1", [NE, D, DI], F32, kind="ExternalInput").ap()
    w3d = nc.dram_tensor("w3", [NE, D, DI], F32, kind="ExternalInput").ap()
    w2d = nc.dram_tensor("w2", [NE, DI, D], F32, kind="ExternalInput").ap()
    gateT = nc.dram_tensor("gateT", [D, NE], F32, kind="ExternalInput").ap()
    winT = nc.dram_tensor("winT", [D, D], F32, kind="ExternalInput").ap()
    bin_c = nc.dram_tensor("bin_c", [128, KC], F32, kind="ExternalInput").ap()
    gam_c = nc.dram_tensor("gam_c", [128, KC], F32, kind="ExternalInput").ap()
    bet_c = nc.dram_tensor("bet_c", [128, KC], F32, kind="ExternalInput").ap()
    dre_c = nc.dram_tensor("dre_c", [128, KC], F32, kind="ExternalInput").ap()
    abcold = nc.dram_tensor("abcol", [NST, 4], F32, kind="ExternalInput").ap()
    ctred = nc.dram_tensor("ctre", [NST, D], F32, kind="ExternalInput").ap()
    ctimd = nc.dram_tensor("ctim", [NST, D], F32, kind="ExternalInput").ap()
    onecd = nc.dram_tensor("onec_bf", [128, 1], BF16, kind="ExternalInput").ap()
    onerd = nc.dram_tensor("oner_bf", [1, 128], BF16, kind="ExternalInput").ap()
    onerfd = nc.dram_tensor("oner_f", [1, 128], F32, kind="ExternalInput").ap()
    identd = nc.dram_tensor("identd", [128, 128], BF16, kind="ExternalInput").ap()
    outd = nc.dram_tensor("out", [D, L], F32, kind="ExternalOutput").ap()
    if c.dbg:
        d_xhat = nc.dram_tensor("d_xhat", [128, KC, L], BF16, kind="ExternalOutput").ap()
        d_sc = nc.dram_tensor("d_sc", [NE, L], BF16, kind="ExternalOutput").ap()
        d_R1 = nc.dram_tensor("d_R1", [128, KC, L], F32, kind="ExternalOutput").ap()
        d_x3 = nc.dram_tensor("d_x3", [128, KC, L], BF16, kind="ExternalOutput").ap()
        d_R2 = nc.dram_tensor("d_R2", [128, KC, L], F32, kind="ExternalOutput").ap()
        d_k0 = nc.dram_tensor("d_k0", [128, KC, NTAP], F32, kind="ExternalOutput").ap()
        d_x1 = nc.dram_tensor("d_x1", [128, KC, TT], BF16, kind="ExternalOutput").ap()
        d_x2 = nc.dram_tensor("d_x2", [128, KC, TT], F32, kind="ExternalOutput").ap()

    top = ExitStack()
    with top:
        S = B(nc, c, top)
        sb = lambda name, shp, dt: top.enter_context(nc.sbuf_tensor(name, shp, dt))

        R = sb("R", [128, KC, L], F32)
        WSZ = 2 * KC * DI + IC * D
        OW1, OW3, OW2 = 0, KC * DI, 2 * KC * DI
        LSZ = max(KC * DI, IC * D, KC * HALF)
        gatebf = sb("gatebf", [128, KC, NE], BF16)
        prm = sb("prm", [128, 4 * KC], F32)
        acol = sb("acol", [NST, 4], F32)
        k0sb = sb("k0sb", [128, KC, NTAP], F32)
        onec = sb("onec", [128, 1], BF16)
        oner2 = sb("oner2", [1, 128], BF16)
        onerf = sb("onerf", [1, 128], F32)
        idn = sb("idn", [128, 128], BF16)
        xbf_s = sb("xbf_s", [128, 2, TT], BF16)
        xsq_s = sb("xsq_s", [128, 2, TT], BF16)
        lnt = sb("lnt", [128, 2, TT], F32)
        start_mem = ExitStack()
        ssb = lambda name, shp, dt: start_mem.enter_context(
            nc.sbuf_tensor(name, shp, dt))
        gland = ssb("gland", [128, KC, NE], F32)
        ctbf = ssb("ctbf", [NST, 2 * D], BF16)
        ctl = ssb("ctl", [NST, D], F32)
        ajb = ssb("ajb", [NST, 2 * NTAP], F32)
        ajbbf = ssb("ajbbf", [NST, 2 * NTAP], BF16)
        scr2 = ssb("scr2", [NST, 2], F32)

        with nc.Block() as blk:
            @blk.sync
            def _(s):
                for kc in range(KC):
                    S.inc(s.dma_start(R[:, kc, :],
                                      xT.rearrange("(c p) l -> c p l", p=128)[kc]),
                          "dq", 16)
                S.inc(s.dma_start(gland[:, :, :],
                                  gateT.rearrange("(c p) e -> p c e", p=128)), "dq", 16)
                S.inc(s.dma_start(prm[:, 0 * KC:1 * KC], bin_c), "dq", 16)
                S.inc(s.dma_start(prm[:, 1 * KC:2 * KC], gam_c), "dq", 16)
                S.inc(s.dma_start(prm[:, 2 * KC:3 * KC], bet_c), "dq", 16)
                S.inc(s.dma_start(prm[:, 3 * KC:4 * KC], dre_c), "dq", 16)
                S.inc(s.dma_start(acol[:, :], abcold), "dq", 16)
                S.inc(s.dma_start(onec[:, :], onecd), "dq", 16)
                S.inc(s.dma_start(oner2[:, :], onerd), "dq", 16)
                S.inc(s.dma_start(onerf[:, :], onerfd), "dq", 16)
                S.inc(s.dma_start(idn[:, :], identd), "dq", 16)
                S.inc(s.dma_start(ctl[:, :], ctred), "dq", 16)

            @blk.scalar
            def _(sc):
                S.wait(sc, "dq")
                S.inc(sc.activation(gatebf[:, :, :], gland[:, :, :], AF.Copy), "gc")
                S.inc(sc.activation(ctbf[:, 0:D], ctl[:, :], AF.Copy), "gc")


        with nc.Block() as blk:
            @blk.sync
            def _(s):
                S.inc(s.dma_start(ctl[:, :], ctimd), "dq", 16)

            @blk.scalar
            def _(sc):
                S.wait(sc, "dq")
                S.inc(sc.activation(ctbf[:, D:2 * D], ctl[:, :], AF.Copy), "gc")

        with nc.Block() as blk:
            @blk.vector
            def _(v):
                are, aim = acol[:, 0:1], acol[:, 1:2]
                v.tensor_copy(ajb[:, 0:1], acol[:, 2:3])
                vv = S.inc(v.tensor_copy(ajb[:, NTAP:NTAP + 1], acol[:, 3:4]), "vmi")
                for j in range(1, NTAP):
                    S.wait(v, "vmi", vv)
                    pr, pi = ajb[:, j - 1:j], ajb[:, NTAP + j - 1:NTAP + j]
                    cr, ci_ = ajb[:, j:j + 1], ajb[:, NTAP + j:NTAP + j + 1]
                    v.tensor_tensor(scr2[:, 0:1], pi, aim, ALU.mult)
                    vv = S.inc(v.tensor_tensor(scr2[:, 1:2], pr, aim, ALU.mult), "vmi")
                    S.wait(v, "vmi", vv)
                    v.scalar_tensor_tensor(cr, pr, are, scr2[:, 0:1], ALU.mult,
                                           ALU.subtract)
                    vv = S.inc(v.scalar_tensor_tensor(ci_, pi, are, scr2[:, 1:2],
                                                      ALU.mult, ALU.add), "vmi")
                S.wait(v, "vmi", vv)
                v.tensor_copy(ajbbf[:, 0:NTAP], ajb[:, 0:NTAP])
                S.inc(v.tensor_scalar(ajbbf[:, NTAP:2 * NTAP], ajb[:, NTAP:2 * NTAP],
                                      -1.0, None, ALU.mult), "vmi")

        with ExitStack() as ps0:
            k0ps = ps0.enter_context(nc.psum_tensor("k0ps", [128, 512], F32))
            with nc.Block() as blk:
                for kc in range(KC):
                    @blk.tensor
                    def _(t, kc=kc):
                        if kc == 0:
                            S.wait(t, "gc")
                            S.wait(t, "vmi")
                        S.use(t, "k0ps")
                        t.matmul(k0ps[:, 0:NTAP], ctbf[:, kc * 128:(kc + 1) * 128],
                                 ajbbf[:, 0:NTAP], start=True, stop=False)
                        S.inc(t.matmul(k0ps[:, 0:NTAP],
                                       ctbf[:, D + kc * 128:D + (kc + 1) * 128],
                                       ajbbf[:, NTAP:2 * NTAP], start=False,
                                       stop=True), "tst")

                    @blk.vector
                    def _(v, kc=kc):
                        S.wait(v, "tst")
                        vv = S.inc(v.tensor_copy(k0sb[:, kc, :], k0ps[:, 0:NTAP]), "vmi")
                        S.release("k0ps", "vmi", vv)
        with nc.Block() as blk:
            @blk.vector
            def _(v):
                S.wait(v, "vmi")
        start_mem.close()
        S.barrier_clear()

        def ln_stats_part1(blk, src_chunk, st0, st1, skey, bslot,
                           src_tok=None, bufs=None):
            xb, xq = bufs if bufs is not None else (xbf_s, xsq_s)
            sst_base = S.val("vst")
            tst_base = S.val("tst")

            @blk.vector
            def _(vq):
                for cc in range(KC):
                    bi = cc % 2 + 2 * bslot * 0
                    if src_tok is not None:
                        for (n_, v_) in src_tok(cc):
                            vq.wait_ge(S.sem[n_], v_)
                    if cc >= 2:
                        S.wait(vq, "tst", tst_base + cc - 1)
                    else:
                        S.use(vq, ("stat", bslot, bi))
                    vq.tensor_copy(xb[:, bi, :], src_chunk(cc))
                    S.inc(vq.tensor_tensor(xq[:, bi, :], src_chunk(cc),
                                           src_chunk(cc), ALU.mult), "vst")

            @blk.tensor
            def _(t):
                S.use(t, (skey, 0))
                S.use(t, (skey, 1))
                for cc in range(KC):
                    bi = cc % 2
                    S.wait(t, "vst", sst_base + cc + 1)
                    t.matmul(st0[:, :], onec[:, :], xb[:, bi, :],
                             start=(cc == 0), stop=(cc == KC - 1))
                    v = S.inc(t.matmul(st1[:, :], onec[:, :], xq[:, bi, :],
                                       start=(cc == 0), stop=(cc == KC - 1)), "tst")
                    S.release(("stat", bslot, bi), "tst", v)
            return S.val("tst")

        def ln_stats_part2(blk, rows, st0, st1, skey, mm_tok):
            @blk.vector
            def _(v):
                S.wait(v, "tst", mm_tok)
                S.use(v, ("rows",))
                v.tensor_copy(rows[0:1, 0, :], st0[0:1, :])
                vv = S.inc(v.tensor_copy(rows[0:1, 1, :], st1[0:1, :]), "vrow")
                S.release((skey, 0), "vrow", vv)
                S.release((skey, 1), "vrow", vv)

        def ln_stats_tile(blk, src_chunk, rows, st0, st1, src_tok=None,
                          skey="stp", bslot=0):
            tok = ln_stats_part1(blk, src_chunk, st0, st1, skey, bslot, src_tok)
            ln_stats_part2(blk, rows, st0, st1, skey, tok)

        def ln_rows(blk, rows):
            def rr(i):
                return rows[0:1, i, :]

            @blk.vector
            def _(v):
                S.wait(v, "vrow")
                v.tensor_scalar(rr(2), rr(0), 1.0 / D, None, ALU.mult)
                vv = S.inc(v.tensor_scalar(rr(3), rr(1), 1.0 / D, None, ALU.mult),
                           "vrow")
                S.wait(v, "vrow", vv)
                vv = S.inc(v.tensor_tensor(rr(0), rr(2), rr(2), ALU.mult), "vrow")
                S.wait(v, "vrow", vv)
                vv = S.inc(v.scalar_tensor_tensor(rr(3), rr(0), -1.0, rr(3),
                                                  ALU.mult, ALU.add), "vrow")
                S.wait(v, "vrow", vv)
                vv = S.inc(v.tensor_scalar(rr(0), rr(3), float(c.EPS), None,
                                           ALU.add), "vrow")

            @blk.scalar
            def _(sc):
                S.wait(sc, "vrow")
                vv = S.inc(sc.activation(rr(3), rr(0), AF.Ln), "srow")
                S.wait(sc, "srow", vv)
                S.inc(sc.activation(rr(0), rr(3), AF.Exp, scale=-0.5), "srow")

            @blk.vector
            def _(v):
                S.wait(v, "srow")
                S.inc(v.tensor_tensor(rr(1), rr(2), rr(0), ALU.mult), "vrow")

        def ln_bcast_tile(blk, rows, bc0, bc1):
            @blk.tensor
            def _(t):
                S.wait(t, "vrow")
                S.wait(t, "srow")
                S.use(t, "bc0")
                S.use(t, "bc1")
                t.matmul(bc0[:, :], onerf[:, :], rows[0:1, 0, :],
                         start=True, stop=True)
                tok = S.inc(t.matmul(bc1[:, :], onerf[:, :], rows[0:1, 1, :],
                                     start=True, stop=True), "tst")
                S.release(("rows",), "tst", tok)

        def ln_apply_tile(blk, src_chunk, dst_chunk, bc0, bc1, gb=False):
            @blk.vector
            def _(v):
                S.wait(v, "tst")
                toks = {}
                last = 0

                def emit_b(pc):
                    nonlocal last
                    S.wait(v, "vln", toks[pc])
                    if not gb:
                        ib = v.tensor_tensor(dst_chunk(pc), lnt[:, pc % 2, :],
                                             bc1[:, :], ALU.subtract)
                        last = S.inc(ib, "vln")
                    else:
                        ib = v.tensor_tensor(lnt[:, pc % 2, :], lnt[:, pc % 2, :],
                                             bc1[:, :], ALU.subtract)
                        vv2 = S.inc(ib, "vln")
                        S.wait(v, "vln", vv2)
                        gcol = prm[:, KC + pc:KC + pc + 1]
                        bcol = prm[:, 2 * KC + pc:2 * KC + pc + 1]
                        bb = bass.AP(bcol.tensor, bcol.offset,
                                     [list(bcol.ap[0]), [0, TT]])
                        ic2 = v.scalar_tensor_tensor(dst_chunk(pc),
                                                     lnt[:, pc % 2, :], gcol, bb,
                                                     ALU.mult, ALU.add)
                        last = S.inc(ic2, "vln")
                    S.release(("lnt", pc % 2), "vln", last)

                for cc in range(KC):
                    S.use(v, ("lnt", cc % 2))
                    ia = v.tensor_tensor(lnt[:, cc % 2, :], src_chunk(cc), bc0[:, :],
                                         ALU.mult)
                    toks[cc] = S.inc(ia, "vln")
                    if cc >= 1:
                        emit_b(cc - 1)
                emit_b(KC - 1)
                S.release("bc0", "vln", last)
                S.release("bc1", "vln", last)

        def emit_ffn(call):
            with ExitStack() as mem, nc.Block() as blk:
                psn = lambda name, shp, dt=F32: mem.enter_context(
                    nc.psum_tensor(f"{name}_f{call}", shp, dt))
                sbf = lambda name, shp, dt: mem.enter_context(
                    nc.sbuf_tensor(f"{name}_f{call}", shp, dt))
                wbf = [sbf(f"wbf{i}", [128, WSZ], BF16) for i in range(2)]
                wland = sbf("wland", [128, LSZ // 2], F32)
                xhat = sbf("xhat", [128, KC, L], BF16)
                scbc = sbf("scbc", [128, NT, TT], BF16)
                rows = sbf("rows", [1, 4, TT], F32)
                silu_s = sbf("silu_s", [128, IC, TT], BF16)
                tmp_bf = sbf("tmp_bf", [128, NT * IC, TT], BF16)
                scoresF = sbf("scoresF", [NE, L], BF16)
                scores0 = sbf("scores0", [1, L], BF16)
                logf_bf = sbf("logf_bf", [NE, TT], BF16)
                ltok = sbf("ltok", [128, LC, NE], F32)
                wrk = sbf("wrk", [128, LC, NE], F32)
                msk = sbf("msk", [128, LC, NE], F32)
                mcol = sbf("mcol", [128, LC, 1], F32)
                rden = sbf("rden", [128, LC, 1], F32)
                stok_bf = sbf("stok_bf", [128, LC, NE], BF16)

                def w_t(e, part, k, m):
                    off = [OW1, OW3, OW2][part] + k * ([DI, DI, D][part]) + m * 128
                    return wbf[e % 2][:, off:off + 128]
                with ExitStack() as pro:
                    ppsn = lambda name, shp, dt=F32: pro.enter_context(
                        nc.psum_tensor(f"{name}_p{call}", shp, dt))
                    st0 = ppsn("st0", [1, TT])
                    st1 = ppsn("st1", [1, TT])
                    bc0 = ppsn("bc0", [128, TT])
                    bc1 = ppsn("bc1", [128, TT])
                    lgf = ppsn("lgf", [NE, TT])
                    rtp = ppsn("rtp", [128, 4 * 256], BF16)
                    stp0 = ppsn("stp0", [NE, 1024], BF16)
                    stp1 = ppsn("stp1", [NE, 1024], BF16)
                    stp = [stp0, stp1]

                    TC = TT // 128
                    for tl in range(NT):
                        t0, t1 = tl * TT, (tl + 1) * TT
                        ln_stats_tile(blk, (lambda cc, tl=tl:
                                            R[:, cc, tl * TT:(tl + 1) * TT]),
                                      rows, st0, st1)
                        ln_rows(blk, rows)
                        ln_bcast_tile(blk, rows, bc0, bc1)
                        ln_apply_tile(blk, (lambda cc, tl=tl:
                                            R[:, cc, tl * TT:(tl + 1) * TT]),
                                      (lambda cc, tl=tl:
                                       xhat[:, cc, tl * TT:(tl + 1) * TT]),
                                      bc0, bc1)

                        @blk.tensor
                        def _(t, t0=t0, t1=t1):
                            S.wait(t, "vln")
                            S.use(t, "lgf")
                            for k in range(KC):
                                i = t.matmul(lgf[:, :], gatebf[:, k, :],
                                             xhat[:, k, t0:t1],
                                             start=(k == 0), stop=(k == KC - 1))
                            S.inc(i, "tst")

                        @blk.vector
                        def _(v):
                            S.wait(v, "tst")
                            S.use(v, "logf")
                            vv = S.inc(v.tensor_copy(logf_bf[:, :], lgf[:, :]),
                                       "vmi")
                            S.release("lgf", "vmi", vv)

                        @blk.tensor
                        def _(t, tl=tl):
                            S.wait(t, "vmi")
                            S.use(t, "rtp")
                            for i in range(TC):
                                ii = t.transpose(rtp[:, i * 256:i * 256 + NE],
                                                 logf_bf[:, i * 128:(i + 1) * 128],
                                                 idn[0:NE, 0:NE])
                            tok = S.inc(ii, "tst")
                            S.release("logf", "tst", tok)

                        @blk.vector
                        def _(v, tl=tl):
                            S.wait(v, "tst")
                            dst = bass.AP(ltok, tl * TC * NE,
                                          [[LC * NE, 128], [NE, TC], [1, NE]])
                            srcp = bass.AP(rtp, 0,
                                           [[4 * 256, 128], [256, TC], [1, NE]])
                            vv = S.inc(v.tensor_copy(dst, srcp), "vmi")
                            S.release("rtp", "vmi", vv)

                    @blk.vector
                    def _(v):
                        S.wait(v, "vmi")
                        full = ltok[:, :, :]
                        mb = bass.AP(mcol, 0, [[LC, 128], [1, LC], [0, NE]])
                        vv = S.inc(v.reduce_max(mcol[:, :, :], full, axis=AX.X),
                                   "vmi")
                        S.wait(v, "vmi", vv)
                        S.inc(v.tensor_tensor(wrk[:, :, :], full, mb, ALU.subtract),
                              "vmi")

                    @blk.scalar
                    def _(sc):
                        S.wait(sc, "vmi")
                        S.inc(sc.activation(ltok[:, :, :], wrk[:, :, :], AF.Exp),
                              "sst")

                    @blk.vector
                    def _(v):
                        S.wait(v, "sst")
                        full = ltok[:, :, :]
                        mb = bass.AP(mcol, 0, [[LC, 128], [1, LC], [0, NE]])
                        vv = S.inc(v.tensor_copy(wrk[:, :, :], full), "vmi")
                        for it in range(c.TOPK - 1):
                            S.wait(v, "vmi", vv)
                            vv = S.inc(v.reduce_max(mcol[:, :, :], wrk[:, :, :],
                                                    axis=AX.X), "vmi")
                            S.wait(v, "vmi", vv)
                            vv = S.inc(v.tensor_tensor(msk[:, :, :], wrk[:, :, :],
                                                       mb, ALU.is_ge), "vmi")
                            S.wait(v, "vmi", vv)
                            vv = S.inc(v.scalar_tensor_tensor(wrk[:, :, :],
                                                              msk[:, :, :], -BIG,
                                                              wrk[:, :, :],
                                                              ALU.mult, ALU.add),
                                       "vmi")
                        S.wait(v, "vmi", vv)
                        vv = S.inc(v.reduce_max(mcol[:, :, :], wrk[:, :, :],
                                                axis=AX.X), "vmi")
                        S.wait(v, "vmi", vv)
                        vv = S.inc(v.tensor_tensor(msk[:, :, :], full, mb,
                                                   ALU.is_ge), "vmi")
                        S.wait(v, "vmi", vv)
                        vv = S.inc(v.tensor_tensor(wrk[:, :, :], full, msk[:, :, :],
                                                   ALU.mult), "vmi")
                        S.wait(v, "vmi", vv)
                        vv = S.inc(v.reduce_sum(mcol[:, :, :], wrk[:, :, :],
                                                axis=AX.X), "vmi")
                        S.wait(v, "vmi", vv)
                        vv = S.inc(v.reciprocal(rden[:, :, :], mcol[:, :, :]), "vmi")
                        S.wait(v, "vmi", vv)
                        rb = bass.AP(rden, 0, [[LC, 128], [1, LC], [0, NE]])
                        vv = S.inc(v.tensor_tensor(stok_bf[:, :, :], wrk[:, :, :],
                                                   rb, ALU.mult), "vmi")

                    for lc in range(LC):
                        @blk.tensor
                        def _(t, lc=lc):
                            if lc == 0:
                                S.wait(t, "vmi")
                            S.use(t, ("stp", lc % 2))
                            S.inc(t.transpose(
                                stp[lc % 2][:, 0:128],
                                stok_bf[:, lc, :], idn[:, :]), "tst")

                        @blk.vector
                        def _(v, lc=lc):
                            S.wait(v, "tst")
                            vv = S.inc(v.tensor_copy(
                                scoresF[:, lc * 128:(lc + 1) * 128],
                                stp[lc % 2][:, 0:128]), "vtp")
                            S.release(("stp", lc % 2), "vtp", vv)


                A = [psn(f"A{i}", [128, TT]) for i in range(4)]
                C = [psn(f"C{i}", [128, TT]) for i in range(4)]

                @blk.tensor
                def _(t):
                    S.wait(t, "tst", S.val("tst"))
                    S.wait(t, "vln", S.val("vln"))
                    S.wait(t, "vtp", S.val("vtp"))
                    S.wait(t, "vmi", S.val("vmi"))
                    S.wait(t, "vrow", S.val("vrow"))

                def emit_wdma(e):
                    gc_now = S.val("gc")

                    @blk.sync
                    def _(s, e=e, gc_now=gc_now):
                        hcnt = 0
                        for p, (srcd, wid) in enumerate([(w1d, KC * DI),
                                                         (w3d, KC * DI),
                                                         (w2d, IC * D)]):
                            nch = srcd.shape[1] // 128
                            hch = nch // 2
                            wper = wid // nch
                            for hh in range(2):
                                S.wait(s, "gc", gc_now + hcnt)
                                S.use(s, ("wbuf", e % 2))
                                ap = srcd[e][hh * hch * 128:(hh + 1) * hch * 128, :] \
                                    .rearrange("(q p) n -> p q n", p=128)
                                dst = bass.AP(wland, 0,
                                              [[LSZ // 2, 128], [wper, hch],
                                               [1, wper]])
                                S.inc(s.dma_start(dst, ap), "dwq", 16)
                                hcnt += 1

                WPARTS = [(OW1, KC * DI), (OW3, KC * DI), (OW2, IC * D)]
                CAST_SLICES = {0: [0, 1], 1: [2, 3], 2: [4], 3: [5]}

                def emit_one_cast(v, e, hcnt):
                    p, hh = hcnt // 2, hcnt % 2
                    off, wid = WPARTS[p]
                    dwq_need = DWQ_BASE[e] + 16 * (hcnt + 1)
                    v.wait_ge(S.sem["dwq"], dwq_need)
                    S.inc(v.tensor_copy(
                        wbf[e % 2][:, off + hh * (wid // 2):
                                   off + (hh + 1) * (wid // 2)],
                        wland[:, 0:wid // 2]), "gc")

                def emit_wcast_slice(e, tl, gcb):
                    pass

                DWQ_BASE = {}

                def emit_wdma_tracked(e):
                    DWQ_BASE[e] = S.val("dwq")
                    emit_wdma(e)

                def emit_wcasts_now(e):
                    @blk.vector
                    def _(v, e=e):
                        for hcnt in range(6):
                            emit_one_cast(v, e, hcnt)

                def emit_w2_full(ctx):
                    (e, tl, t0, t1, g_tok2, gc_done) = ctx
                    nq = KC // 2

                    def w2_mms(t, q, ks):
                        pair = q % 2
                        ms = (2 * q, 2 * q + 1)
                        ks = list(ks)
                        S.wait(t, "gact", g_tok2[ks[-1]])
                        mm = None
                        for k in ks:
                            for m in ms:
                                mm = t.matmul(C[2 * pair + (m % 2)][:, :],
                                              w_t(e, 2, k, m),
                                              tmp_bf[:, tl * IC + k, :],
                                              start=(k == 0),
                                              stop=(k == IC - 1))
                        return mm

                    def w2_fin(q, tok):
                        if q == nq - 1:
                            S.release(("wbuf", e % 2), "tw2", tok)
                            for ci in range(IC):
                                S.release(("act", ci), "tw2", tok)
                            if tl == NT - 1:
                                S.release(("scbc",), "tw2", tok)
                        S.rel[("q", q)] = [("tw2", tok)]

                    def emit_rupd(q):
                        pair = q % 2

                        @blk.vector
                        def _(v, q=q, pair=pair):
                            S.use(v, ("q", q))
                            vv = 0
                            for m in (2 * q, 2 * q + 1):
                                S.use(v, ("Rs", m, tl))
                                ii = v.scalar_tensor_tensor(
                                    R[:, m, t0:t1], C[2 * pair + (m % 2)][:, :],
                                    0.5, R[:, m, t0:t1], ALU.mult, ALU.add)
                                vv = S.inc(ii, "vev")
                                S.release(("Rs", m, tl), "vev", vv)
                            S.release(("Cp", pair), "vev", vv)

                    if nq >= 2:
                        @blk.tensor
                        def _(t):
                            if tl == 0:
                                S.wait(t, "gc", gc_done)
                            S.use(t, ("Cp", 0))
                            w2_mms(t, 0, range(IC - 1))
                            S.use(t, ("Cp", 1))
                            w2_mms(t, 1, range(IC - 1))
                            tok0 = S.inc(w2_mms(t, 0, [IC - 1]), "tw2")
                            w2_fin(0, tok0)
                            tok1 = S.inc(w2_mms(t, 1, [IC - 1]), "tw2")
                            w2_fin(1, tok1)
                        emit_rupd(0)
                        emit_rupd(1)
                        for q in range(2, nq):
                            @blk.tensor
                            def _(t, q=q):
                                S.use(t, ("Cp", q % 2))
                                tok = S.inc(w2_mms(t, q, range(IC)), "tw2")
                                w2_fin(q, tok)
                            emit_rupd(q)
                    else:
                        @blk.tensor
                        def _(t):
                            if tl == 0:
                                S.wait(t, "gc", gc_done)
                            S.use(t, ("Cp", 0))
                            tok = S.inc(w2_mms(t, 0, range(IC)), "tw2")
                            w2_fin(0, tok)
                        emit_rupd(0)

                gc_base = {0: S.val("gc")}
                emit_wdma_tracked(0)
                emit_wcasts_now(0)
                gcd = {0: S.val("gc")}
                pend = None
                for e in range(NE):
                    @blk.sync
                    def _(s, e=e):
                        if e == 0:
                            S.wait(s, "vtp", S.val("vtp"))
                        S.use(s, ("sc0",))
                        S.inc(s.dma_start(scores0[0:1, :], scoresF[e:e + 1, :]),
                              "dq", 16)
                    sc0_tok = S.val("dq")

                    if e + 1 < NE:
                        gc_base[e + 1] = S.val("gc")
                        emit_wdma_tracked(e + 1)
                        gcd[e + 1] = gc_base[e + 1] + 6
                    gc_done = gcd[e]
                    sc_ev_tok = {}

                    for tl in range(NT):
                        t0, t1 = tl * TT, (tl + 1) * TT
                        gu_tok = {}
                        vau_b = S.val("vau")

                        @blk.tensor
                        def _(t, e=e, tl=tl, t0=t0, t1=t1, gu_tok=gu_tok,
                              vau_b=vau_b, gc_done=gc_done):
                            if tl == 0:
                                S.wait(t, "gc", gc_done - 2)
                                if e == 0:
                                    S.wait(t, "vln", S.val("vln"))
                                    S.wait(t, "vev", S.val("vev"))
                            S.wait(t, "vau", vau_b)
                            for ci in range(IC):
                                pair = ci % 2
                                if ci >= 2:
                                    S.wait(t, "vau", vau_b + ci - 1)
                                for k in range(KC):
                                    mm = t.matmul(A[2 * pair][:, :], w_t(e, 0, k, ci),
                                                  xhat[:, k, t0:t1],
                                                  start=(k == 0), stop=(k == KC - 1))
                                g_tok = S.inc(mm, "tgu")
                                for k in range(KC):
                                    mm = t.matmul(A[2 * pair + 1][:, :],
                                                  w_t(e, 1, k, ci),
                                                  xhat[:, k, t0:t1],
                                                  start=(k == 0), stop=(k == KC - 1))
                                u_tok = S.inc(mm, "tgu")
                                gu_tok[ci] = (g_tok, u_tok)

                        s_tok = {}

                        @blk.scalar
                        def _(sc, gu_tok=gu_tok, s_tok=s_tok, vau_b=vau_b):
                            S.wait(sc, "vau", vau_b)
                            for ci in range(IC):
                                S.wait(sc, "tgu", gu_tok[ci][0])
                                ii = sc.activation(silu_s[:, ci, :],
                                                   A[2 * (ci % 2)][:, :], AF.Silu)
                                s_tok[ci] = S.inc(ii, "sact")

                        v_tok = {}
                        gact_b = S.val("gact")

                        @blk.vector
                        def _(v, e=e, tl=tl, gu_tok=gu_tok, s_tok=s_tok,
                              v_tok=v_tok, gact_b=gact_b):
                            S.wait(v, "gact", gact_b)
                            for ci in range(IC):
                                S.wait(v, "sact", s_tok[ci])
                                S.wait(v, "tgu", gu_tok[ci][1])
                                ii = v.tensor_tensor(tmp_bf[:, tl * IC + ci, :],
                                                     silu_s[:, ci, :],
                                                     A[2 * (ci % 2) + 1][:, :],
                                                     ALU.mult)
                                v_tok[ci] = S.inc(ii, "vau")

                        if tl == 0:
                            if pend is not None:
                                emit_w2_full(pend)
                                pend = None
                            for stl in range(NT):
                                @blk.tensor
                                def _(t, e=e, stl=stl, sc0_tok=sc0_tok):
                                    if stl < 2:
                                        S.use(t, ("Cp", 0))
                                    else:
                                        S.use(t, ("Cb", stl % 2))
                                    if stl == 0:
                                        S.wait(t, "dq", sc0_tok)
                                    ii = t.matmul(
                                        C[stl % 2][:, :], oner2[:, :],
                                        scores0[0:1, stl * TT:(stl + 1) * TT],
                                        start=True, stop=True)
                                    tok = S.inc(ii, "tsc")
                                    if stl == NT - 1:
                                        S.release(("sc0",), "tsc", tok)

                                @blk.vector
                                def _(v, e=e, stl=stl):
                                    S.wait(v, "tsc", S.val("tsc"))
                                    S.use(v, ("scbc",))
                                    vv = S.inc(v.tensor_copy(scbc[:, stl, :],
                                                             C[stl % 2][:, :]),
                                               "vev")
                                    sc_ev_tok[stl] = vv
                                    S.release(("Cb", stl % 2), "vev", vv)
                                    if stl == NT - 1:
                                        S.release(("Cp", 0), "vev", vv)
                        else:
                            emit_w2_full(pend)
                            pend = None

                        @blk.vector
                        def _(v, e=e, tl=tl):
                            if e + 1 < NE:
                                for hcnt in CAST_SLICES[tl % 4] if NT == 4 \
                                        else ([0, 1, 2] if tl % 2 == 0 else [3, 4, 5]):
                                    emit_one_cast(v, e + 1, hcnt)

                        g_tok2 = {}

                        @blk.gpsimd
                        def _(g, e=e, tl=tl, v_tok=v_tok, g_tok2=g_tok2,
                              sc_ev_tok=sc_ev_tok):
                            for ci in range(IC):
                                S.use(g, ("act", ci))
                                if ci == 0:
                                    g.wait_ge(S.sem["vev"], sc_ev_tok[tl])
                                g.wait_ge(S.sem["vau"], v_tok[ci])
                                ii = g.tensor_tensor(
                                    tmp_bf[:, tl * IC + ci, :],
                                    tmp_bf[:, tl * IC + ci, :],
                                    scbc[:, tl, :], ALU.mult)
                                g_tok2[ci] = S.inc(ii, "gact")

                        pend = (e, tl, t0, t1, g_tok2, gc_done)

                emit_w2_full(pend)
                pend = None
            S.barrier_clear()

        def emit_attn():
            with ExitStack() as mem, nc.Block() as blk:
                psn = lambda name, shp, dt=F32: mem.enter_context(
                    nc.psum_tensor(f"{name}_at", shp, dt))
                st0 = psn("st0", [1, TT])
                st1 = psn("st1", [1, TT])
                st2 = psn("st2", [1, TT])
                st3 = psn("st3", [1, TT])
                bc0 = psn("bc0", [128, TT])
                bc1 = psn("bc1", [128, TT])
                x2ps = [psn(f"x2ps{i}", [128, TT]) for i in range(2)]

                sba = lambda name, shp, dt: mem.enter_context(
                    nc.sbuf_tensor(name, shp, dt))
                winbf = sba("winbf", [128, KC, D], BF16)
                wland = sba("wland_at", [128, LSZ // 2], F32)
                rows = sba("rows_at", [1, 4, TT], F32)
                x1t = sba("x1t", [128, 2, KC, TT], BF16)
                x2t = sba("x2t", [128, KC, TT], F32)
                x3t = sba("x3t", [128, KC, TT], F32)
                PAD = 16
                assert NTAP <= PAD
                x3bf = sba("x3bf", [128, KC, PAD + TT], BF16)
                cst = sba("cst", [128, TT], F32)
                xbf_q = sba("xbf_q", [128, 2, TT], BF16)
                xsq_q = sba("xsq_q", [128, 2, TT], BF16)
                carry = sba("carry", [128, KC], F32)
                diagall = sba("diagall", [128, KC, NTAP, 128], BF16)
                ones_t = sba("ones_t", [128, TT], F32)

                QN = max(1, D // (LSZ // 2 // KC)) if KC else 1
                qw = D // QN
                gc_now = S.val("gc")
                dwq_now = S.val("dwq")

                @blk.sync
                def _(s, gc_now=gc_now):
                    for qt in range(QN):
                        S.wait(s, "gc", gc_now + qt)
                        ap = winT[:, qt * qw:(qt + 1) * qw] \
                            .rearrange("(q p) n -> p q n", p=128)
                        dst = bass.AP(wland, 0,
                                      [[LSZ // 2, 128], [qw, KC], [1, qw]])
                        S.inc(s.dma_start(dst, ap), "dwq", 16)

                @blk.scalar
                def _(sc, dwq_now=dwq_now):
                    for qt in range(QN):
                        sc.wait_ge(S.sem["dwq"], dwq_now + 16 * (qt + 1))
                        srcw = bass.AP(wland, 0,
                                       [[LSZ // 2, 128], [qw, KC], [1, qw]])
                        S.inc(sc.activation(winbf[:, :, qt * qw:(qt + 1) * qw],
                                            srcw, AF.Copy), "gc")
                gc_win = S.val("gc")

                @blk.vector
                def _(v):
                    v.memset(ones_t[:, :], 1.0)
                    S.inc(v.memset(carry[:, :], 0.0), "vsc")
                    S.inc(v.memset(x3bf[:, :, 0:PAD], 0.0), "vsc")
                    prevv = None
                    for cc in range(KC):
                        for j in range(NTAP):
                            kcol = k0sb[:, cc, j:j + 1]
                            kbc = bass.AP(k0sb, kcol.offset,
                                          [list(kcol.ap[0]), [0, 128]])
                            prevv = v.tensor_tensor(diagall[:, cc, j, :],
                                                    idn[:, :], kbc, ALU.mult)
                    S.inc(prevv, "vtp")
                dg_tok = S.val("vtp")

                ln1_tok = {0: ln_stats_part1(blk, (lambda cc:
                                                   R[:, cc, 0:TT]),
                                             st2, st3, "stq", 1,
                                             bufs=(xbf_q, xsq_q))}
                apply_tok = {}

                def emit_ln1(tl):
                    ln_stats_part2(blk, rows, st2, st3, "stq", ln1_tok[tl])
                    ln_rows(blk, rows)
                    ln_bcast_tile(blk, rows, bc0, bc1)

                    @blk.vector
                    def _(v, tl=tl):
                        S.use(v, ("x1t", tl % 2))

                    ln_apply_tile(blk, (lambda cc, tl=tl:
                                        R[:, cc, tl * TT:(tl + 1) * TT]),
                                  (lambda cc, tl=tl: x1t[:, tl % 2, cc, :]),
                                  bc0, bc1)
                    apply_tok[tl] = S.val("vln")

                emit_ln1(0)
                for tl in range(NT):
                    t0, t1 = tl * TT, (tl + 1) * TT
                    vln_tile = apply_tok[tl]

                    x2tok = {}
                    for m in range(KC):
                        @blk.tensor
                        def _(t, tl=tl, m=m, vln_tile=vln_tile):
                            if m == 0:
                                S.wait(t, "vln", vln_tile)
                                if tl == 0:
                                    S.wait(t, "gc", gc_win)
                            S.use(t, ("x2ps", m % 2))
                            mm = None
                            for k in range(KC):
                                mm = t.matmul(x2ps[m % 2][:, :],
                                              winbf[:, k, m * 128:(m + 1) * 128],
                                              x1t[:, tl % 2, k, :],
                                              start=(k == 0), stop=(k == KC - 1))
                            tok = S.inc(mm, "tst")
                            if m == KC - 1:
                                S.release(("x1t", tl % 2), "tst", tok)
                            S.rel[("x2m", m)] = [("tst", tok)]

                        @blk.vector
                        def _(v, m=m, x2tok=x2tok):
                            S.use(v, ("x2m", m))
                            S.use(v, ("x2t", m))
                            ii = v.tensor_scalar(x2t[:, m, :], x2ps[m % 2][:, :],
                                                 prm[:, m:m + 1], None, ALU.add)
                            vv = S.inc(ii, "vmi")
                            x2tok[m] = vv
                            S.release(("x2ps", m % 2), "vmi", vv)

                    ln_stats_tile(blk, (lambda cc: x2t[:, cc, :]),
                                  rows, st0, st1,
                                  src_tok=(lambda cc, x2tok=x2tok:
                                           [("vmi", x2tok[cc])]))
                    ln_rows(blk, rows)
                    if tl + 1 < NT:
                        ln1_tok[tl + 1] = ln_stats_part1(
                            blk, (lambda cc, tl=tl:
                                  R[:, cc, (tl + 1) * TT:(tl + 2) * TT]),
                            st2, st3, "stq", 1, bufs=(xbf_q, xsq_q))
                    ln_bcast_tile(blk, rows, bc0, bc1)

                    @blk.vector
                    def _(v):
                        S.use(v, ("x3t",))

                    ln_apply_tile(blk, (lambda cc: x2t[:, cc, :]),
                                  (lambda cc: x3t[:, cc, :]),
                                  bc0, bc1, gb=True)
                    vln_x3 = S.val("vln")
                    for m in range(KC):
                        S.release(("x2t", m), "vln", vln_x3)

                    gmi_b = S.val("gmi")

                    @blk.gpsimd
                    def _(g, tl=tl, vln_x3=vln_x3, gmi_b=gmi_b):
                        g.wait_ge(S.sem["vln"], vln_x3)
                        if tl > 0:
                            g.wait_ge(S.sem["gmi"], gmi_b)
                        for cc in range(KC):
                            S.inc(g.tensor_copy(x3bf[:, cc, PAD:PAD + TT],
                                                x3t[:, cc, :]), "gmi")

                    if tl + 1 < NT:
                        emit_ln1(tl + 1)

                    x3rel = []
                    tap_tok = {}

                    def emit_evac(v, ec, tl=tl, t0=t0, t1=t1):
                        S.use(v, ("ytq", ec, tl))
                        S.use(v, ("Rs2", ec, tl))
                        ii = v.tensor_tensor(R[:, ec, t0:t1], R[:, ec, t0:t1],
                                             x2ps[ec % 2][:, :], ALU.add)
                        rv = S.inc(ii, "vev")
                        S.release(("x2ps", ec % 2), "vev", rv)
                        S.release(("Rs2", ec, tl), "vev", rv)

                    for cc in range(KC):
                        @blk.vector
                        def _(v, cc=cc, tl=tl, t0=t0, t1=t1, vln_x3=vln_x3,
                              x3rel=x3rel):
                            if cc == 0:
                                S.wait(v, "vln", vln_x3)
                            S.use(v, ("cst",))
                            S.use(v, ("carry", cc))
                            ii = v.tensor_tensor_scan(cst[:, :],
                                                      ones_t[:, :],
                                                      x3t[:, cc, :],
                                                      carry[:, cc:cc + 1],
                                                      ALU.mult, ALU.add)
                            vv = S.inc(ii, "vsc")
                            S.wait(v, "vsc", vv)
                            ii = v.tensor_copy(carry[:, cc:cc + 1],
                                               cst[:, TT - 1:TT])
                            cv = S.inc(ii, "vsc")
                            S.release(("carry", cc), "vsc", cv)
                            dcol = prm[:, 3 * KC + cc:3 * KC + cc + 1]
                            S.use(v, ("Rs2", cc, tl))
                            ii = v.scalar_tensor_tensor(R[:, cc, t0:t1],
                                                        cst[:, :], dcol,
                                                        R[:, cc, t0:t1],
                                                        ALU.mult, ALU.add)
                            rv = S.inc(ii, "vsc")
                            S.release(("cst",), "vsc", rv)
                            S.wait(v, "vsc", rv)
                            ii = v.tensor_tensor(R[:, cc, t0:t1], R[:, cc, t0:t1],
                                                 x3t[:, cc, :], ALU.add)
                            rv2 = S.inc(ii, "vsc")
                            S.release(("Rs2", cc, tl), "vsc", rv2)
                            x3rel.append(("vsc", rv2))
                            if cc >= 2:
                                emit_evac(v, cc - 2)

                        @blk.tensor
                        def _(t, cc=cc, tl=tl, t0=t0, t1=t1, gmi_b=gmi_b,
                              tap_tok=tap_tok):
                            if tl == 0 and cc == 0:
                                S.wait(t, "vtp", dg_tok)
                            S.wait(t, "gmi", gmi_b + cc + 1)
                            S.use(t, ("x2ps", cc % 2))
                            mm = None
                            for j in range(NTAP):
                                o0 = max(t0, j)
                                mm = t.matmul(x2ps[cc % 2][:, o0 - t0:TT],
                                              diagall[:, cc, j, :],
                                              x3bf[:, cc, PAD + o0 - t0 - j:
                                                   PAD + TT - j],
                                              start=(j == 0),
                                              stop=(j == NTAP - 1))
                            tok = S.inc(mm, "tw2")
                            tap_tok[cc] = tok
                            S.rel[("ytq", cc, tl)] = [("tw2", tok)]

                    @blk.vector
                    def _(v, tl=tl, t0=t0, t1=t1):
                        emit_evac(v, KC - 2)
                        emit_evac(v, KC - 1)

                    S.rel[("x3t",)] = x3rel + [("gmi", gmi_b + KC)]

                    if tl + 1 < NT:
                        @blk.gpsimd
                        def _(g, tap_tok=tap_tok):
                            g.wait_ge(S.sem["tw2"], tap_tok[KC - 1])
                            S.inc(g.tensor_copy(x3bf[:, :, 0:PAD],
                                                x3bf[:, :, TT:TT + PAD]), "gmi")
            S.barrier_clear()

        def dump(pairs):
            with nc.Block() as blk:
                @blk.sync
                def _(s):
                    for dst, srcb in pairs:
                        S.inc(s.dma_start(dst, srcb), "dq", 16)
                    S.wait(s, "dq")
            S.barrier_clear()

        emit_ffn(0)
        if c.dbg:
            dump([(d_R1, R[:, :, :]), (d_k0, k0sb[:, :, :])])
        emit_attn()
        if c.dbg:
            dump([(d_R2, R[:, :, :])])
        emit_ffn(1)

        with nc.Block() as blk:
            @blk.sync
            def _(s):
                for kc in range(KC):
                    S.inc(s.dma_start(
                        outd.rearrange("(c p) l -> c p l", p=128)[kc],
                        R[:, kc, :]), "oq", 16)
                S.wait(s, "oq")

    return nc


def _col_layout(v, KC):
    return np.ascontiguousarray(np.asarray(v, dtype=np.float32).reshape(KC, 128).T)


def make_core_inputs(c, x_b, W_in, b_in, gamma, beta, A_re, A_im, B_re, B_im,
                     C_re, C_im, D_re, gate_w, w1, w3, w2):
    KC = c.KC
    f32 = lambda a: np.ascontiguousarray(np.asarray(a, dtype=np.float32))
    return {
        "xT": f32(x_b.T),
        "w1": f32(w1), "w3": f32(w3), "w2": f32(w2),
        "gateT": f32(np.asarray(gate_w).T),
        "winT": f32(np.asarray(W_in).T),
        "bin_c": _col_layout(b_in, KC),
        "gam_c": _col_layout(gamma, KC),
        "bet_c": _col_layout(beta, KC),
        "dre_c": _col_layout(D_re, KC),
        "abcol": f32(np.stack([np.asarray(A_re), np.asarray(A_im),
                               np.asarray(B_re), np.asarray(B_im)], axis=1)),
        "ctre": f32(np.asarray(C_re).T),
        "ctim": f32(np.asarray(C_im).T),
        "onec_bf": np.ones((128, 1), dtype=ml_dtypes.bfloat16),
        "oner_bf": np.ones((1, 128), dtype=ml_dtypes.bfloat16),
        "oner_f": np.ones((1, 128), dtype=np.float32),
        "identd": np.eye(128, dtype=ml_dtypes.bfloat16),
    }


_NC_CACHE = {}


def kernel(L0_attn_x, L0_attn_W_in, L0_attn_b_in, L0_attn_gamma, L0_attn_beta,
           L0_attn_A_re, L0_attn_A_im, L0_attn_B_re, L0_attn_B_im,
           L0_attn_C_re, L0_attn_C_im, L0_attn_D_re, L0_attn_D_im,
           L0_ffn_gate_w, L0_ffn_w1, L0_ffn_w3, L0_ffn_w2):
    c = Cfg()
    if "nc" not in _NC_CACHE:
        _NC_CACHE["nc"] = build_nc(c)
    nc = _NC_CACHE["nc"]
    x = np.asarray(L0_attn_x, dtype=np.float32)
    nb = x.shape[0]
    base = make_core_inputs(c, x[0], L0_attn_W_in, L0_attn_b_in, L0_attn_gamma,
                            L0_attn_beta, L0_attn_A_re, L0_attn_A_im,
                            L0_attn_B_re, L0_attn_B_im, L0_attn_C_re,
                            L0_attn_C_im, L0_attn_D_re, L0_ffn_gate_w,
                            L0_ffn_w1, L0_ffn_w3, L0_ffn_w2)
    in_maps = []
    for b in range(nb):
        m = dict(base)
        m["xT"] = np.ascontiguousarray(x[b].T)
        in_maps.append(m)
    res = run_bass_kernel_spmd(nc, in_maps, core_ids=list(range(nb)))
    out = np.stack([np.asarray(res.results[b]["out"], dtype=np.float32).T
                    for b in range(nb)], axis=0)
    return np.ascontiguousarray(out)


# version 31
# speedup vs baseline: 1.0096x; 1.0019x over previous
import numpy as np
import ml_dtypes
from contextlib import ExitStack

import concourse.bass as bass
import concourse.mybir as mybir
from concourse.bass_utils import run_bass_kernel_spmd

F32 = mybir.dt.float32
BF16 = mybir.dt.bfloat16
AF = mybir.ActivationFunctionType
ALU = mybir.AluOpType
AX = mybir.AxisListType

BIG = 1.0e30


class Cfg:
    def __init__(self, D=1024, DI=512, NE=16, L=2048, TT=512, NTAP=12, NST=64,
                 TOPK=4, EPS=1e-5):
        self.D, self.DI, self.NE, self.L, self.TT = D, DI, NE, L, TT
        self.NTAP, self.NST, self.TOPK, self.EPS = NTAP, NST, TOPK, EPS
        self.dbg = False
        self.KC = D // 128
        self.IC = DI // 128
        self.NT = L // TT
        self.LC = L // 128
        assert D % 128 == 0 and DI % 128 == 0 and L % TT == 0 and TT % 128 == 0
        assert self.KC % 2 == 0 and self.LC % 4 == 0


class B:
    def __init__(self, nc, cfg, ctx):
        self.nc = nc
        self.c = cfg
        names = ["dq", "dwq", "oq", "gc",
                 "tst", "tgu", "tw2", "tsc",
                 "vln", "vrow", "vau", "vev", "vmi", "vsc", "vtp", "vst",
                 "srow", "sst", "sact",
                 "gmi", "gact"]
        self.sem = {n: ctx.enter_context(nc.semaphore(n)) for n in names}
        self.cnt = {n: 0 for n in names}
        self.rel = {}

    def inc(self, instr, name, k=1):
        instr.then_inc(self.sem[name], k)
        self.cnt[name] += k
        return self.cnt[name]

    def val(self, name):
        return self.cnt[name]

    def wait(self, eng, name, v=None):
        v = self.cnt[name] if v is None else v
        if v > 0:
            eng.wait_ge(self.sem[name], v)

    def use(self, eng, res):
        for (n, v) in self.rel.get(res, []):
            if v > 0:
                eng.wait_ge(self.sem[n], v)

    def release(self, res, name, v):
        self.rel[res] = [(name, v)]

    def barrier_clear(self):
        self.rel = {}


def build_nc(c: Cfg):
    nc = bass.Bass("TRN2", target_bir_lowering=False)
    D, DI, NE, L, TT = c.D, c.DI, c.NE, c.L, c.TT
    KC, IC, NT, LC = c.KC, c.IC, c.NT, c.LC
    NTAP, NST = c.NTAP, c.NST
    HALF = D // 2

    xT = nc.dram_tensor("xT", [D, L], F32, kind="ExternalInput").ap()
    w1d = nc.dram_tensor("w1", [NE, D, DI], F32, kind="ExternalInput").ap()
    w3d = nc.dram_tensor("w3", [NE, D, DI], F32, kind="ExternalInput").ap()
    w2d = nc.dram_tensor("w2", [NE, DI, D], F32, kind="ExternalInput").ap()
    gateT = nc.dram_tensor("gateT", [D, NE], F32, kind="ExternalInput").ap()
    winT = nc.dram_tensor("winT", [D, D], F32, kind="ExternalInput").ap()
    bin_c = nc.dram_tensor("bin_c", [128, KC], F32, kind="ExternalInput").ap()
    gam_c = nc.dram_tensor("gam_c", [128, KC], F32, kind="ExternalInput").ap()
    bet_c = nc.dram_tensor("bet_c", [128, KC], F32, kind="ExternalInput").ap()
    dre_c = nc.dram_tensor("dre_c", [128, KC], F32, kind="ExternalInput").ap()
    abcold = nc.dram_tensor("abcol", [NST, 4], F32, kind="ExternalInput").ap()
    ctred = nc.dram_tensor("ctre", [NST, D], F32, kind="ExternalInput").ap()
    ctimd = nc.dram_tensor("ctim", [NST, D], F32, kind="ExternalInput").ap()
    onecd = nc.dram_tensor("onec_bf", [128, 1], BF16, kind="ExternalInput").ap()
    onerd = nc.dram_tensor("oner_bf", [1, 128], BF16, kind="ExternalInput").ap()
    onerfd = nc.dram_tensor("oner_f", [1, 128], F32, kind="ExternalInput").ap()
    identd = nc.dram_tensor("identd", [128, 128], BF16, kind="ExternalInput").ap()
    outd = nc.dram_tensor("out", [D, L], F32, kind="ExternalOutput").ap()
    if c.dbg:
        d_xhat = nc.dram_tensor("d_xhat", [128, KC, L], BF16, kind="ExternalOutput").ap()
        d_sc = nc.dram_tensor("d_sc", [NE, L], BF16, kind="ExternalOutput").ap()
        d_R1 = nc.dram_tensor("d_R1", [128, KC, L], F32, kind="ExternalOutput").ap()
        d_x3 = nc.dram_tensor("d_x3", [128, KC, L], BF16, kind="ExternalOutput").ap()
        d_R2 = nc.dram_tensor("d_R2", [128, KC, L], F32, kind="ExternalOutput").ap()
        d_k0 = nc.dram_tensor("d_k0", [128, KC, NTAP], F32, kind="ExternalOutput").ap()
        d_x1 = nc.dram_tensor("d_x1", [128, KC, TT], BF16, kind="ExternalOutput").ap()
        d_x2 = nc.dram_tensor("d_x2", [128, KC, TT], F32, kind="ExternalOutput").ap()

    top = ExitStack()
    with top:
        S = B(nc, c, top)
        sb = lambda name, shp, dt: top.enter_context(nc.sbuf_tensor(name, shp, dt))

        R = sb("R", [128, KC, L], F32)
        WSZ = 2 * KC * DI + IC * D
        OW1, OW3, OW2 = 0, KC * DI, 2 * KC * DI
        LSZ = max(KC * DI, IC * D, KC * HALF)
        gatebf = sb("gatebf", [128, KC, NE], BF16)
        prm = sb("prm", [128, 4 * KC], F32)
        acol = sb("acol", [NST, 4], F32)
        k0sb = sb("k0sb", [128, KC, NTAP], F32)
        onec = sb("onec", [128, 1], BF16)
        oner2 = sb("oner2", [1, 128], BF16)
        onerf = sb("onerf", [1, 128], F32)
        idn = sb("idn", [128, 128], BF16)
        ones_t = sb("ones_t", [128, TT], F32)
        xbf_s = sb("xbf_s", [128, 2, TT], BF16)
        xsq_s = sb("xsq_s", [128, 2, TT], BF16)
        lnt = sb("lnt", [128, 2, TT], F32)
        start_mem = ExitStack()
        ssb = lambda name, shp, dt: start_mem.enter_context(
            nc.sbuf_tensor(name, shp, dt))
        gland = ssb("gland", [128, KC, NE], F32)
        ctbf = ssb("ctbf", [NST, 2 * D], BF16)
        ctl = ssb("ctl", [NST, D], F32)
        ajb = ssb("ajb", [NST, 2 * NTAP], F32)
        ajbbf = ssb("ajbbf", [NST, 2 * NTAP], BF16)
        scr2 = ssb("scr2", [NST, 2], F32)

        dq_small = [0]
        dq_full = [0]
        with nc.Block() as blk:
            @blk.sync
            def _(s):
                S.inc(s.dma_start(gland[:, :, :],
                                  gateT.rearrange("(c p) e -> p c e", p=128)), "dq", 16)
                S.inc(s.dma_start(prm[:, 0 * KC:1 * KC], bin_c), "dq", 16)
                S.inc(s.dma_start(prm[:, 1 * KC:2 * KC], gam_c), "dq", 16)
                S.inc(s.dma_start(prm[:, 2 * KC:3 * KC], bet_c), "dq", 16)
                S.inc(s.dma_start(prm[:, 3 * KC:4 * KC], dre_c), "dq", 16)
                S.inc(s.dma_start(acol[:, :], abcold), "dq", 16)
                S.inc(s.dma_start(onec[:, :], onecd), "dq", 16)
                S.inc(s.dma_start(oner2[:, :], onerd), "dq", 16)
                S.inc(s.dma_start(onerf[:, :], onerfd), "dq", 16)
                S.inc(s.dma_start(idn[:, :], identd), "dq", 16)
                S.inc(s.dma_start(ctl[:, :], ctred), "dq", 16)
                dq_small[0] = S.val("dq")
                for kc in range(KC):
                    S.inc(s.dma_start(R[:, kc, :],
                                      xT.rearrange("(c p) l -> c p l", p=128)[kc]),
                          "dq", 16)
                dq_full[0] = S.val("dq")

            @blk.scalar
            def _(sc):
                S.wait(sc, "dq", dq_small[0])
                S.inc(sc.activation(gatebf[:, :, :], gland[:, :, :], AF.Copy), "gc")
                S.inc(sc.activation(ctbf[:, 0:D], ctl[:, :], AF.Copy), "gc")

            @blk.vector
            def _(v):
                v.memset(ones_t[:, :], 1.0)

        with nc.Block() as blk:
            @blk.sync
            def _(s):
                S.inc(s.dma_start(ctl[:, :], ctimd), "dq", 16)

            @blk.scalar
            def _(sc):
                S.wait(sc, "dq")
                S.inc(sc.activation(ctbf[:, D:2 * D], ctl[:, :], AF.Copy), "gc")

        with nc.Block() as blk:
            @blk.vector
            def _(v):
                are, aim = acol[:, 0:1], acol[:, 1:2]
                v.tensor_copy(ajb[:, 0:1], acol[:, 2:3])
                vv = S.inc(v.tensor_copy(ajb[:, NTAP:NTAP + 1], acol[:, 3:4]), "vmi")
                for j in range(1, NTAP):
                    S.wait(v, "vmi", vv)
                    pr, pi = ajb[:, j - 1:j], ajb[:, NTAP + j - 1:NTAP + j]
                    cr, ci_ = ajb[:, j:j + 1], ajb[:, NTAP + j:NTAP + j + 1]
                    v.tensor_tensor(scr2[:, 0:1], pi, aim, ALU.mult)
                    vv = S.inc(v.tensor_tensor(scr2[:, 1:2], pr, aim, ALU.mult), "vmi")
                    S.wait(v, "vmi", vv)
                    v.scalar_tensor_tensor(cr, pr, are, scr2[:, 0:1], ALU.mult,
                                           ALU.subtract)
                    vv = S.inc(v.scalar_tensor_tensor(ci_, pi, are, scr2[:, 1:2],
                                                      ALU.mult, ALU.add), "vmi")
                S.wait(v, "vmi", vv)
                v.tensor_copy(ajbbf[:, 0:NTAP], ajb[:, 0:NTAP])
                S.inc(v.tensor_scalar(ajbbf[:, NTAP:2 * NTAP], ajb[:, NTAP:2 * NTAP],
                                      -1.0, None, ALU.mult), "vmi")

        with ExitStack() as ps0:
            k0ps = ps0.enter_context(nc.psum_tensor("k0ps", [128, 512], F32))
            with nc.Block() as blk:
                for kc in range(KC):
                    @blk.tensor
                    def _(t, kc=kc):
                        if kc == 0:
                            S.wait(t, "gc")
                            S.wait(t, "vmi")
                        S.use(t, "k0ps")
                        t.matmul(k0ps[:, 0:NTAP], ctbf[:, kc * 128:(kc + 1) * 128],
                                 ajbbf[:, 0:NTAP], start=True, stop=False)
                        S.inc(t.matmul(k0ps[:, 0:NTAP],
                                       ctbf[:, D + kc * 128:D + (kc + 1) * 128],
                                       ajbbf[:, NTAP:2 * NTAP], start=False,
                                       stop=True), "tst")

                    @blk.vector
                    def _(v, kc=kc):
                        S.wait(v, "tst")
                        vv = S.inc(v.tensor_copy(k0sb[:, kc, :], k0ps[:, 0:NTAP]), "vmi")
                        S.release("k0ps", "vmi", vv)
        with nc.Block() as blk:
            @blk.vector
            def _(v):
                S.wait(v, "vmi")
        start_mem.close()
        S.barrier_clear()

        def ln_stats_part1(blk, src_chunk, st0, st1, skey, bslot,
                           src_tok=None, bufs=None):
            xb, xq = bufs if bufs is not None else (xbf_s, xsq_s)
            sst_base = S.val("vst")
            tst_base = S.val("tst")

            @blk.vector
            def _(vq):
                for cc in range(KC):
                    bi = cc % 2 + 2 * bslot * 0
                    if src_tok is not None:
                        for (n_, v_) in src_tok(cc):
                            vq.wait_ge(S.sem[n_], v_)
                    if cc >= 2:
                        S.wait(vq, "tst", tst_base + cc - 1)
                    else:
                        S.use(vq, ("stat", bslot, bi))
                    vq.tensor_copy(xb[:, bi, :], src_chunk(cc))
                    S.inc(vq.tensor_tensor(xq[:, bi, :], src_chunk(cc),
                                           src_chunk(cc), ALU.mult), "vst")

            @blk.tensor
            def _(t):
                S.use(t, (skey, 0))
                S.use(t, (skey, 1))
                for cc in range(KC):
                    bi = cc % 2
                    S.wait(t, "vst", sst_base + cc + 1)
                    t.matmul(st0[:, :], onec[:, :], xb[:, bi, :],
                             start=(cc == 0), stop=(cc == KC - 1))
                    v = S.inc(t.matmul(st1[:, :], onec[:, :], xq[:, bi, :],
                                       start=(cc == 0), stop=(cc == KC - 1)), "tst")
                    S.release(("stat", bslot, bi), "tst", v)
            return S.val("tst")

        def ln_stats_part2(blk, rows, st0, st1, skey, mm_tok):
            @blk.vector
            def _(v):
                S.wait(v, "tst", mm_tok)
                S.use(v, ("rows",))
                v.tensor_copy(rows[0:1, 0, :], st0[0:1, :])
                vv = S.inc(v.tensor_copy(rows[0:1, 1, :], st1[0:1, :]), "vrow")
                S.release((skey, 0), "vrow", vv)
                S.release((skey, 1), "vrow", vv)

        def ln_stats_tile(blk, src_chunk, rows, st0, st1, src_tok=None,
                          skey="stp", bslot=0):
            tok = ln_stats_part1(blk, src_chunk, st0, st1, skey, bslot, src_tok)
            ln_stats_part2(blk, rows, st0, st1, skey, tok)

        def ln_rows(blk, rows):
            def rr(i):
                return rows[0:1, i, :]

            @blk.vector
            def _(v):
                S.wait(v, "vrow")
                v.tensor_scalar(rr(2), rr(0), 1.0 / D, None, ALU.mult)
                vv = S.inc(v.tensor_scalar(rr(3), rr(1), 1.0 / D, None, ALU.mult),
                           "vrow")
                S.wait(v, "vrow", vv)
                vv = S.inc(v.tensor_tensor(rr(4), rr(2), rr(2), ALU.mult), "vrow")
                S.wait(v, "vrow", vv)
                vv = S.inc(v.scalar_tensor_tensor(rr(3), rr(4), -1.0, rr(3),
                                                  ALU.mult, ALU.add), "vrow")
                S.wait(v, "vrow", vv)
                vv = S.inc(v.tensor_scalar(rr(4), rr(3), float(c.EPS), None,
                                           ALU.add), "vrow")

            @blk.scalar
            def _(sc):
                S.wait(sc, "vrow")
                vv = S.inc(sc.activation(rr(3), rr(4), AF.Ln), "srow")
                S.wait(sc, "srow", vv)
                S.inc(sc.activation(rr(5), rr(3), AF.Exp, scale=-0.5), "srow")

            @blk.vector
            def _(v):
                S.wait(v, "srow")
                S.inc(v.tensor_tensor(rr(6), rr(2), rr(5), ALU.mult), "vrow")

        def ln_bcast_tile(blk, rows, bc0, bc1):
            @blk.tensor
            def _(t):
                S.wait(t, "vrow")
                S.wait(t, "srow")
                S.use(t, "bc0")
                S.use(t, "bc1")
                t.matmul(bc0[:, :], onerf[:, :], rows[0:1, 5, :],
                         start=True, stop=True)
                tok = S.inc(t.matmul(bc1[:, :], onerf[:, :], rows[0:1, 6, :],
                                     start=True, stop=True), "tst")
                S.release(("rows",), "tst", tok)

        def ln_apply_tile(blk, src_chunk, dst_chunk, bc0, bc1, gb=False):
            @blk.vector
            def _(v):
                S.wait(v, "tst")
                toks = {}
                last = 0

                def emit_b(pc):
                    nonlocal last
                    S.wait(v, "vln", toks[pc])
                    if not gb:
                        ib = v.tensor_tensor(dst_chunk(pc), lnt[:, pc % 2, :],
                                             bc1[:, :], ALU.subtract)
                        last = S.inc(ib, "vln")
                    else:
                        ib = v.tensor_tensor(lnt[:, pc % 2, :], lnt[:, pc % 2, :],
                                             bc1[:, :], ALU.subtract)
                        vv2 = S.inc(ib, "vln")
                        S.wait(v, "vln", vv2)
                        gcol = prm[:, KC + pc:KC + pc + 1]
                        bcol = prm[:, 2 * KC + pc:2 * KC + pc + 1]
                        bb = bass.AP(bcol.tensor, bcol.offset,
                                     [list(bcol.ap[0]), [0, TT]])
                        ic2 = v.scalar_tensor_tensor(dst_chunk(pc),
                                                     lnt[:, pc % 2, :], gcol, bb,
                                                     ALU.mult, ALU.add)
                        last = S.inc(ic2, "vln")
                    S.release(("lnt", pc % 2), "vln", last)

                for cc in range(KC):
                    S.use(v, ("lnt", cc % 2))
                    ia = v.tensor_tensor(lnt[:, cc % 2, :], src_chunk(cc), bc0[:, :],
                                         ALU.mult)
                    toks[cc] = S.inc(ia, "vln")
                    if cc >= 1:
                        emit_b(cc - 1)
                emit_b(KC - 1)
                S.release("bc0", "vln", last)
                S.release("bc1", "vln", last)

        def emit_ffn(call):
            with ExitStack() as mem, nc.Block() as blk:
                if call == 0:
                    @blk.vector
                    def _(v):
                        S.wait(v, "dq", dq_full[0])
                psn = lambda name, shp, dt=F32: mem.enter_context(
                    nc.psum_tensor(f"{name}_f{call}", shp, dt))
                sbf = lambda name, shp, dt: mem.enter_context(
                    nc.sbuf_tensor(f"{name}_f{call}", shp, dt))
                wbf = [sbf(f"wbf{i}", [128, WSZ], BF16) for i in range(2)]
                wland = sbf("wland", [128, LSZ // 2], F32)
                xhat = sbf("xhat", [128, KC, L], BF16)
                scbc = sbf("scbc", [128, NT, TT], BF16)
                act_s = sbf("act_s", [128, IC, TT], BF16)
                rows = sbf("rows", [1, 7, TT], F32)
                silu_s = sbf("silu_s", [128, IC, TT], BF16)
                tmp_bf = sbf("tmp_bf", [128, IC, TT], BF16)
                scoresF = sbf("scoresF", [NE, L], BF16)
                scores0 = sbf("scores0", [1, L], BF16)
                logf_bf = sbf("logf_bf", [NE, TT], BF16)
                ltok = sbf("ltok", [128, LC, NE], F32)
                wrk = sbf("wrk", [128, LC, NE], F32)
                msk = sbf("msk", [128, LC, NE], F32)
                mcol = sbf("mcol", [128, LC, 1], F32)
                rden = sbf("rden", [128, LC, 1], F32)
                stok_bf = sbf("stok_bf", [128, LC, NE], BF16)

                def w_t(e, part, k, m):
                    off = [OW1, OW3, OW2][part] + k * ([DI, DI, D][part]) + m * 128
                    return wbf[e % 2][:, off:off + 128]
                with ExitStack() as pro:
                    ppsn = lambda name, shp, dt=F32: pro.enter_context(
                        nc.psum_tensor(f"{name}_p{call}", shp, dt))
                    st0 = ppsn("st0", [1, TT])
                    st1 = ppsn("st1", [1, TT])
                    bc0 = ppsn("bc0", [128, TT])
                    bc1 = ppsn("bc1", [128, TT])
                    lgf = ppsn("lgf", [NE, TT])
                    rtp = ppsn("rtp", [128, 4 * 256], BF16)
                    stp0 = ppsn("stp0", [NE, 1024], BF16)
                    stp1 = ppsn("stp1", [NE, 1024], BF16)
                    stp = [stp0, stp1]

                    for tl in range(NT):
                        ln_stats_tile(blk, (lambda cc, tl=tl:
                                            R[:, cc, tl * TT:(tl + 1) * TT]),
                                      rows, st0, st1)
                        ln_rows(blk, rows)
                        ln_bcast_tile(blk, rows, bc0, bc1)
                        ln_apply_tile(blk, (lambda cc, tl=tl:
                                            R[:, cc, tl * TT:(tl + 1) * TT]),
                                      (lambda cc, tl=tl:
                                       xhat[:, cc, tl * TT:(tl + 1) * TT]),
                                      bc0, bc1)

                    TC = TT // 128
                    for tl in range(NT):
                        t0, t1 = tl * TT, (tl + 1) * TT

                        @blk.tensor
                        def _(t, t0=t0, t1=t1):
                            S.wait(t, "vln")
                            S.use(t, "lgf")
                            for k in range(KC):
                                i = t.matmul(lgf[:, :], gatebf[:, k, :],
                                             xhat[:, k, t0:t1],
                                             start=(k == 0), stop=(k == KC - 1))
                            S.inc(i, "tst")

                        @blk.vector
                        def _(v):
                            S.wait(v, "tst")
                            S.use(v, "logf")
                            vv = S.inc(v.tensor_copy(logf_bf[:, :], lgf[:, :]),
                                       "vmi")
                            S.release("lgf", "vmi", vv)

                        @blk.tensor
                        def _(t, tl=tl):
                            S.wait(t, "vmi")
                            S.use(t, "rtp")
                            for i in range(TC):
                                ii = t.transpose(rtp[:, i * 256:i * 256 + NE],
                                                 logf_bf[:, i * 128:(i + 1) * 128],
                                                 idn[0:NE, 0:NE])
                            tok = S.inc(ii, "tst")
                            S.release("logf", "tst", tok)

                        @blk.vector
                        def _(v, tl=tl):
                            S.wait(v, "tst")
                            dst = bass.AP(ltok, tl * TC * NE,
                                          [[LC * NE, 128], [NE, TC], [1, NE]])
                            srcp = bass.AP(rtp, 0,
                                           [[4 * 256, 128], [256, TC], [1, NE]])
                            vv = S.inc(v.tensor_copy(dst, srcp), "vmi")
                            S.release("rtp", "vmi", vv)

                    @blk.vector
                    def _(v):
                        S.wait(v, "vmi")
                        full = ltok[:, :, :]
                        mb = bass.AP(mcol, 0, [[LC, 128], [1, LC], [0, NE]])
                        vv = S.inc(v.reduce_max(mcol[:, :, :], full, axis=AX.X),
                                   "vmi")
                        S.wait(v, "vmi", vv)
                        S.inc(v.tensor_tensor(wrk[:, :, :], full, mb, ALU.subtract),
                              "vmi")

                    @blk.scalar
                    def _(sc):
                        S.wait(sc, "vmi")
                        S.inc(sc.activation(ltok[:, :, :], wrk[:, :, :], AF.Exp),
                              "sst")

                    @blk.vector
                    def _(v):
                        S.wait(v, "sst")
                        full = ltok[:, :, :]
                        mb = bass.AP(mcol, 0, [[LC, 128], [1, LC], [0, NE]])
                        vv = S.inc(v.tensor_copy(wrk[:, :, :], full), "vmi")
                        for it in range(c.TOPK - 1):
                            S.wait(v, "vmi", vv)
                            vv = S.inc(v.reduce_max(mcol[:, :, :], wrk[:, :, :],
                                                    axis=AX.X), "vmi")
                            S.wait(v, "vmi", vv)
                            vv = S.inc(v.tensor_tensor(msk[:, :, :], wrk[:, :, :],
                                                       mb, ALU.is_ge), "vmi")
                            S.wait(v, "vmi", vv)
                            vv = S.inc(v.scalar_tensor_tensor(wrk[:, :, :],
                                                              msk[:, :, :], -BIG,
                                                              wrk[:, :, :],
                                                              ALU.mult, ALU.add),
                                       "vmi")
                        S.wait(v, "vmi", vv)
                        vv = S.inc(v.reduce_max(mcol[:, :, :], wrk[:, :, :],
                                                axis=AX.X), "vmi")
                        S.wait(v, "vmi", vv)
                        vv = S.inc(v.tensor_tensor(msk[:, :, :], full, mb,
                                                   ALU.is_ge), "vmi")
                        S.wait(v, "vmi", vv)
                        vv = S.inc(v.tensor_tensor(wrk[:, :, :], full, msk[:, :, :],
                                                   ALU.mult), "vmi")
                        S.wait(v, "vmi", vv)
                        vv = S.inc(v.reduce_sum(mcol[:, :, :], wrk[:, :, :],
                                                axis=AX.X), "vmi")
                        S.wait(v, "vmi", vv)
                        vv = S.inc(v.reciprocal(rden[:, :, :], mcol[:, :, :]), "vmi")
                        S.wait(v, "vmi", vv)
                        rb = bass.AP(rden, 0, [[LC, 128], [1, LC], [0, NE]])
                        vv = S.inc(v.tensor_tensor(stok_bf[:, :, :], wrk[:, :, :],
                                                   rb, ALU.mult), "vmi")

                    for lc in range(LC):
                        @blk.tensor
                        def _(t, lc=lc):
                            if lc == 0:
                                S.wait(t, "vmi")
                            S.use(t, ("stp", lc % 2))
                            S.inc(t.transpose(
                                stp[lc % 2][:, 0:128],
                                stok_bf[:, lc, :], idn[:, :]), "tst")

                        @blk.vector
                        def _(v, lc=lc):
                            S.wait(v, "tst")
                            vv = S.inc(v.tensor_copy(
                                scoresF[:, lc * 128:(lc + 1) * 128],
                                stp[lc % 2][:, 0:128]), "vtp")
                            S.release(("stp", lc % 2), "vtp", vv)


                A = [psn(f"A{i}", [128, TT]) for i in range(4)]
                C = [psn(f"C{i}", [128, TT]) for i in range(4)]

                @blk.tensor
                def _(t):
                    S.wait(t, "tst", S.val("tst"))
                    S.wait(t, "vln", S.val("vln"))
                    S.wait(t, "vtp", S.val("vtp"))
                    S.wait(t, "vmi", S.val("vmi"))
                    S.wait(t, "vrow", S.val("vrow"))

                def emit_wdma(e):
                    gc_now = S.val("gc")

                    @blk.sync
                    def _(s, e=e, gc_now=gc_now):
                        hcnt = 0
                        for p, (srcd, wid) in enumerate([(w1d, KC * DI),
                                                         (w3d, KC * DI),
                                                         (w2d, IC * D)]):
                            nch = srcd.shape[1] // 128
                            hch = nch // 2
                            wper = wid // nch
                            for hh in range(2):
                                S.wait(s, "gc", gc_now + hcnt)
                                S.use(s, ("wbuf", e % 2))
                                ap = srcd[e][hh * hch * 128:(hh + 1) * hch * 128, :] \
                                    .rearrange("(q p) n -> p q n", p=128)
                                dst = bass.AP(wland, 0,
                                              [[LSZ // 2, 128], [wper, hch],
                                               [1, wper]])
                                S.inc(s.dma_start(dst, ap), "dwq", 16)
                                hcnt += 1

                WPARTS = [(OW1, KC * DI), (OW3, KC * DI), (OW2, IC * D)]
                CAST_SLICES = {0: [0, 1], 1: [2, 3], 2: [4], 3: [5]}

                def emit_one_cast(v, e, hcnt):
                    p, hh = hcnt // 2, hcnt % 2
                    off, wid = WPARTS[p]
                    dwq_need = DWQ_BASE[e] + 16 * (hcnt + 1)
                    v.wait_ge(S.sem["dwq"], dwq_need)
                    S.inc(v.tensor_copy(
                        wbf[e % 2][:, off + hh * (wid // 2):
                                   off + (hh + 1) * (wid // 2)],
                        wland[:, 0:wid // 2]), "gc")

                def emit_wcast_slice(e, tl, gcb):
                    pass

                DWQ_BASE = {}

                def emit_wdma_tracked(e):
                    DWQ_BASE[e] = S.val("dwq")
                    emit_wdma(e)

                def emit_wcasts_now(e):
                    @blk.vector
                    def _(v, e=e):
                        for hcnt in range(6):
                            emit_one_cast(v, e, hcnt)

                def emit_w2_full(ctx):
                    (e, tl, t0, t1, g_tok2, gc_done) = ctx
                    nq = KC // 2

                    def w2_mms(t, q, ks):
                        pair = q % 2
                        ms = (2 * q, 2 * q + 1)
                        ks = list(ks)
                        S.wait(t, "gact", g_tok2[ks[-1]])
                        mm = None
                        for k in ks:
                            for m in ms:
                                mm = t.matmul(C[2 * pair + (m % 2)][:, :],
                                              w_t(e, 2, k, m),
                                              act_s[:, k, :],
                                              start=(k == 0),
                                              stop=(k == IC - 1))
                        return mm

                    def w2_fin(q, tok):
                        if q == nq - 1:
                            S.release(("wbuf", e % 2), "tw2", tok)
                            for ci in range(IC):
                                S.release(("act", ci), "tw2", tok)
                            if tl == NT - 1:
                                S.release(("scbc",), "tw2", tok)
                        S.rel[("q", q)] = [("tw2", tok)]

                    def emit_rupd(q):
                        pair = q % 2

                        @blk.vector
                        def _(v, q=q, pair=pair):
                            S.use(v, ("q", q))
                            vv = 0
                            for m in (2 * q, 2 * q + 1):
                                S.use(v, ("Rs", m, tl))
                                ii = v.scalar_tensor_tensor(
                                    R[:, m, t0:t1], C[2 * pair + (m % 2)][:, :],
                                    0.5, R[:, m, t0:t1], ALU.mult, ALU.add)
                                vv = S.inc(ii, "vev")
                                S.release(("Rs", m, tl), "vev", vv)
                            S.release(("Cp", pair), "vev", vv)

                    if nq >= 2:
                        @blk.tensor
                        def _(t):
                            if tl == 0:
                                S.wait(t, "gc", gc_done)
                            S.use(t, ("Cp", 0))
                            w2_mms(t, 0, range(IC - 1))
                            S.use(t, ("Cp", 1))
                            w2_mms(t, 1, range(IC - 1))
                            tok0 = S.inc(w2_mms(t, 0, [IC - 1]), "tw2")
                            w2_fin(0, tok0)
                            tok1 = S.inc(w2_mms(t, 1, [IC - 1]), "tw2")
                            w2_fin(1, tok1)
                        emit_rupd(0)
                        emit_rupd(1)
                        for q in range(2, nq):
                            @blk.tensor
                            def _(t, q=q):
                                S.use(t, ("Cp", q % 2))
                                tok = S.inc(w2_mms(t, q, range(IC)), "tw2")
                                w2_fin(q, tok)
                            emit_rupd(q)
                    else:
                        @blk.tensor
                        def _(t):
                            if tl == 0:
                                S.wait(t, "gc", gc_done)
                            S.use(t, ("Cp", 0))
                            tok = S.inc(w2_mms(t, 0, range(IC)), "tw2")
                            w2_fin(0, tok)
                        emit_rupd(0)

                gc_base = {0: S.val("gc")}
                emit_wdma_tracked(0)
                emit_wcasts_now(0)
                gcd = {0: S.val("gc")}
                pend = None
                for e in range(NE):
                    @blk.sync
                    def _(s, e=e):
                        if e == 0:
                            S.wait(s, "vtp", S.val("vtp"))
                        S.use(s, ("sc0",))
                        S.inc(s.dma_start(scores0[0:1, :], scoresF[e:e + 1, :]),
                              "dq", 16)
                    sc0_tok = S.val("dq")

                    if e + 1 < NE:
                        gc_base[e + 1] = S.val("gc")
                        emit_wdma_tracked(e + 1)
                        gcd[e + 1] = gc_base[e + 1] + 6
                    gc_done = gcd[e]
                    sc_ev_tok = {}

                    for tl in range(NT):
                        t0, t1 = tl * TT, (tl + 1) * TT
                        gu_tok = {}
                        vau_b = S.val("vau")

                        @blk.tensor
                        def _(t, e=e, tl=tl, t0=t0, t1=t1, gu_tok=gu_tok,
                              vau_b=vau_b, gc_done=gc_done):
                            if tl == 0:
                                S.wait(t, "gc", gc_done - 2)
                                if e == 0:
                                    S.wait(t, "vln", S.val("vln"))
                                    S.wait(t, "vev", S.val("vev"))
                            S.wait(t, "vau", vau_b)
                            for ci in range(IC):
                                pair = ci % 2
                                if ci >= 2:
                                    S.wait(t, "vau", vau_b + ci - 1)
                                for k in range(KC):
                                    mm = t.matmul(A[2 * pair][:, :], w_t(e, 0, k, ci),
                                                  xhat[:, k, t0:t1],
                                                  start=(k == 0), stop=(k == KC - 1))
                                g_tok = S.inc(mm, "tgu")
                                for k in range(KC):
                                    mm = t.matmul(A[2 * pair + 1][:, :],
                                                  w_t(e, 1, k, ci),
                                                  xhat[:, k, t0:t1],
                                                  start=(k == 0), stop=(k == KC - 1))
                                u_tok = S.inc(mm, "tgu")
                                gu_tok[ci] = (g_tok, u_tok)

                        s_tok = {}

                        @blk.scalar
                        def _(sc, gu_tok=gu_tok, s_tok=s_tok, vau_b=vau_b):
                            S.wait(sc, "vau", vau_b)
                            for ci in range(IC):
                                S.wait(sc, "tgu", gu_tok[ci][0])
                                ii = sc.activation(silu_s[:, ci, :],
                                                   A[2 * (ci % 2)][:, :], AF.Silu)
                                s_tok[ci] = S.inc(ii, "sact")

                        v_tok = {}
                        gact_b = S.val("gact")

                        @blk.vector
                        def _(v, e=e, tl=tl, gu_tok=gu_tok, s_tok=s_tok,
                              v_tok=v_tok, gact_b=gact_b):
                            S.wait(v, "gact", gact_b)
                            for ci in range(IC):
                                S.wait(v, "sact", s_tok[ci])
                                S.wait(v, "tgu", gu_tok[ci][1])
                                ii = v.tensor_tensor(tmp_bf[:, ci, :],
                                                     silu_s[:, ci, :],
                                                     A[2 * (ci % 2) + 1][:, :],
                                                     ALU.mult)
                                v_tok[ci] = S.inc(ii, "vau")

                        if tl == 0:
                            if pend is not None:
                                emit_w2_full(pend)
                                pend = None
                            for stl in range(NT):
                                @blk.tensor
                                def _(t, e=e, stl=stl, sc0_tok=sc0_tok):
                                    if stl < 2:
                                        S.use(t, ("Cp", 0))
                                    else:
                                        S.use(t, ("Cb", stl % 2))
                                    if stl == 0:
                                        S.wait(t, "dq", sc0_tok)
                                    ii = t.matmul(
                                        C[stl % 2][:, :], oner2[:, :],
                                        scores0[0:1, stl * TT:(stl + 1) * TT],
                                        start=True, stop=True)
                                    tok = S.inc(ii, "tsc")
                                    if stl == NT - 1:
                                        S.release(("sc0",), "tsc", tok)

                                @blk.vector
                                def _(v, e=e, stl=stl):
                                    S.wait(v, "tsc", S.val("tsc"))
                                    S.use(v, ("scbc",))
                                    vv = S.inc(v.tensor_copy(scbc[:, stl, :],
                                                             C[stl % 2][:, :]),
                                               "vev")
                                    sc_ev_tok[stl] = vv
                                    S.release(("Cb", stl % 2), "vev", vv)
                                    if stl == NT - 1:
                                        S.release(("Cp", 0), "vev", vv)
                        else:
                            emit_w2_full(pend)
                            pend = None

                        @blk.vector
                        def _(v, e=e, tl=tl):
                            if e + 1 < NE:
                                for hcnt in CAST_SLICES[tl % 4] if NT == 4 \
                                        else ([0, 1, 2] if tl % 2 == 0 else [3, 4, 5]):
                                    emit_one_cast(v, e + 1, hcnt)

                        g_tok2 = {}

                        @blk.gpsimd
                        def _(g, e=e, tl=tl, v_tok=v_tok, g_tok2=g_tok2,
                              sc_ev_tok=sc_ev_tok):
                            for ci in range(IC):
                                S.use(g, ("act", ci))
                                if ci == 0:
                                    g.wait_ge(S.sem["vev"], sc_ev_tok[tl])
                                g.wait_ge(S.sem["vau"], v_tok[ci])
                                ii = g.tensor_tensor(act_s[:, ci, :],
                                                     tmp_bf[:, ci, :],
                                                     scbc[:, tl, :], ALU.mult)
                                g_tok2[ci] = S.inc(ii, "gact")

                        pend = (e, tl, t0, t1, g_tok2, gc_done)

                emit_w2_full(pend)
                pend = None
            S.barrier_clear()

        def emit_attn():
            with ExitStack() as mem, nc.Block() as blk:
                psn = lambda name, shp, dt=F32: mem.enter_context(
                    nc.psum_tensor(f"{name}_at", shp, dt))
                st0 = psn("st0", [1, TT])
                st1 = psn("st1", [1, TT])
                st2 = psn("st2", [1, TT])
                st3 = psn("st3", [1, TT])
                bc0 = psn("bc0", [128, TT])
                bc1 = psn("bc1", [128, TT])
                x2ps = [psn(f"x2ps{i}", [128, TT]) for i in range(2)]

                sba = lambda name, shp, dt: mem.enter_context(
                    nc.sbuf_tensor(name, shp, dt))
                winbf = sba("winbf", [128, KC, D], BF16)
                wland = sba("wland_at", [128, LSZ // 2], F32)
                rows = sba("rows_at", [1, 7, TT], F32)
                x1t = sba("x1t", [128, 2, KC, TT], BF16)
                x2t = sba("x2t", [128, KC, TT], F32)
                x3t = sba("x3t", [128, KC, TT], F32)
                PAD = 16
                assert NTAP <= PAD
                x3bf = sba("x3bf", [128, KC, PAD + TT], BF16)
                cst = sba("cst", [128, TT], F32)
                xbf_q = sba("xbf_q", [128, 2, TT], BF16)
                xsq_q = sba("xsq_q", [128, 2, TT], BF16)
                carry = sba("carry", [128, KC], F32)
                diagall = sba("diagall", [128, KC, NTAP, 128], BF16)

                QN = max(1, D // (LSZ // 2 // KC)) if KC else 1
                qw = D // QN
                gc_now = S.val("gc")
                dwq_now = S.val("dwq")

                @blk.sync
                def _(s, gc_now=gc_now):
                    for qt in range(QN):
                        S.wait(s, "gc", gc_now + qt)
                        ap = winT[:, qt * qw:(qt + 1) * qw] \
                            .rearrange("(q p) n -> p q n", p=128)
                        dst = bass.AP(wland, 0,
                                      [[LSZ // 2, 128], [qw, KC], [1, qw]])
                        S.inc(s.dma_start(dst, ap), "dwq", 16)

                @blk.scalar
                def _(sc, dwq_now=dwq_now):
                    for qt in range(QN):
                        sc.wait_ge(S.sem["dwq"], dwq_now + 16 * (qt + 1))
                        srcw = bass.AP(wland, 0,
                                       [[LSZ // 2, 128], [qw, KC], [1, qw]])
                        S.inc(sc.activation(winbf[:, :, qt * qw:(qt + 1) * qw],
                                            srcw, AF.Copy), "gc")
                gc_win = S.val("gc")

                @blk.vector
                def _(v):
                    S.inc(v.memset(carry[:, :], 0.0), "vsc")
                    S.inc(v.memset(x3bf[:, :, 0:PAD], 0.0), "vsc")
                    prevv = None
                    for cc in range(KC):
                        for j in range(NTAP):
                            kcol = k0sb[:, cc, j:j + 1]
                            kbc = bass.AP(k0sb, kcol.offset,
                                          [list(kcol.ap[0]), [0, 128]])
                            prevv = v.tensor_tensor(diagall[:, cc, j, :],
                                                    idn[:, :], kbc, ALU.mult)
                    S.inc(prevv, "vtp")
                dg_tok = S.val("vtp")

                ln1_tok = {0: ln_stats_part1(blk, (lambda cc:
                                                   R[:, cc, 0:TT]),
                                             st2, st3, "stq", 1,
                                             bufs=(xbf_q, xsq_q))}
                apply_tok = {}

                def emit_ln1(tl):
                    ln_stats_part2(blk, rows, st2, st3, "stq", ln1_tok[tl])
                    ln_rows(blk, rows)
                    ln_bcast_tile(blk, rows, bc0, bc1)

                    @blk.vector
                    def _(v, tl=tl):
                        S.use(v, ("x1t", tl % 2))

                    ln_apply_tile(blk, (lambda cc, tl=tl:
                                        R[:, cc, tl * TT:(tl + 1) * TT]),
                                  (lambda cc, tl=tl: x1t[:, tl % 2, cc, :]),
                                  bc0, bc1)
                    apply_tok[tl] = S.val("vln")

                emit_ln1(0)
                for tl in range(NT):
                    t0, t1 = tl * TT, (tl + 1) * TT
                    vln_tile = apply_tok[tl]

                    x2tok = {}
                    for m in range(KC):
                        @blk.tensor
                        def _(t, tl=tl, m=m, vln_tile=vln_tile):
                            if m == 0:
                                S.wait(t, "vln", vln_tile)
                                if tl == 0:
                                    S.wait(t, "gc", gc_win)
                            S.use(t, ("x2ps", m % 2))
                            mm = None
                            for k in range(KC):
                                mm = t.matmul(x2ps[m % 2][:, :],
                                              winbf[:, k, m * 128:(m + 1) * 128],
                                              x1t[:, tl % 2, k, :],
                                              start=(k == 0), stop=(k == KC - 1))
                            tok = S.inc(mm, "tst")
                            if m == KC - 1:
                                S.release(("x1t", tl % 2), "tst", tok)
                            S.rel[("x2m", m)] = [("tst", tok)]

                        @blk.vector
                        def _(v, m=m, x2tok=x2tok):
                            S.use(v, ("x2m", m))
                            S.use(v, ("x2t", m))
                            ii = v.tensor_scalar(x2t[:, m, :], x2ps[m % 2][:, :],
                                                 prm[:, m:m + 1], None, ALU.add)
                            vv = S.inc(ii, "vmi")
                            x2tok[m] = vv
                            S.release(("x2ps", m % 2), "vmi", vv)

                    ln_stats_tile(blk, (lambda cc: x2t[:, cc, :]),
                                  rows, st0, st1,
                                  src_tok=(lambda cc, x2tok=x2tok:
                                           [("vmi", x2tok[cc])]))
                    ln_rows(blk, rows)
                    if tl + 1 < NT:
                        ln1_tok[tl + 1] = ln_stats_part1(
                            blk, (lambda cc, tl=tl:
                                  R[:, cc, (tl + 1) * TT:(tl + 2) * TT]),
                            st2, st3, "stq", 1, bufs=(xbf_q, xsq_q))
                    ln_bcast_tile(blk, rows, bc0, bc1)

                    @blk.vector
                    def _(v):
                        S.use(v, ("x3t",))

                    ln_apply_tile(blk, (lambda cc: x2t[:, cc, :]),
                                  (lambda cc: x3t[:, cc, :]),
                                  bc0, bc1, gb=True)
                    vln_x3 = S.val("vln")
                    for m in range(KC):
                        S.release(("x2t", m), "vln", vln_x3)

                    gmi_b = S.val("gmi")

                    @blk.gpsimd
                    def _(g, tl=tl, vln_x3=vln_x3, gmi_b=gmi_b):
                        g.wait_ge(S.sem["vln"], vln_x3)
                        if tl > 0:
                            g.wait_ge(S.sem["gmi"], gmi_b)
                        for cc in range(KC):
                            S.inc(g.tensor_copy(x3bf[:, cc, PAD:PAD + TT],
                                                x3t[:, cc, :]), "gmi")

                    if tl + 1 < NT:
                        emit_ln1(tl + 1)

                    x3rel = []
                    tap_tok = {}

                    def emit_evac(v, ec, tl=tl, t0=t0, t1=t1):
                        S.use(v, ("ytq", ec, tl))
                        S.use(v, ("Rs2", ec, tl))
                        ii = v.tensor_tensor(R[:, ec, t0:t1], R[:, ec, t0:t1],
                                             x2ps[ec % 2][:, :], ALU.add)
                        rv = S.inc(ii, "vev")
                        S.release(("x2ps", ec % 2), "vev", rv)
                        S.release(("Rs2", ec, tl), "vev", rv)

                    for cc in range(KC):
                        @blk.vector
                        def _(v, cc=cc, tl=tl, t0=t0, t1=t1, vln_x3=vln_x3,
                              x3rel=x3rel):
                            if cc == 0:
                                S.wait(v, "vln", vln_x3)
                            S.use(v, ("cst",))
                            S.use(v, ("carry", cc))
                            ii = v.tensor_tensor_scan(cst[:, :],
                                                      ones_t[:, :],
                                                      x3t[:, cc, :],
                                                      carry[:, cc:cc + 1],
                                                      ALU.mult, ALU.add)
                            vv = S.inc(ii, "vsc")
                            S.wait(v, "vsc", vv)
                            ii = v.tensor_copy(carry[:, cc:cc + 1],
                                               cst[:, TT - 1:TT])
                            cv = S.inc(ii, "vsc")
                            S.release(("carry", cc), "vsc", cv)
                            dcol = prm[:, 3 * KC + cc:3 * KC + cc + 1]
                            S.use(v, ("Rs2", cc, tl))
                            ii = v.scalar_tensor_tensor(R[:, cc, t0:t1],
                                                        cst[:, :], dcol,
                                                        R[:, cc, t0:t1],
                                                        ALU.mult, ALU.add)
                            rv = S.inc(ii, "vsc")
                            S.release(("cst",), "vsc", rv)
                            S.wait(v, "vsc", rv)
                            ii = v.tensor_tensor(R[:, cc, t0:t1], R[:, cc, t0:t1],
                                                 x3t[:, cc, :], ALU.add)
                            rv2 = S.inc(ii, "vsc")
                            S.release(("Rs2", cc, tl), "vsc", rv2)
                            x3rel.append(("vsc", rv2))
                            if cc >= 2:
                                emit_evac(v, cc - 2)

                        @blk.tensor
                        def _(t, cc=cc, tl=tl, t0=t0, t1=t1, gmi_b=gmi_b,
                              tap_tok=tap_tok):
                            if tl == 0 and cc == 0:
                                S.wait(t, "vtp", dg_tok)
                            S.wait(t, "gmi", gmi_b + cc + 1)
                            S.use(t, ("x2ps", cc % 2))
                            mm = None
                            for j in range(NTAP):
                                o0 = max(t0, j)
                                mm = t.matmul(x2ps[cc % 2][:, o0 - t0:TT],
                                              diagall[:, cc, j, :],
                                              x3bf[:, cc, PAD + o0 - t0 - j:
                                                   PAD + TT - j],
                                              start=(j == 0),
                                              stop=(j == NTAP - 1))
                            tok = S.inc(mm, "tw2")
                            tap_tok[cc] = tok
                            S.rel[("ytq", cc, tl)] = [("tw2", tok)]

                    @blk.vector
                    def _(v, tl=tl, t0=t0, t1=t1):
                        emit_evac(v, KC - 2)
                        emit_evac(v, KC - 1)

                    S.rel[("x3t",)] = x3rel + [("gmi", gmi_b + KC)]

                    if tl + 1 < NT:
                        @blk.gpsimd
                        def _(g, tap_tok=tap_tok):
                            g.wait_ge(S.sem["tw2"], tap_tok[KC - 1])
                            S.inc(g.tensor_copy(x3bf[:, :, 0:PAD],
                                                x3bf[:, :, TT:TT + PAD]), "gmi")
            S.barrier_clear()

        def dump(pairs):
            with nc.Block() as blk:
                @blk.sync
                def _(s):
                    for dst, srcb in pairs:
                        S.inc(s.dma_start(dst, srcb), "dq", 16)
                    S.wait(s, "dq")
            S.barrier_clear()

        emit_ffn(0)
        if c.dbg:
            dump([(d_R1, R[:, :, :]), (d_k0, k0sb[:, :, :])])
        emit_attn()
        if c.dbg:
            dump([(d_R2, R[:, :, :])])
        emit_ffn(1)

        with nc.Block() as blk:
            @blk.sync
            def _(s):
                for kc in range(KC):
                    S.inc(s.dma_start(
                        outd.rearrange("(c p) l -> c p l", p=128)[kc],
                        R[:, kc, :]), "oq", 16)
                S.wait(s, "oq")

    return nc


def _col_layout(v, KC):
    return np.ascontiguousarray(np.asarray(v, dtype=np.float32).reshape(KC, 128).T)


def make_core_inputs(c, x_b, W_in, b_in, gamma, beta, A_re, A_im, B_re, B_im,
                     C_re, C_im, D_re, gate_w, w1, w3, w2):
    KC = c.KC
    f32 = lambda a: np.ascontiguousarray(np.asarray(a, dtype=np.float32))
    return {
        "xT": f32(x_b.T),
        "w1": f32(w1), "w3": f32(w3), "w2": f32(w2),
        "gateT": f32(np.asarray(gate_w).T),
        "winT": f32(np.asarray(W_in).T),
        "bin_c": _col_layout(b_in, KC),
        "gam_c": _col_layout(gamma, KC),
        "bet_c": _col_layout(beta, KC),
        "dre_c": _col_layout(D_re, KC),
        "abcol": f32(np.stack([np.asarray(A_re), np.asarray(A_im),
                               np.asarray(B_re), np.asarray(B_im)], axis=1)),
        "ctre": f32(np.asarray(C_re).T),
        "ctim": f32(np.asarray(C_im).T),
        "onec_bf": np.ones((128, 1), dtype=ml_dtypes.bfloat16),
        "oner_bf": np.ones((1, 128), dtype=ml_dtypes.bfloat16),
        "oner_f": np.ones((1, 128), dtype=np.float32),
        "identd": np.eye(128, dtype=ml_dtypes.bfloat16),
    }


_NC_CACHE = {}


def kernel(L0_attn_x, L0_attn_W_in, L0_attn_b_in, L0_attn_gamma, L0_attn_beta,
           L0_attn_A_re, L0_attn_A_im, L0_attn_B_re, L0_attn_B_im,
           L0_attn_C_re, L0_attn_C_im, L0_attn_D_re, L0_attn_D_im,
           L0_ffn_gate_w, L0_ffn_w1, L0_ffn_w3, L0_ffn_w2):
    c = Cfg()
    if "nc" not in _NC_CACHE:
        _NC_CACHE["nc"] = build_nc(c)
    nc = _NC_CACHE["nc"]
    x = np.asarray(L0_attn_x, dtype=np.float32)
    nb = x.shape[0]
    base = make_core_inputs(c, x[0], L0_attn_W_in, L0_attn_b_in, L0_attn_gamma,
                            L0_attn_beta, L0_attn_A_re, L0_attn_A_im,
                            L0_attn_B_re, L0_attn_B_im, L0_attn_C_re,
                            L0_attn_C_im, L0_attn_D_re, L0_ffn_gate_w,
                            L0_ffn_w1, L0_ffn_w3, L0_ffn_w2)
    in_maps = []
    for b in range(nb):
        m = dict(base)
        m["xT"] = np.ascontiguousarray(x[b].T)
        in_maps.append(m)
    res = run_bass_kernel_spmd(nc, in_maps, core_ids=list(range(nb)))
    out = np.stack([np.asarray(res.results[b]["out"], dtype=np.float32).T
                    for b in range(nb)], axis=0)
    return np.ascontiguousarray(out)


# version 32 (final)
# speedup vs baseline: 1.0111x; 1.0015x over previous
import numpy as np
import ml_dtypes
from contextlib import ExitStack

import concourse.bass as bass
import concourse.mybir as mybir
from concourse.bass_utils import run_bass_kernel_spmd

F32 = mybir.dt.float32
BF16 = mybir.dt.bfloat16
AF = mybir.ActivationFunctionType
ALU = mybir.AluOpType
AX = mybir.AxisListType

BIG = 1.0e30


class Cfg:
    def __init__(self, D=1024, DI=512, NE=16, L=2048, TT=512, NTAP=12, NST=64,
                 TOPK=4, EPS=1e-5):
        self.D, self.DI, self.NE, self.L, self.TT = D, DI, NE, L, TT
        self.NTAP, self.NST, self.TOPK, self.EPS = NTAP, NST, TOPK, EPS
        self.dbg = False
        self.KC = D // 128
        self.IC = DI // 128
        self.NT = L // TT
        self.LC = L // 128
        assert D % 128 == 0 and DI % 128 == 0 and L % TT == 0 and TT % 128 == 0
        assert self.KC % 2 == 0 and self.LC % 4 == 0


class B:
    def __init__(self, nc, cfg, ctx):
        self.nc = nc
        self.c = cfg
        names = ["dq", "dwq", "oq", "gc",
                 "tst", "tgu", "tw2", "tsc",
                 "vln", "vrow", "vau", "vev", "vmi", "vsc", "vtp", "vst",
                 "srow", "sst", "sact",
                 "gmi", "gact"]
        self.sem = {n: ctx.enter_context(nc.semaphore(n)) for n in names}
        self.cnt = {n: 0 for n in names}
        self.rel = {}

    def inc(self, instr, name, k=1):
        instr.then_inc(self.sem[name], k)
        self.cnt[name] += k
        return self.cnt[name]

    def val(self, name):
        return self.cnt[name]

    def wait(self, eng, name, v=None):
        v = self.cnt[name] if v is None else v
        if v > 0:
            eng.wait_ge(self.sem[name], v)

    def use(self, eng, res):
        for (n, v) in self.rel.get(res, []):
            if v > 0:
                eng.wait_ge(self.sem[n], v)

    def release(self, res, name, v):
        self.rel[res] = [(name, v)]

    def barrier_clear(self):
        self.rel = {}


def build_nc(c: Cfg):
    nc = bass.Bass("TRN2", target_bir_lowering=False)
    D, DI, NE, L, TT = c.D, c.DI, c.NE, c.L, c.TT
    KC, IC, NT, LC = c.KC, c.IC, c.NT, c.LC
    NTAP, NST = c.NTAP, c.NST
    HALF = D // 2

    xT = nc.dram_tensor("xT", [D, L], F32, kind="ExternalInput").ap()
    w1d = nc.dram_tensor("w1", [NE, D, DI], F32, kind="ExternalInput").ap()
    w3d = nc.dram_tensor("w3", [NE, D, DI], F32, kind="ExternalInput").ap()
    w2d = nc.dram_tensor("w2", [NE, DI, D], F32, kind="ExternalInput").ap()
    gateT = nc.dram_tensor("gateT", [D, NE], F32, kind="ExternalInput").ap()
    winT = nc.dram_tensor("winT", [D, D], F32, kind="ExternalInput").ap()
    bin_c = nc.dram_tensor("bin_c", [128, KC], F32, kind="ExternalInput").ap()
    gam_c = nc.dram_tensor("gam_c", [128, KC], F32, kind="ExternalInput").ap()
    bet_c = nc.dram_tensor("bet_c", [128, KC], F32, kind="ExternalInput").ap()
    dre_c = nc.dram_tensor("dre_c", [128, KC], F32, kind="ExternalInput").ap()
    abcold = nc.dram_tensor("abcol", [NST, 4], F32, kind="ExternalInput").ap()
    ctred = nc.dram_tensor("ctre", [NST, D], F32, kind="ExternalInput").ap()
    ctimd = nc.dram_tensor("ctim", [NST, D], F32, kind="ExternalInput").ap()
    onecd = nc.dram_tensor("onec_bf", [128, 1], BF16, kind="ExternalInput").ap()
    onerd = nc.dram_tensor("oner_bf", [1, 128], BF16, kind="ExternalInput").ap()
    onerfd = nc.dram_tensor("oner_f", [1, 128], F32, kind="ExternalInput").ap()
    identd = nc.dram_tensor("identd", [128, 128], BF16, kind="ExternalInput").ap()
    outd = nc.dram_tensor("out", [D, L], F32, kind="ExternalOutput").ap()
    if c.dbg:
        d_xhat = nc.dram_tensor("d_xhat", [128, KC, L], BF16, kind="ExternalOutput").ap()
        d_sc = nc.dram_tensor("d_sc", [NE, L], BF16, kind="ExternalOutput").ap()
        d_R1 = nc.dram_tensor("d_R1", [128, KC, L], F32, kind="ExternalOutput").ap()
        d_x3 = nc.dram_tensor("d_x3", [128, KC, L], BF16, kind="ExternalOutput").ap()
        d_R2 = nc.dram_tensor("d_R2", [128, KC, L], F32, kind="ExternalOutput").ap()
        d_k0 = nc.dram_tensor("d_k0", [128, KC, NTAP], F32, kind="ExternalOutput").ap()
        d_x1 = nc.dram_tensor("d_x1", [128, KC, TT], BF16, kind="ExternalOutput").ap()
        d_x2 = nc.dram_tensor("d_x2", [128, KC, TT], F32, kind="ExternalOutput").ap()

    top = ExitStack()
    with top:
        S = B(nc, c, top)
        sb = lambda name, shp, dt: top.enter_context(nc.sbuf_tensor(name, shp, dt))

        R = sb("R", [128, KC, L], F32)
        WSZ = 2 * KC * DI + IC * D
        OW1, OW3, OW2 = 0, KC * DI, 2 * KC * DI
        LSZ = max(KC * DI, IC * D, KC * HALF)
        gatebf = sb("gatebf", [128, KC, NE], BF16)
        prm = sb("prm", [128, 4 * KC], F32)
        acol = sb("acol", [NST, 4], F32)
        k0sb = sb("k0sb", [128, KC, NTAP], F32)
        onec = sb("onec", [128, 1], BF16)
        oner2 = sb("oner2", [1, 128], BF16)
        onerf = sb("onerf", [1, 128], F32)
        idn = sb("idn", [128, 128], BF16)
        ones_t = sb("ones_t", [128, TT], F32)
        xbf_s = sb("xbf_s", [128, 2, TT], BF16)
        xsq_s = sb("xsq_s", [128, 2, TT], BF16)
        lnt = sb("lnt", [128, 2, TT], F32)
        start_mem = ExitStack()
        ssb = lambda name, shp, dt: start_mem.enter_context(
            nc.sbuf_tensor(name, shp, dt))
        gland = ssb("gland", [128, KC, NE], F32)
        ctbf = ssb("ctbf", [NST, 2 * D], BF16)
        ctl = ssb("ctl", [NST, D], F32)
        ajb = ssb("ajb", [NST, 2 * NTAP], F32)
        ajbbf = ssb("ajbbf", [NST, 2 * NTAP], BF16)
        scr2 = ssb("scr2", [NST, 2], F32)

        dq_small = [0]
        dq_full = [0]
        with nc.Block() as blk:
            @blk.sync
            def _(s):
                S.inc(s.dma_start(gland[:, :, :],
                                  gateT.rearrange("(c p) e -> p c e", p=128)), "dq", 16)
                S.inc(s.dma_start(prm[:, 0 * KC:1 * KC], bin_c), "dq", 16)
                S.inc(s.dma_start(prm[:, 1 * KC:2 * KC], gam_c), "dq", 16)
                S.inc(s.dma_start(prm[:, 2 * KC:3 * KC], bet_c), "dq", 16)
                S.inc(s.dma_start(prm[:, 3 * KC:4 * KC], dre_c), "dq", 16)
                S.inc(s.dma_start(acol[:, :], abcold), "dq", 16)
                S.inc(s.dma_start(onec[:, :], onecd), "dq", 16)
                S.inc(s.dma_start(oner2[:, :], onerd), "dq", 16)
                S.inc(s.dma_start(onerf[:, :], onerfd), "dq", 16)
                S.inc(s.dma_start(idn[:, :], identd), "dq", 16)
                S.inc(s.dma_start(ctl[:, :], ctred), "dq", 16)
                dq_small[0] = S.val("dq")
                for kc in range(KC):
                    S.inc(s.dma_start(R[:, kc, :],
                                      xT.rearrange("(c p) l -> c p l", p=128)[kc]),
                          "dq", 16)
                dq_full[0] = S.val("dq")

            @blk.scalar
            def _(sc):
                S.wait(sc, "dq", dq_small[0])
                S.inc(sc.activation(gatebf[:, :, :], gland[:, :, :], AF.Copy), "gc")
                S.inc(sc.activation(ctbf[:, 0:D], ctl[:, :], AF.Copy), "gc")

            @blk.vector
            def _(v):
                v.memset(ones_t[:, :], 1.0)

        with nc.Block() as blk:
            @blk.sync
            def _(s):
                S.inc(s.dma_start(ctl[:, :], ctimd), "dq", 16)

            @blk.scalar
            def _(sc):
                S.wait(sc, "dq")
                S.inc(sc.activation(ctbf[:, D:2 * D], ctl[:, :], AF.Copy), "gc")

        with nc.Block() as blk:
            @blk.vector
            def _(v):
                are, aim = acol[:, 0:1], acol[:, 1:2]
                v.tensor_copy(ajb[:, 0:1], acol[:, 2:3])
                vv = S.inc(v.tensor_copy(ajb[:, NTAP:NTAP + 1], acol[:, 3:4]), "vmi")
                for j in range(1, NTAP):
                    S.wait(v, "vmi", vv)
                    pr, pi = ajb[:, j - 1:j], ajb[:, NTAP + j - 1:NTAP + j]
                    cr, ci_ = ajb[:, j:j + 1], ajb[:, NTAP + j:NTAP + j + 1]
                    v.tensor_tensor(scr2[:, 0:1], pi, aim, ALU.mult)
                    vv = S.inc(v.tensor_tensor(scr2[:, 1:2], pr, aim, ALU.mult), "vmi")
                    S.wait(v, "vmi", vv)
                    v.scalar_tensor_tensor(cr, pr, are, scr2[:, 0:1], ALU.mult,
                                           ALU.subtract)
                    vv = S.inc(v.scalar_tensor_tensor(ci_, pi, are, scr2[:, 1:2],
                                                      ALU.mult, ALU.add), "vmi")
                S.wait(v, "vmi", vv)
                v.tensor_copy(ajbbf[:, 0:NTAP], ajb[:, 0:NTAP])
                S.inc(v.tensor_scalar(ajbbf[:, NTAP:2 * NTAP], ajb[:, NTAP:2 * NTAP],
                                      -1.0, None, ALU.mult), "vmi")

        with ExitStack() as ps0:
            k0ps = ps0.enter_context(nc.psum_tensor("k0ps", [128, 512], F32))
            with nc.Block() as blk:
                for kc in range(KC):
                    @blk.tensor
                    def _(t, kc=kc):
                        if kc == 0:
                            S.wait(t, "gc")
                            S.wait(t, "vmi")
                        S.use(t, "k0ps")
                        t.matmul(k0ps[:, 0:NTAP], ctbf[:, kc * 128:(kc + 1) * 128],
                                 ajbbf[:, 0:NTAP], start=True, stop=False)
                        S.inc(t.matmul(k0ps[:, 0:NTAP],
                                       ctbf[:, D + kc * 128:D + (kc + 1) * 128],
                                       ajbbf[:, NTAP:2 * NTAP], start=False,
                                       stop=True), "tst")

                    @blk.vector
                    def _(v, kc=kc):
                        S.wait(v, "tst")
                        vv = S.inc(v.tensor_copy(k0sb[:, kc, :], k0ps[:, 0:NTAP]), "vmi")
                        S.release("k0ps", "vmi", vv)
        with nc.Block() as blk:
            @blk.vector
            def _(v):
                S.wait(v, "vmi")
        start_mem.close()
        S.barrier_clear()

        def ln_stats_part1(blk, src_chunk, st0, st1, skey, bslot,
                           src_tok=None, bufs=None):
            xb, xq = bufs if bufs is not None else (xbf_s, xsq_s)
            sst_base = S.val("vst")
            tst_base = S.val("tst")

            @blk.vector
            def _(vq):
                for cc in range(KC):
                    bi = cc % 2 + 2 * bslot * 0
                    if src_tok is not None:
                        for (n_, v_) in src_tok(cc):
                            vq.wait_ge(S.sem[n_], v_)
                    if cc >= 2:
                        S.wait(vq, "tst", tst_base + cc - 1)
                    else:
                        S.use(vq, ("stat", bslot, bi))
                    vq.tensor_copy(xb[:, bi, :], src_chunk(cc))
                    S.inc(vq.tensor_tensor(xq[:, bi, :], src_chunk(cc),
                                           src_chunk(cc), ALU.mult), "vst")

            @blk.tensor
            def _(t):
                S.use(t, (skey, 0))
                S.use(t, (skey, 1))
                for cc in range(KC):
                    bi = cc % 2
                    S.wait(t, "vst", sst_base + cc + 1)
                    t.matmul(st0[:, :], onec[:, :], xb[:, bi, :],
                             start=(cc == 0), stop=(cc == KC - 1))
                    v = S.inc(t.matmul(st1[:, :], onec[:, :], xq[:, bi, :],
                                       start=(cc == 0), stop=(cc == KC - 1)), "tst")
                    S.release(("stat", bslot, bi), "tst", v)
            return S.val("tst")

        def ln_stats_part2(blk, rows, st0, st1, skey, mm_tok):
            @blk.vector
            def _(v):
                S.wait(v, "tst", mm_tok)
                S.use(v, ("rows",))
                v.tensor_copy(rows[0:1, 0, :], st0[0:1, :])
                vv = S.inc(v.tensor_copy(rows[0:1, 1, :], st1[0:1, :]), "vrow")
                S.release((skey, 0), "vrow", vv)
                S.release((skey, 1), "vrow", vv)

        def ln_stats_tile(blk, src_chunk, rows, st0, st1, src_tok=None,
                          skey="stp", bslot=0):
            tok = ln_stats_part1(blk, src_chunk, st0, st1, skey, bslot, src_tok)
            ln_stats_part2(blk, rows, st0, st1, skey, tok)

        def ln_rows(blk, rows):
            def rr(i):
                return rows[0:1, i, :]

            @blk.vector
            def _(v):
                S.wait(v, "vrow")
                v.tensor_scalar(rr(2), rr(0), 1.0 / D, None, ALU.mult)
                vv = S.inc(v.tensor_scalar(rr(3), rr(1), 1.0 / D, None, ALU.mult),
                           "vrow")
                S.wait(v, "vrow", vv)
                vv = S.inc(v.tensor_tensor(rr(4), rr(2), rr(2), ALU.mult), "vrow")
                S.wait(v, "vrow", vv)
                vv = S.inc(v.scalar_tensor_tensor(rr(3), rr(4), -1.0, rr(3),
                                                  ALU.mult, ALU.add), "vrow")
                S.wait(v, "vrow", vv)
                vv = S.inc(v.tensor_scalar(rr(4), rr(3), float(c.EPS), None,
                                           ALU.add), "vrow")

            @blk.scalar
            def _(sc):
                S.wait(sc, "vrow")
                vv = S.inc(sc.activation(rr(3), rr(4), AF.Ln), "srow")
                S.wait(sc, "srow", vv)
                S.inc(sc.activation(rr(5), rr(3), AF.Exp, scale=-0.5), "srow")

            @blk.vector
            def _(v):
                S.wait(v, "srow")
                S.inc(v.tensor_tensor(rr(6), rr(2), rr(5), ALU.mult), "vrow")

        def ln_bcast_tile(blk, rows, bc0, bc1):
            @blk.tensor
            def _(t):
                S.wait(t, "vrow")
                S.wait(t, "srow")
                S.use(t, "bc0")
                S.use(t, "bc1")
                t.matmul(bc0[:, :], onerf[:, :], rows[0:1, 5, :],
                         start=True, stop=True)
                tok = S.inc(t.matmul(bc1[:, :], onerf[:, :], rows[0:1, 6, :],
                                     start=True, stop=True), "tst")
                S.release(("rows",), "tst", tok)

        def ln_apply_tile(blk, src_chunk, dst_chunk, bc0, bc1, gb=False):
            @blk.vector
            def _(v):
                S.wait(v, "tst")
                toks = {}
                last = 0

                def emit_b(pc):
                    nonlocal last
                    S.wait(v, "vln", toks[pc])
                    if not gb:
                        ib = v.tensor_tensor(dst_chunk(pc), lnt[:, pc % 2, :],
                                             bc1[:, :], ALU.subtract)
                        last = S.inc(ib, "vln")
                    else:
                        ib = v.tensor_tensor(lnt[:, pc % 2, :], lnt[:, pc % 2, :],
                                             bc1[:, :], ALU.subtract)
                        vv2 = S.inc(ib, "vln")
                        S.wait(v, "vln", vv2)
                        gcol = prm[:, KC + pc:KC + pc + 1]
                        bcol = prm[:, 2 * KC + pc:2 * KC + pc + 1]
                        bb = bass.AP(bcol.tensor, bcol.offset,
                                     [list(bcol.ap[0]), [0, TT]])
                        ic2 = v.scalar_tensor_tensor(dst_chunk(pc),
                                                     lnt[:, pc % 2, :], gcol, bb,
                                                     ALU.mult, ALU.add)
                        last = S.inc(ic2, "vln")
                    S.release(("lnt", pc % 2), "vln", last)

                for cc in range(KC):
                    S.use(v, ("lnt", cc % 2))
                    ia = v.tensor_tensor(lnt[:, cc % 2, :], src_chunk(cc), bc0[:, :],
                                         ALU.mult)
                    toks[cc] = S.inc(ia, "vln")
                    if cc >= 1:
                        emit_b(cc - 1)
                emit_b(KC - 1)
                S.release("bc0", "vln", last)
                S.release("bc1", "vln", last)

        def emit_ffn(call):
            with ExitStack() as mem, nc.Block() as blk:
                if call == 0:
                    @blk.vector
                    def _(v):
                        S.wait(v, "dq", dq_full[0])
                psn = lambda name, shp, dt=F32: mem.enter_context(
                    nc.psum_tensor(f"{name}_f{call}", shp, dt))
                sbf = lambda name, shp, dt: mem.enter_context(
                    nc.sbuf_tensor(f"{name}_f{call}", shp, dt))
                wbf = [sbf(f"wbf{i}", [128, WSZ], BF16) for i in range(2)]
                wland = sbf("wland", [128, LSZ // 2], F32)
                xhat = sbf("xhat", [128, KC, L], BF16)
                scbc = sbf("scbc", [128, NT, TT], BF16)
                act_s = sbf("act_s", [128, IC, TT], BF16)
                rows = sbf("rows", [1, 7, TT], F32)
                silu_s = sbf("silu_s", [128, IC, TT], BF16)
                tmp_bf = sbf("tmp_bf", [128, IC, TT], BF16)
                scoresF = sbf("scoresF", [NE, L], BF16)
                scores0 = sbf("scores0", [1, L], BF16)
                logf_bf = sbf("logf_bf", [NE, TT], BF16)
                ltok = sbf("ltok", [128, LC, NE], F32)
                wrk = sbf("wrk", [128, LC, NE], F32)
                msk = sbf("msk", [128, LC, NE], F32)
                mcol = sbf("mcol", [128, LC, 1], F32)
                rden = sbf("rden", [128, LC, 1], F32)
                stok_bf = sbf("stok_bf", [128, LC, NE], BF16)

                def w_t(e, part, k, m):
                    off = [OW1, OW3, OW2][part] + k * ([DI, DI, D][part]) + m * 128
                    return wbf[e % 2][:, off:off + 128]
                with ExitStack() as pro:
                    ppsn = lambda name, shp, dt=F32: pro.enter_context(
                        nc.psum_tensor(f"{name}_p{call}", shp, dt))
                    st0 = ppsn("st0", [1, TT])
                    st1 = ppsn("st1", [1, TT])
                    bc0 = ppsn("bc0", [128, TT])
                    bc1 = ppsn("bc1", [128, TT])
                    lgf = ppsn("lgf", [NE, TT])
                    rtp = ppsn("rtp", [128, 4 * 256], BF16)
                    stp0 = ppsn("stp0", [NE, 1024], BF16)
                    stp1 = ppsn("stp1", [NE, 1024], BF16)
                    stp = [stp0, stp1]

                    for tl in range(NT):
                        ln_stats_tile(blk, (lambda cc, tl=tl:
                                            R[:, cc, tl * TT:(tl + 1) * TT]),
                                      rows, st0, st1)
                        ln_rows(blk, rows)
                        ln_bcast_tile(blk, rows, bc0, bc1)
                        ln_apply_tile(blk, (lambda cc, tl=tl:
                                            R[:, cc, tl * TT:(tl + 1) * TT]),
                                      (lambda cc, tl=tl:
                                       xhat[:, cc, tl * TT:(tl + 1) * TT]),
                                      bc0, bc1)

                    TC = TT // 128
                    for tl in range(NT):
                        t0, t1 = tl * TT, (tl + 1) * TT

                        @blk.tensor
                        def _(t, t0=t0, t1=t1):
                            S.wait(t, "vln")
                            S.use(t, "lgf")
                            for k in range(KC):
                                i = t.matmul(lgf[:, :], gatebf[:, k, :],
                                             xhat[:, k, t0:t1],
                                             start=(k == 0), stop=(k == KC - 1))
                            S.inc(i, "tst")

                        @blk.vector
                        def _(v):
                            S.wait(v, "tst")
                            S.use(v, "logf")
                            vv = S.inc(v.tensor_copy(logf_bf[:, :], lgf[:, :]),
                                       "vmi")
                            S.release("lgf", "vmi", vv)

                        @blk.tensor
                        def _(t, tl=tl):
                            S.wait(t, "vmi")
                            S.use(t, "rtp")
                            for i in range(TC):
                                ii = t.transpose(rtp[:, i * 256:i * 256 + NE],
                                                 logf_bf[:, i * 128:(i + 1) * 128],
                                                 idn[0:NE, 0:NE])
                            tok = S.inc(ii, "tst")
                            S.release("logf", "tst", tok)

                        @blk.vector
                        def _(v, tl=tl):
                            S.wait(v, "tst")
                            dst = bass.AP(ltok, tl * TC * NE,
                                          [[LC * NE, 128], [NE, TC], [1, NE]])
                            srcp = bass.AP(rtp, 0,
                                           [[4 * 256, 128], [256, TC], [1, NE]])
                            vv = S.inc(v.tensor_copy(dst, srcp), "vmi")
                            S.release("rtp", "vmi", vv)

                    @blk.vector
                    def _(v):
                        S.wait(v, "vmi")
                        full = ltok[:, :, :]
                        mb = bass.AP(mcol, 0, [[LC, 128], [1, LC], [0, NE]])
                        vv = S.inc(v.reduce_max(mcol[:, :, :], full, axis=AX.X),
                                   "vmi")
                        S.wait(v, "vmi", vv)
                        S.inc(v.tensor_tensor(wrk[:, :, :], full, mb, ALU.subtract),
                              "vmi")

                    @blk.scalar
                    def _(sc):
                        S.wait(sc, "vmi")
                        S.inc(sc.activation(ltok[:, :, :], wrk[:, :, :], AF.Exp),
                              "sst")

                    @blk.vector
                    def _(v):
                        S.wait(v, "sst")
                        full = ltok[:, :, :]
                        mb = bass.AP(mcol, 0, [[LC, 128], [1, LC], [0, NE]])
                        vv = S.inc(v.tensor_copy(wrk[:, :, :], full), "vmi")
                        for it in range(c.TOPK - 1):
                            S.wait(v, "vmi", vv)
                            vv = S.inc(v.reduce_max(mcol[:, :, :], wrk[:, :, :],
                                                    axis=AX.X), "vmi")
                            S.wait(v, "vmi", vv)
                            vv = S.inc(v.tensor_tensor(msk[:, :, :], wrk[:, :, :],
                                                       mb, ALU.is_ge), "vmi")
                            S.wait(v, "vmi", vv)
                            vv = S.inc(v.scalar_tensor_tensor(wrk[:, :, :],
                                                              msk[:, :, :], -BIG,
                                                              wrk[:, :, :],
                                                              ALU.mult, ALU.add),
                                       "vmi")
                        S.wait(v, "vmi", vv)
                        vv = S.inc(v.reduce_max(mcol[:, :, :], wrk[:, :, :],
                                                axis=AX.X), "vmi")
                        S.wait(v, "vmi", vv)
                        vv = S.inc(v.tensor_tensor(msk[:, :, :], full, mb,
                                                   ALU.is_ge), "vmi")
                        S.wait(v, "vmi", vv)
                        vv = S.inc(v.tensor_tensor(wrk[:, :, :], full, msk[:, :, :],
                                                   ALU.mult), "vmi")
                        S.wait(v, "vmi", vv)
                        vv = S.inc(v.reduce_sum(mcol[:, :, :], wrk[:, :, :],
                                                axis=AX.X), "vmi")
                        S.wait(v, "vmi", vv)
                        vv = S.inc(v.reciprocal(rden[:, :, :], mcol[:, :, :]), "vmi")
                        S.wait(v, "vmi", vv)
                        rb = bass.AP(rden, 0, [[LC, 128], [1, LC], [0, NE]])
                        vv = S.inc(v.tensor_tensor(stok_bf[:, :, :], wrk[:, :, :],
                                                   rb, ALU.mult), "vmi")

                    for lc in range(LC):
                        @blk.tensor
                        def _(t, lc=lc):
                            if lc == 0:
                                S.wait(t, "vmi")
                            S.use(t, ("stp", lc % 2))
                            S.inc(t.transpose(
                                stp[lc % 2][:, 0:128],
                                stok_bf[:, lc, :], idn[:, :]), "tst")

                        @blk.vector
                        def _(v, lc=lc):
                            S.wait(v, "tst")
                            vv = S.inc(v.tensor_copy(
                                scoresF[:, lc * 128:(lc + 1) * 128],
                                stp[lc % 2][:, 0:128]), "vtp")
                            S.release(("stp", lc % 2), "vtp", vv)


                A = [psn(f"A{i}", [128, TT]) for i in range(4)]
                C = [psn(f"C{i}", [128, TT]) for i in range(4)]

                @blk.tensor
                def _(t):
                    S.wait(t, "tst", S.val("tst"))
                    S.wait(t, "vln", S.val("vln"))
                    S.wait(t, "vtp", S.val("vtp"))
                    S.wait(t, "vmi", S.val("vmi"))
                    S.wait(t, "vrow", S.val("vrow"))

                def emit_wdma(e):
                    gc_now = S.val("gc")

                    @blk.sync
                    def _(s, e=e, gc_now=gc_now):
                        hcnt = 0
                        for p, (srcd, wid) in enumerate([(w1d, KC * DI),
                                                         (w3d, KC * DI),
                                                         (w2d, IC * D)]):
                            nch = srcd.shape[1] // 128
                            hch = nch // 2
                            wper = wid // nch
                            for hh in range(2):
                                S.wait(s, "gc", gc_now + hcnt)
                                S.use(s, ("wbuf", e % 2))
                                ap = srcd[e][hh * hch * 128:(hh + 1) * hch * 128, :] \
                                    .rearrange("(q p) n -> p q n", p=128)
                                dst = bass.AP(wland, 0,
                                              [[LSZ // 2, 128], [wper, hch],
                                               [1, wper]])
                                S.inc(s.dma_start(dst, ap), "dwq", 16)
                                hcnt += 1

                WPARTS = [(OW1, KC * DI), (OW3, KC * DI), (OW2, IC * D)]
                CAST_SLICES = {0: [0, 1], 1: [2, 3], 2: [4], 3: [5]}

                def emit_one_cast(v, e, hcnt):
                    p, hh = hcnt // 2, hcnt % 2
                    off, wid = WPARTS[p]
                    dwq_need = DWQ_BASE[e] + 16 * (hcnt + 1)
                    v.wait_ge(S.sem["dwq"], dwq_need)
                    S.inc(v.tensor_copy(
                        wbf[e % 2][:, off + hh * (wid // 2):
                                   off + (hh + 1) * (wid // 2)],
                        wland[:, 0:wid // 2]), "gc")

                def emit_wcast_slice(e, tl, gcb):
                    pass

                DWQ_BASE = {}

                def emit_wdma_tracked(e):
                    DWQ_BASE[e] = S.val("dwq")
                    emit_wdma(e)

                def emit_wcasts_now(e):
                    @blk.vector
                    def _(v, e=e):
                        for hcnt in range(6):
                            emit_one_cast(v, e, hcnt)

                def emit_w2_full(ctx):
                    (e, tl, t0, t1, g_tok2, gc_done) = ctx
                    nq = KC // 2

                    def w2_mms(t, q, ks):
                        pair = q % 2
                        ms = (2 * q, 2 * q + 1)
                        ks = list(ks)
                        S.wait(t, "gact", g_tok2[ks[-1]])
                        mm = None
                        for k in ks:
                            for m in ms:
                                mm = t.matmul(C[2 * pair + (m % 2)][:, :],
                                              w_t(e, 2, k, m),
                                              act_s[:, k, :],
                                              start=(k == 0),
                                              stop=(k == IC - 1))
                        return mm

                    def w2_fin(q, tok):
                        if q == nq - 1:
                            S.release(("wbuf", e % 2), "tw2", tok)
                            for ci in range(IC):
                                S.release(("act", ci), "tw2", tok)
                            if tl == NT - 1:
                                S.release(("scbc",), "tw2", tok)
                        S.rel[("q", q)] = [("tw2", tok)]

                    def emit_rupd(q):
                        pair = q % 2

                        @blk.vector
                        def _(v, q=q, pair=pair):
                            S.use(v, ("q", q))
                            vv = 0
                            for m in (2 * q, 2 * q + 1):
                                S.use(v, ("Rs", m, tl))
                                ii = v.scalar_tensor_tensor(
                                    R[:, m, t0:t1], C[2 * pair + (m % 2)][:, :],
                                    0.5, R[:, m, t0:t1], ALU.mult, ALU.add)
                                vv = S.inc(ii, "vev")
                                S.release(("Rs", m, tl), "vev", vv)
                            S.release(("Cp", pair), "vev", vv)

                    if nq >= 2:
                        @blk.tensor
                        def _(t):
                            if tl == 0:
                                S.wait(t, "gc", gc_done)
                            S.use(t, ("Cp", 0))
                            w2_mms(t, 0, range(IC - 1))
                            S.use(t, ("Cp", 1))
                            w2_mms(t, 1, range(IC - 1))
                            tok0 = S.inc(w2_mms(t, 0, [IC - 1]), "tw2")
                            w2_fin(0, tok0)
                            tok1 = S.inc(w2_mms(t, 1, [IC - 1]), "tw2")
                            w2_fin(1, tok1)
                        emit_rupd(0)
                        emit_rupd(1)
                        for q in range(2, nq):
                            @blk.tensor
                            def _(t, q=q):
                                S.use(t, ("Cp", q % 2))
                                tok = S.inc(w2_mms(t, q, range(IC)), "tw2")
                                w2_fin(q, tok)
                            emit_rupd(q)
                    else:
                        @blk.tensor
                        def _(t):
                            if tl == 0:
                                S.wait(t, "gc", gc_done)
                            S.use(t, ("Cp", 0))
                            tok = S.inc(w2_mms(t, 0, range(IC)), "tw2")
                            w2_fin(0, tok)
                        emit_rupd(0)

                gc_base = {0: S.val("gc")}
                emit_wdma_tracked(0)
                emit_wcasts_now(0)
                gcd = {0: S.val("gc")}
                pend = None
                for e in range(NE):
                    @blk.sync
                    def _(s, e=e):
                        if e == 0:
                            S.wait(s, "vtp", S.val("vtp"))
                        S.use(s, ("sc0",))
                        S.inc(s.dma_start(scores0[0:1, :], scoresF[e:e + 1, :]),
                              "dq", 16)
                    sc0_tok = S.val("dq")

                    if e + 1 < NE:
                        gc_base[e + 1] = S.val("gc")
                        emit_wdma_tracked(e + 1)
                        gcd[e + 1] = gc_base[e + 1] + 6
                    gc_done = gcd[e]
                    sc_ev_tok = {}

                    for tl in range(NT):
                        t0, t1 = tl * TT, (tl + 1) * TT
                        gu_tok = {}
                        vau_b = S.val("vau")

                        @blk.tensor
                        def _(t, e=e, tl=tl, t0=t0, t1=t1, gu_tok=gu_tok,
                              vau_b=vau_b, gc_done=gc_done):
                            if tl == 0:
                                S.wait(t, "gc", gc_done - 2)
                                if e == 0:
                                    S.wait(t, "vln", S.val("vln"))
                                    S.wait(t, "vev", S.val("vev"))
                            S.wait(t, "vau", vau_b)
                            for ci in range(IC):
                                pair = ci % 2
                                if ci >= 2:
                                    S.wait(t, "vau", vau_b + ci - 1)
                                for k in range(KC):
                                    mm = t.matmul(A[2 * pair][:, :], w_t(e, 0, k, ci),
                                                  xhat[:, k, t0:t1],
                                                  start=(k == 0), stop=(k == KC - 1))
                                g_tok = S.inc(mm, "tgu")
                                for k in range(KC):
                                    mm = t.matmul(A[2 * pair + 1][:, :],
                                                  w_t(e, 1, k, ci),
                                                  xhat[:, k, t0:t1],
                                                  start=(k == 0), stop=(k == KC - 1))
                                u_tok = S.inc(mm, "tgu")
                                gu_tok[ci] = (g_tok, u_tok)

                        s_tok = {}

                        @blk.scalar
                        def _(sc, gu_tok=gu_tok, s_tok=s_tok, vau_b=vau_b):
                            S.wait(sc, "vau", vau_b)
                            for ci in range(IC):
                                S.wait(sc, "tgu", gu_tok[ci][0])
                                ii = sc.activation(silu_s[:, ci, :],
                                                   A[2 * (ci % 2)][:, :], AF.Silu)
                                s_tok[ci] = S.inc(ii, "sact")

                        v_tok = {}
                        gact_b = S.val("gact")

                        @blk.vector
                        def _(v, e=e, tl=tl, gu_tok=gu_tok, s_tok=s_tok,
                              v_tok=v_tok, gact_b=gact_b):
                            S.wait(v, "gact", gact_b)
                            for ci in range(IC):
                                S.wait(v, "sact", s_tok[ci])
                                S.wait(v, "tgu", gu_tok[ci][1])
                                ii = v.tensor_tensor(tmp_bf[:, ci, :],
                                                     silu_s[:, ci, :],
                                                     A[2 * (ci % 2) + 1][:, :],
                                                     ALU.mult)
                                v_tok[ci] = S.inc(ii, "vau")

                        if tl == 0:
                            if pend is not None:
                                emit_w2_full(pend)
                                pend = None
                            for stl in range(NT):
                                @blk.tensor
                                def _(t, e=e, stl=stl, sc0_tok=sc0_tok):
                                    if stl < 2:
                                        S.use(t, ("Cp", 0))
                                    else:
                                        S.use(t, ("Cb", stl % 2))
                                    if stl == 0:
                                        S.wait(t, "dq", sc0_tok)
                                    ii = t.matmul(
                                        C[stl % 2][:, :], oner2[:, :],
                                        scores0[0:1, stl * TT:(stl + 1) * TT],
                                        start=True, stop=True)
                                    tok = S.inc(ii, "tsc")
                                    if stl == NT - 1:
                                        S.release(("sc0",), "tsc", tok)

                                @blk.vector
                                def _(v, e=e, stl=stl):
                                    S.wait(v, "tsc", S.val("tsc"))
                                    S.use(v, ("scbc",))
                                    vv = S.inc(v.tensor_copy(scbc[:, stl, :],
                                                             C[stl % 2][:, :]),
                                               "vev")
                                    sc_ev_tok[stl] = vv
                                    S.release(("Cb", stl % 2), "vev", vv)
                                    if stl == NT - 1:
                                        S.release(("Cp", 0), "vev", vv)
                        else:
                            emit_w2_full(pend)
                            pend = None

                        @blk.vector
                        def _(v, e=e, tl=tl):
                            if e + 1 < NE:
                                for hcnt in CAST_SLICES[tl % 4] if NT == 4 \
                                        else ([0, 1, 2] if tl % 2 == 0 else [3, 4, 5]):
                                    emit_one_cast(v, e + 1, hcnt)

                        g_tok2 = {}

                        @blk.gpsimd
                        def _(g, e=e, tl=tl, v_tok=v_tok, g_tok2=g_tok2,
                              sc_ev_tok=sc_ev_tok):
                            for ci in range(IC):
                                S.use(g, ("act", ci))
                                if ci == 0:
                                    g.wait_ge(S.sem["vev"], sc_ev_tok[tl])
                                g.wait_ge(S.sem["vau"], v_tok[ci])
                                ii = g.tensor_tensor(act_s[:, ci, :],
                                                     tmp_bf[:, ci, :],
                                                     scbc[:, tl, :], ALU.mult)
                                g_tok2[ci] = S.inc(ii, "gact")

                        pend = (e, tl, t0, t1, g_tok2, gc_done)

                emit_w2_full(pend)
                pend = None
                if call == 1:
                    @blk.sync
                    def _(s):
                        for kc in range(KC):
                            for tl in range(NT):
                                S.use(s, ("Rs", kc, tl))
                            S.inc(s.dma_start(
                                outd.rearrange("(c p) l -> c p l", p=128)[kc],
                                R[:, kc, :]), "oq", 16)
            S.barrier_clear()

        def emit_attn():
            with ExitStack() as mem, nc.Block() as blk:
                psn = lambda name, shp, dt=F32: mem.enter_context(
                    nc.psum_tensor(f"{name}_at", shp, dt))
                st0 = psn("st0", [1, TT])
                st1 = psn("st1", [1, TT])
                st2 = psn("st2", [1, TT])
                st3 = psn("st3", [1, TT])
                bc0 = psn("bc0", [128, TT])
                bc1 = psn("bc1", [128, TT])
                x2ps = [psn(f"x2ps{i}", [128, TT]) for i in range(2)]

                sba = lambda name, shp, dt: mem.enter_context(
                    nc.sbuf_tensor(name, shp, dt))
                winbf = sba("winbf", [128, KC, D], BF16)
                wland = sba("wland_at", [128, LSZ // 2], F32)
                rows = sba("rows_at", [1, 7, TT], F32)
                x1t = sba("x1t", [128, 2, KC, TT], BF16)
                x2t = sba("x2t", [128, KC, TT], F32)
                x3t = sba("x3t", [128, KC, TT], F32)
                PAD = 16
                assert NTAP <= PAD
                x3bf = sba("x3bf", [128, KC, PAD + TT], BF16)
                cst = sba("cst", [128, TT], F32)
                xbf_q = sba("xbf_q", [128, 2, TT], BF16)
                xsq_q = sba("xsq_q", [128, 2, TT], BF16)
                carry = sba("carry", [128, KC], F32)
                diagall = sba("diagall", [128, KC, NTAP, 128], BF16)

                QN = max(1, D // (LSZ // 2 // KC)) if KC else 1
                qw = D // QN
                gc_now = S.val("gc")
                dwq_now = S.val("dwq")

                @blk.sync
                def _(s, gc_now=gc_now):
                    for qt in range(QN):
                        S.wait(s, "gc", gc_now + qt)
                        ap = winT[:, qt * qw:(qt + 1) * qw] \
                            .rearrange("(q p) n -> p q n", p=128)
                        dst = bass.AP(wland, 0,
                                      [[LSZ // 2, 128], [qw, KC], [1, qw]])
                        S.inc(s.dma_start(dst, ap), "dwq", 16)

                @blk.scalar
                def _(sc, dwq_now=dwq_now):
                    for qt in range(QN):
                        sc.wait_ge(S.sem["dwq"], dwq_now + 16 * (qt + 1))
                        srcw = bass.AP(wland, 0,
                                       [[LSZ // 2, 128], [qw, KC], [1, qw]])
                        S.inc(sc.activation(winbf[:, :, qt * qw:(qt + 1) * qw],
                                            srcw, AF.Copy), "gc")
                gc_win = S.val("gc")

                @blk.vector
                def _(v):
                    S.inc(v.memset(carry[:, :], 0.0), "vsc")
                    S.inc(v.memset(x3bf[:, :, 0:PAD], 0.0), "vsc")
                    prevv = None
                    for cc in range(KC):
                        for j in range(NTAP):
                            kcol = k0sb[:, cc, j:j + 1]
                            kbc = bass.AP(k0sb, kcol.offset,
                                          [list(kcol.ap[0]), [0, 128]])
                            prevv = v.tensor_tensor(diagall[:, cc, j, :],
                                                    idn[:, :], kbc, ALU.mult)
                    S.inc(prevv, "vtp")
                dg_tok = S.val("vtp")

                ln1_tok = {0: ln_stats_part1(blk, (lambda cc:
                                                   R[:, cc, 0:TT]),
                                             st2, st3, "stq", 1,
                                             bufs=(xbf_q, xsq_q))}
                apply_tok = {}

                def emit_ln1(tl):
                    ln_stats_part2(blk, rows, st2, st3, "stq", ln1_tok[tl])
                    ln_rows(blk, rows)
                    ln_bcast_tile(blk, rows, bc0, bc1)

                    @blk.vector
                    def _(v, tl=tl):
                        S.use(v, ("x1t", tl % 2))

                    ln_apply_tile(blk, (lambda cc, tl=tl:
                                        R[:, cc, tl * TT:(tl + 1) * TT]),
                                  (lambda cc, tl=tl: x1t[:, tl % 2, cc, :]),
                                  bc0, bc1)
                    apply_tok[tl] = S.val("vln")

                emit_ln1(0)
                for tl in range(NT):
                    t0, t1 = tl * TT, (tl + 1) * TT
                    vln_tile = apply_tok[tl]

                    x2tok = {}
                    for m in range(KC):
                        @blk.tensor
                        def _(t, tl=tl, m=m, vln_tile=vln_tile):
                            if m == 0:
                                S.wait(t, "vln", vln_tile)
                                if tl == 0:
                                    S.wait(t, "gc", gc_win)
                            S.use(t, ("x2ps", m % 2))
                            mm = None
                            for k in range(KC):
                                mm = t.matmul(x2ps[m % 2][:, :],
                                              winbf[:, k, m * 128:(m + 1) * 128],
                                              x1t[:, tl % 2, k, :],
                                              start=(k == 0), stop=(k == KC - 1))
                            tok = S.inc(mm, "tst")
                            if m == KC - 1:
                                S.release(("x1t", tl % 2), "tst", tok)
                            S.rel[("x2m", m)] = [("tst", tok)]

                        @blk.vector
                        def _(v, m=m, x2tok=x2tok):
                            S.use(v, ("x2m", m))
                            S.use(v, ("x2t", m))
                            ii = v.tensor_scalar(x2t[:, m, :], x2ps[m % 2][:, :],
                                                 prm[:, m:m + 1], None, ALU.add)
                            vv = S.inc(ii, "vmi")
                            x2tok[m] = vv
                            S.release(("x2ps", m % 2), "vmi", vv)

                    ln_stats_tile(blk, (lambda cc: x2t[:, cc, :]),
                                  rows, st0, st1,
                                  src_tok=(lambda cc, x2tok=x2tok:
                                           [("vmi", x2tok[cc])]))
                    ln_rows(blk, rows)
                    if tl + 1 < NT:
                        ln1_tok[tl + 1] = ln_stats_part1(
                            blk, (lambda cc, tl=tl:
                                  R[:, cc, (tl + 1) * TT:(tl + 2) * TT]),
                            st2, st3, "stq", 1, bufs=(xbf_q, xsq_q))
                    ln_bcast_tile(blk, rows, bc0, bc1)

                    @blk.vector
                    def _(v):
                        S.use(v, ("x3t",))

                    ln_apply_tile(blk, (lambda cc: x2t[:, cc, :]),
                                  (lambda cc: x3t[:, cc, :]),
                                  bc0, bc1, gb=True)
                    vln_x3 = S.val("vln")
                    for m in range(KC):
                        S.release(("x2t", m), "vln", vln_x3)

                    gmi_b = S.val("gmi")

                    @blk.gpsimd
                    def _(g, tl=tl, vln_x3=vln_x3, gmi_b=gmi_b):
                        g.wait_ge(S.sem["vln"], vln_x3)
                        if tl > 0:
                            g.wait_ge(S.sem["gmi"], gmi_b)
                        for cc in range(KC):
                            S.inc(g.tensor_copy(x3bf[:, cc, PAD:PAD + TT],
                                                x3t[:, cc, :]), "gmi")

                    if tl + 1 < NT:
                        emit_ln1(tl + 1)

                    x3rel = []
                    tap_tok = {}

                    def emit_evac(v, ec, tl=tl, t0=t0, t1=t1):
                        S.use(v, ("ytq", ec, tl))
                        S.use(v, ("Rs2", ec, tl))
                        ii = v.tensor_tensor(R[:, ec, t0:t1], R[:, ec, t0:t1],
                                             x2ps[ec % 2][:, :], ALU.add)
                        rv = S.inc(ii, "vev")
                        S.release(("x2ps", ec % 2), "vev", rv)
                        S.release(("Rs2", ec, tl), "vev", rv)

                    for cc in range(KC):
                        @blk.vector
                        def _(v, cc=cc, tl=tl, t0=t0, t1=t1, vln_x3=vln_x3,
                              x3rel=x3rel):
                            if cc == 0:
                                S.wait(v, "vln", vln_x3)
                            S.use(v, ("cst",))
                            S.use(v, ("carry", cc))
                            ii = v.tensor_tensor_scan(cst[:, :],
                                                      ones_t[:, :],
                                                      x3t[:, cc, :],
                                                      carry[:, cc:cc + 1],
                                                      ALU.mult, ALU.add)
                            vv = S.inc(ii, "vsc")
                            S.wait(v, "vsc", vv)
                            ii = v.tensor_copy(carry[:, cc:cc + 1],
                                               cst[:, TT - 1:TT])
                            cv = S.inc(ii, "vsc")
                            S.release(("carry", cc), "vsc", cv)
                            dcol = prm[:, 3 * KC + cc:3 * KC + cc + 1]
                            S.use(v, ("Rs2", cc, tl))
                            ii = v.scalar_tensor_tensor(R[:, cc, t0:t1],
                                                        cst[:, :], dcol,
                                                        R[:, cc, t0:t1],
                                                        ALU.mult, ALU.add)
                            rv = S.inc(ii, "vsc")
                            S.release(("cst",), "vsc", rv)
                            S.wait(v, "vsc", rv)
                            ii = v.tensor_tensor(R[:, cc, t0:t1], R[:, cc, t0:t1],
                                                 x3t[:, cc, :], ALU.add)
                            rv2 = S.inc(ii, "vsc")
                            S.release(("Rs2", cc, tl), "vsc", rv2)
                            x3rel.append(("vsc", rv2))
                            if cc >= 2:
                                emit_evac(v, cc - 2)

                        @blk.tensor
                        def _(t, cc=cc, tl=tl, t0=t0, t1=t1, gmi_b=gmi_b,
                              tap_tok=tap_tok):
                            if tl == 0 and cc == 0:
                                S.wait(t, "vtp", dg_tok)
                            S.wait(t, "gmi", gmi_b + cc + 1)
                            S.use(t, ("x2ps", cc % 2))
                            mm = None
                            for j in range(NTAP):
                                o0 = max(t0, j)
                                mm = t.matmul(x2ps[cc % 2][:, o0 - t0:TT],
                                              diagall[:, cc, j, :],
                                              x3bf[:, cc, PAD + o0 - t0 - j:
                                                   PAD + TT - j],
                                              start=(j == 0),
                                              stop=(j == NTAP - 1))
                            tok = S.inc(mm, "tw2")
                            tap_tok[cc] = tok
                            S.rel[("ytq", cc, tl)] = [("tw2", tok)]

                    @blk.vector
                    def _(v, tl=tl, t0=t0, t1=t1):
                        emit_evac(v, KC - 2)
                        emit_evac(v, KC - 1)

                    S.rel[("x3t",)] = x3rel + [("gmi", gmi_b + KC)]

                    if tl + 1 < NT:
                        @blk.gpsimd
                        def _(g, tap_tok=tap_tok):
                            g.wait_ge(S.sem["tw2"], tap_tok[KC - 1])
                            S.inc(g.tensor_copy(x3bf[:, :, 0:PAD],
                                                x3bf[:, :, TT:TT + PAD]), "gmi")
            S.barrier_clear()

        def dump(pairs):
            with nc.Block() as blk:
                @blk.sync
                def _(s):
                    for dst, srcb in pairs:
                        S.inc(s.dma_start(dst, srcb), "dq", 16)
                    S.wait(s, "dq")
            S.barrier_clear()

        emit_ffn(0)
        if c.dbg:
            dump([(d_R1, R[:, :, :]), (d_k0, k0sb[:, :, :])])
        emit_attn()
        if c.dbg:
            dump([(d_R2, R[:, :, :])])
        emit_ffn(1)

        with nc.Block() as blk:
            @blk.sync
            def _(s):
                S.wait(s, "oq")

    return nc


def _col_layout(v, KC):
    return np.ascontiguousarray(np.asarray(v, dtype=np.float32).reshape(KC, 128).T)


def make_core_inputs(c, x_b, W_in, b_in, gamma, beta, A_re, A_im, B_re, B_im,
                     C_re, C_im, D_re, gate_w, w1, w3, w2):
    KC = c.KC
    f32 = lambda a: np.ascontiguousarray(np.asarray(a, dtype=np.float32))
    return {
        "xT": f32(x_b.T),
        "w1": f32(w1), "w3": f32(w3), "w2": f32(w2),
        "gateT": f32(np.asarray(gate_w).T),
        "winT": f32(np.asarray(W_in).T),
        "bin_c": _col_layout(b_in, KC),
        "gam_c": _col_layout(gamma, KC),
        "bet_c": _col_layout(beta, KC),
        "dre_c": _col_layout(D_re, KC),
        "abcol": f32(np.stack([np.asarray(A_re), np.asarray(A_im),
                               np.asarray(B_re), np.asarray(B_im)], axis=1)),
        "ctre": f32(np.asarray(C_re).T),
        "ctim": f32(np.asarray(C_im).T),
        "onec_bf": np.ones((128, 1), dtype=ml_dtypes.bfloat16),
        "oner_bf": np.ones((1, 128), dtype=ml_dtypes.bfloat16),
        "oner_f": np.ones((1, 128), dtype=np.float32),
        "identd": np.eye(128, dtype=ml_dtypes.bfloat16),
    }


_NC_CACHE = {}


def kernel(L0_attn_x, L0_attn_W_in, L0_attn_b_in, L0_attn_gamma, L0_attn_beta,
           L0_attn_A_re, L0_attn_A_im, L0_attn_B_re, L0_attn_B_im,
           L0_attn_C_re, L0_attn_C_im, L0_attn_D_re, L0_attn_D_im,
           L0_ffn_gate_w, L0_ffn_w1, L0_ffn_w3, L0_ffn_w2):
    c = Cfg()
    if "nc" not in _NC_CACHE:
        _NC_CACHE["nc"] = build_nc(c)
    nc = _NC_CACHE["nc"]
    x = np.asarray(L0_attn_x, dtype=np.float32)
    nb = x.shape[0]
    base = make_core_inputs(c, x[0], L0_attn_W_in, L0_attn_b_in, L0_attn_gamma,
                            L0_attn_beta, L0_attn_A_re, L0_attn_A_im,
                            L0_attn_B_re, L0_attn_B_im, L0_attn_C_re,
                            L0_attn_C_im, L0_attn_D_re, L0_ffn_gate_w,
                            L0_ffn_w1, L0_ffn_w3, L0_ffn_w2)
    in_maps = []
    for b in range(nb):
        m = dict(base)
        m["xT"] = np.ascontiguousarray(x[b].T)
        in_maps.append(m)
    res = run_bass_kernel_spmd(nc, in_maps, core_ids=list(range(nb)))
    out = np.stack([np.asarray(res.results[b]["out"], dtype=np.float32).T
                    for b in range(nb)], axis=0)
    return np.ascontiguousarray(out)


# version 34
# speedup vs baseline: 1.0149x; 1.0038x over previous
import numpy as np
import ml_dtypes
from contextlib import ExitStack

import concourse.bass as bass
import concourse.mybir as mybir
from concourse.bass_utils import run_bass_kernel_spmd

F32 = mybir.dt.float32
BF16 = mybir.dt.bfloat16
AF = mybir.ActivationFunctionType
ALU = mybir.AluOpType
AX = mybir.AxisListType

BIG = 1.0e30


class Cfg:
    def __init__(self, D=1024, DI=512, NE=16, L=2048, TT=512, NTAP=12, NST=64,
                 TOPK=4, EPS=1e-5):
        self.D, self.DI, self.NE, self.L, self.TT = D, DI, NE, L, TT
        self.NTAP, self.NST, self.TOPK, self.EPS = NTAP, NST, TOPK, EPS
        self.dbg = False
        self.KC = D // 128
        self.IC = DI // 128
        self.NT = L // TT
        self.LC = L // 128
        assert D % 128 == 0 and DI % 128 == 0 and L % TT == 0 and TT % 128 == 0
        assert self.KC % 2 == 0 and self.LC % 4 == 0


class B:
    def __init__(self, nc, cfg, ctx):
        self.nc = nc
        self.c = cfg
        names = ["dq", "dwq", "oq", "gc",
                 "tst", "tgu", "tw2", "tsc",
                 "vln", "vrow", "vau", "vev", "vmi", "vsc", "vtp", "vst",
                 "srow", "sst", "sact",
                 "gmi", "gact"]
        self.sem = {n: ctx.enter_context(nc.semaphore(n)) for n in names}
        self.cnt = {n: 0 for n in names}
        self.rel = {}

    def inc(self, instr, name, k=1):
        instr.then_inc(self.sem[name], k)
        self.cnt[name] += k
        return self.cnt[name]

    def val(self, name):
        return self.cnt[name]

    def wait(self, eng, name, v=None):
        v = self.cnt[name] if v is None else v
        if v > 0:
            eng.wait_ge(self.sem[name], v)

    def use(self, eng, res):
        for (n, v) in self.rel.get(res, []):
            if v > 0:
                eng.wait_ge(self.sem[n], v)

    def release(self, res, name, v):
        self.rel[res] = [(name, v)]

    def barrier_clear(self):
        self.rel = {}


def build_nc(c: Cfg):
    nc = bass.Bass("TRN2", target_bir_lowering=False)
    D, DI, NE, L, TT = c.D, c.DI, c.NE, c.L, c.TT
    KC, IC, NT, LC = c.KC, c.IC, c.NT, c.LC
    NTAP, NST = c.NTAP, c.NST
    HALF = D // 2

    xT = nc.dram_tensor("xT", [D, L], F32, kind="ExternalInput").ap()
    w1d = nc.dram_tensor("w1", [NE, D, DI], F32, kind="ExternalInput").ap()
    w3d = nc.dram_tensor("w3", [NE, D, DI], F32, kind="ExternalInput").ap()
    w2d = nc.dram_tensor("w2", [NE, DI, D], F32, kind="ExternalInput").ap()
    gateT = nc.dram_tensor("gateT", [D, NE], F32, kind="ExternalInput").ap()
    winT = nc.dram_tensor("winT", [D, D], F32, kind="ExternalInput").ap()
    bin_c = nc.dram_tensor("bin_c", [128, KC], F32, kind="ExternalInput").ap()
    gam_c = nc.dram_tensor("gam_c", [128, KC], F32, kind="ExternalInput").ap()
    bet_c = nc.dram_tensor("bet_c", [128, KC], F32, kind="ExternalInput").ap()
    dre_c = nc.dram_tensor("dre_c", [128, KC], F32, kind="ExternalInput").ap()
    abcold = nc.dram_tensor("abcol", [NST, 4], F32, kind="ExternalInput").ap()
    ctred = nc.dram_tensor("ctre", [NST, D], F32, kind="ExternalInput").ap()
    ctimd = nc.dram_tensor("ctim", [NST, D], F32, kind="ExternalInput").ap()
    onecd = nc.dram_tensor("onec_bf", [128, 1], BF16, kind="ExternalInput").ap()
    onerd = nc.dram_tensor("oner_bf", [1, 128], BF16, kind="ExternalInput").ap()
    onerfd = nc.dram_tensor("oner_f", [1, 128], F32, kind="ExternalInput").ap()
    identd = nc.dram_tensor("identd", [128, 128], BF16, kind="ExternalInput").ap()
    outd = nc.dram_tensor("out", [D, L], F32, kind="ExternalOutput").ap()
    if c.dbg:
        d_xhat = nc.dram_tensor("d_xhat", [128, KC, L], BF16, kind="ExternalOutput").ap()
        d_sc = nc.dram_tensor("d_sc", [NE, L], BF16, kind="ExternalOutput").ap()
        d_R1 = nc.dram_tensor("d_R1", [128, KC, L], F32, kind="ExternalOutput").ap()
        d_x3 = nc.dram_tensor("d_x3", [128, KC, L], BF16, kind="ExternalOutput").ap()
        d_R2 = nc.dram_tensor("d_R2", [128, KC, L], F32, kind="ExternalOutput").ap()
        d_k0 = nc.dram_tensor("d_k0", [128, KC, NTAP], F32, kind="ExternalOutput").ap()
        d_x1 = nc.dram_tensor("d_x1", [128, KC, TT], BF16, kind="ExternalOutput").ap()
        d_x2 = nc.dram_tensor("d_x2", [128, KC, TT], F32, kind="ExternalOutput").ap()

    top = ExitStack()
    with top:
        S = B(nc, c, top)
        sb = lambda name, shp, dt: top.enter_context(nc.sbuf_tensor(name, shp, dt))

        R = sb("R", [128, KC, L], F32)
        WSZ = 2 * KC * DI + IC * D
        OW1, OW3, OW2 = 0, KC * DI, 2 * KC * DI
        LSZ = max(KC * DI, IC * D, KC * HALF)
        gatebf = sb("gatebf", [128, KC, NE], BF16)
        prm = sb("prm", [128, 4 * KC], F32)
        acol = sb("acol", [NST, 4], F32)
        k0sb = sb("k0sb", [128, KC, NTAP], F32)
        onec = sb("onec", [128, 1], BF16)
        oner2 = sb("oner2", [65, 128], BF16)
        onerf = sb("onerf", [1, 128], F32)
        idn = sb("idn", [128, 128], BF16)
        ones_t = sb("ones_t", [128, TT], F32)
        xbf_s = sb("xbf_s", [128, 2, TT], BF16)
        xsq_s = sb("xsq_s", [128, 2, TT], BF16)
        lnt = sb("lnt", [128, 2, TT], F32)
        start_mem = ExitStack()
        ssb = lambda name, shp, dt: start_mem.enter_context(
            nc.sbuf_tensor(name, shp, dt))
        gland = ssb("gland", [128, KC, NE], F32)
        ctbf = ssb("ctbf", [NST, 2 * D], BF16)
        ctl = ssb("ctl", [NST, D], F32)
        ajb = ssb("ajb", [NST, 2 * NTAP], F32)
        ajbbf = ssb("ajbbf", [NST, 2 * NTAP], BF16)
        scr2 = ssb("scr2", [NST, 2], F32)

        dq_small = [0]
        dq_full = [0]
        with nc.Block() as blk:
            @blk.sync
            def _(s):
                S.inc(s.dma_start(gland[:, :, :],
                                  gateT.rearrange("(c p) e -> p c e", p=128)), "dq", 16)
                S.inc(s.dma_start(prm[:, 0 * KC:1 * KC], bin_c), "dq", 16)
                S.inc(s.dma_start(prm[:, 1 * KC:2 * KC], gam_c), "dq", 16)
                S.inc(s.dma_start(prm[:, 2 * KC:3 * KC], bet_c), "dq", 16)
                S.inc(s.dma_start(prm[:, 3 * KC:4 * KC], dre_c), "dq", 16)
                S.inc(s.dma_start(acol[:, :], abcold), "dq", 16)
                S.inc(s.dma_start(onec[:, :], onecd), "dq", 16)
                S.inc(s.dma_start(oner2[0:1, :], onerd), "dq", 16)
                S.inc(s.dma_start(oner2[64:65, :], onerd), "dq", 16)
                S.inc(s.dma_start(onerf[:, :], onerfd), "dq", 16)
                S.inc(s.dma_start(idn[:, :], identd), "dq", 16)
                S.inc(s.dma_start(ctl[:, :], ctred), "dq", 16)
                dq_small[0] = S.val("dq")
                for kc in range(KC):
                    S.inc(s.dma_start(R[:, kc, :],
                                      xT.rearrange("(c p) l -> c p l", p=128)[kc]),
                          "dq", 16)
                dq_full[0] = S.val("dq")

            @blk.scalar
            def _(sc):
                S.wait(sc, "dq", dq_small[0])
                S.inc(sc.activation(gatebf[:, :, :], gland[:, :, :], AF.Copy), "gc")
                S.inc(sc.activation(ctbf[:, 0:D], ctl[:, :], AF.Copy), "gc")

            @blk.vector
            def _(v):
                v.memset(ones_t[:, :], 1.0)

        with nc.Block() as blk:
            @blk.sync
            def _(s):
                S.inc(s.dma_start(ctl[:, :], ctimd), "dq", 16)

            @blk.scalar
            def _(sc):
                S.wait(sc, "dq")
                S.inc(sc.activation(ctbf[:, D:2 * D], ctl[:, :], AF.Copy), "gc")

        with nc.Block() as blk:
            @blk.vector
            def _(v):
                are, aim = acol[:, 0:1], acol[:, 1:2]
                v.tensor_copy(ajb[:, 0:1], acol[:, 2:3])
                vv = S.inc(v.tensor_copy(ajb[:, NTAP:NTAP + 1], acol[:, 3:4]), "vmi")
                for j in range(1, NTAP):
                    S.wait(v, "vmi", vv)
                    pr, pi = ajb[:, j - 1:j], ajb[:, NTAP + j - 1:NTAP + j]
                    cr, ci_ = ajb[:, j:j + 1], ajb[:, NTAP + j:NTAP + j + 1]
                    v.tensor_tensor(scr2[:, 0:1], pi, aim, ALU.mult)
                    vv = S.inc(v.tensor_tensor(scr2[:, 1:2], pr, aim, ALU.mult), "vmi")
                    S.wait(v, "vmi", vv)
                    v.scalar_tensor_tensor(cr, pr, are, scr2[:, 0:1], ALU.mult,
                                           ALU.subtract)
                    vv = S.inc(v.scalar_tensor_tensor(ci_, pi, are, scr2[:, 1:2],
                                                      ALU.mult, ALU.add), "vmi")
                S.wait(v, "vmi", vv)
                v.tensor_copy(ajbbf[:, 0:NTAP], ajb[:, 0:NTAP])
                S.inc(v.tensor_scalar(ajbbf[:, NTAP:2 * NTAP], ajb[:, NTAP:2 * NTAP],
                                      -1.0, None, ALU.mult), "vmi")

        with ExitStack() as ps0:
            k0ps = ps0.enter_context(nc.psum_tensor("k0ps", [128, 512], F32))
            with nc.Block() as blk:
                for kc in range(KC):
                    @blk.tensor
                    def _(t, kc=kc):
                        if kc == 0:
                            S.wait(t, "gc")
                            S.wait(t, "vmi")
                        S.use(t, "k0ps")
                        t.matmul(k0ps[:, 0:NTAP], ctbf[:, kc * 128:(kc + 1) * 128],
                                 ajbbf[:, 0:NTAP], start=True, stop=False)
                        S.inc(t.matmul(k0ps[:, 0:NTAP],
                                       ctbf[:, D + kc * 128:D + (kc + 1) * 128],
                                       ajbbf[:, NTAP:2 * NTAP], start=False,
                                       stop=True), "tst")

                    @blk.vector
                    def _(v, kc=kc):
                        S.wait(v, "tst")
                        vv = S.inc(v.tensor_copy(k0sb[:, kc, :], k0ps[:, 0:NTAP]), "vmi")
                        S.release("k0ps", "vmi", vv)
        with nc.Block() as blk:
            @blk.vector
            def _(v):
                S.wait(v, "vmi")
        start_mem.close()
        S.barrier_clear()

        def ln_stats_part1(blk, src_chunk, st0, st1, skey, bslot,
                           src_tok=None, bufs=None):
            xb, xq = bufs if bufs is not None else (xbf_s, xsq_s)
            sst_base = S.val("vst")
            tst_base = S.val("tst")

            @blk.vector
            def _(vq):
                for cc in range(KC):
                    bi = cc % 2 + 2 * bslot * 0
                    if src_tok is not None:
                        for (n_, v_) in src_tok(cc):
                            vq.wait_ge(S.sem[n_], v_)
                    if cc >= 2:
                        S.wait(vq, "tst", tst_base + cc - 1)
                    else:
                        S.use(vq, ("stat", bslot, bi))
                    vq.tensor_copy(xb[:, bi, :], src_chunk(cc))
                    S.inc(vq.tensor_tensor(xq[:, bi, :], src_chunk(cc),
                                           src_chunk(cc), ALU.mult), "vst")

            @blk.tensor
            def _(t):
                S.use(t, (skey, 0))
                S.use(t, (skey, 1))
                for cc in range(KC):
                    bi = cc % 2
                    S.wait(t, "vst", sst_base + cc + 1)
                    t.matmul(st0[:, :], onec[:, :], xb[:, bi, :],
                             start=(cc == 0), stop=(cc == KC - 1))
                    v = S.inc(t.matmul(st1[:, :], onec[:, :], xq[:, bi, :],
                                       start=(cc == 0), stop=(cc == KC - 1)), "tst")
                    S.release(("stat", bslot, bi), "tst", v)
            return S.val("tst")

        def ln_stats_part2(blk, rows, st0, st1, skey, mm_tok):
            @blk.vector
            def _(v):
                S.wait(v, "tst", mm_tok)
                S.use(v, ("rows",))
                v.tensor_copy(rows[0:1, 0, :], st0[0:1, :])
                vv = S.inc(v.tensor_copy(rows[0:1, 1, :], st1[0:1, :]), "vrow")
                S.release((skey, 0), "vrow", vv)
                S.release((skey, 1), "vrow", vv)

        def ln_stats_tile(blk, src_chunk, rows, st0, st1, src_tok=None,
                          skey="stp", bslot=0):
            tok = ln_stats_part1(blk, src_chunk, st0, st1, skey, bslot, src_tok)
            ln_stats_part2(blk, rows, st0, st1, skey, tok)

        def ln_rows(blk, rows):
            def rr(i):
                return rows[0:1, i, :]

            @blk.vector
            def _(v):
                S.wait(v, "vrow")
                v.tensor_scalar(rr(2), rr(0), 1.0 / D, None, ALU.mult)
                vv = S.inc(v.tensor_scalar(rr(3), rr(1), 1.0 / D, None, ALU.mult),
                           "vrow")
                S.wait(v, "vrow", vv)
                vv = S.inc(v.tensor_tensor(rr(4), rr(2), rr(2), ALU.mult), "vrow")
                S.wait(v, "vrow", vv)
                vv = S.inc(v.scalar_tensor_tensor(rr(3), rr(4), -1.0, rr(3),
                                                  ALU.mult, ALU.add), "vrow")
                S.wait(v, "vrow", vv)
                vv = S.inc(v.tensor_scalar(rr(4), rr(3), float(c.EPS), None,
                                           ALU.add), "vrow")

            @blk.scalar
            def _(sc):
                S.wait(sc, "vrow")
                vv = S.inc(sc.activation(rr(3), rr(4), AF.Ln), "srow")
                S.wait(sc, "srow", vv)
                S.inc(sc.activation(rr(5), rr(3), AF.Exp, scale=-0.5), "srow")

            @blk.vector
            def _(v):
                S.wait(v, "srow")
                S.inc(v.tensor_tensor(rr(6), rr(2), rr(5), ALU.mult), "vrow")

        def ln_bcast_tile(blk, rows, bc0, bc1):
            @blk.tensor
            def _(t):
                S.wait(t, "vrow")
                S.wait(t, "srow")
                S.use(t, "bc0")
                S.use(t, "bc1")
                t.matmul(bc0[:, :], onerf[:, :], rows[0:1, 5, :],
                         start=True, stop=True)
                tok = S.inc(t.matmul(bc1[:, :], onerf[:, :], rows[0:1, 6, :],
                                     start=True, stop=True), "tst")
                S.release(("rows",), "tst", tok)

        def ln_apply_tile(blk, src_chunk, dst_chunk, bc0, bc1, gb=False):
            @blk.vector
            def _(v):
                S.wait(v, "tst")
                toks = {}
                last = 0

                def emit_b(pc):
                    nonlocal last
                    S.wait(v, "vln", toks[pc])
                    if not gb:
                        ib = v.tensor_tensor(dst_chunk(pc), lnt[:, pc % 2, :],
                                             bc1[:, :], ALU.subtract)
                        last = S.inc(ib, "vln")
                    else:
                        ib = v.tensor_tensor(lnt[:, pc % 2, :], lnt[:, pc % 2, :],
                                             bc1[:, :], ALU.subtract)
                        vv2 = S.inc(ib, "vln")
                        S.wait(v, "vln", vv2)
                        gcol = prm[:, KC + pc:KC + pc + 1]
                        bcol = prm[:, 2 * KC + pc:2 * KC + pc + 1]
                        bb = bass.AP(bcol.tensor, bcol.offset,
                                     [list(bcol.ap[0]), [0, TT]])
                        ic2 = v.scalar_tensor_tensor(dst_chunk(pc),
                                                     lnt[:, pc % 2, :], gcol, bb,
                                                     ALU.mult, ALU.add)
                        last = S.inc(ic2, "vln")
                    S.release(("lnt", pc % 2), "vln", last)

                for cc in range(KC):
                    S.use(v, ("lnt", cc % 2))
                    ia = v.tensor_tensor(lnt[:, cc % 2, :], src_chunk(cc), bc0[:, :],
                                         ALU.mult)
                    toks[cc] = S.inc(ia, "vln")
                    if cc >= 1:
                        emit_b(cc - 1)
                emit_b(KC - 1)
                S.release("bc0", "vln", last)
                S.release("bc1", "vln", last)

        def emit_ffn(call):
            with ExitStack() as mem, nc.Block() as blk:
                if call == 0:
                    @blk.vector
                    def _(v):
                        S.wait(v, "dq", dq_full[0])
                psn = lambda name, shp, dt=F32: mem.enter_context(
                    nc.psum_tensor(f"{name}_f{call}", shp, dt))
                sbf = lambda name, shp, dt: mem.enter_context(
                    nc.sbuf_tensor(f"{name}_f{call}", shp, dt))
                wbf = [sbf(f"wbf{i}", [128, WSZ], BF16) for i in range(2)]
                wland = sbf("wland", [128, LSZ // 2], F32)
                xhat = sbf("xhat", [128, KC, L], BF16)
                scbc = sbf("scbc", [128, NT, TT], BF16)
                act_s = sbf("act_s", [128, IC, TT], BF16)
                rows = sbf("rows", [1, 7, TT], F32)
                silu_s = sbf("silu_s", [128, IC, TT], BF16)
                tmp_bf = sbf("tmp_bf", [128, IC, TT], BF16)
                scoresF = sbf("scoresF", [NE, L], BF16)
                scores0 = sbf("scores0", [65, L], BF16)
                logf_bf = sbf("logf_bf", [NE, TT], BF16)
                ltok = sbf("ltok", [128, LC, NE], F32)
                wrk = sbf("wrk", [128, LC, NE], F32)
                msk = sbf("msk", [128, LC, NE], F32)
                mcol = sbf("mcol", [128, LC, 1], F32)
                rden = sbf("rden", [128, LC, 1], F32)
                stok_bf = sbf("stok_bf", [128, LC, NE], BF16)

                def w_t(e, part, k, m):
                    off = [OW1, OW3, OW2][part] + k * ([DI, DI, D][part]) + m * 128
                    return wbf[e % 2][:, off:off + 128]
                with ExitStack() as pro:
                    ppsn = lambda name, shp, dt=F32: pro.enter_context(
                        nc.psum_tensor(f"{name}_p{call}", shp, dt))
                    st0 = ppsn("st0", [1, TT])
                    st1 = ppsn("st1", [1, TT])
                    bc0 = ppsn("bc0", [128, TT])
                    bc1 = ppsn("bc1", [128, TT])
                    lgf = ppsn("lgf", [NE, TT])
                    rtp = ppsn("rtp", [128, 4 * 256], BF16)
                    stp0 = ppsn("stp0", [NE, 1024], BF16)
                    stp1 = ppsn("stp1", [NE, 1024], BF16)
                    stp = [stp0, stp1]

                    for tl in range(NT):
                        ln_stats_tile(blk, (lambda cc, tl=tl:
                                            R[:, cc, tl * TT:(tl + 1) * TT]),
                                      rows, st0, st1)
                        ln_rows(blk, rows)
                        ln_bcast_tile(blk, rows, bc0, bc1)
                        ln_apply_tile(blk, (lambda cc, tl=tl:
                                            R[:, cc, tl * TT:(tl + 1) * TT]),
                                      (lambda cc, tl=tl:
                                       xhat[:, cc, tl * TT:(tl + 1) * TT]),
                                      bc0, bc1)

                    TC = TT // 128
                    for tl in range(NT):
                        t0, t1 = tl * TT, (tl + 1) * TT

                        @blk.tensor
                        def _(t, t0=t0, t1=t1):
                            S.wait(t, "vln")
                            S.use(t, "lgf")
                            for k in range(KC):
                                i = t.matmul(lgf[:, :], gatebf[:, k, :],
                                             xhat[:, k, t0:t1],
                                             start=(k == 0), stop=(k == KC - 1))
                            S.inc(i, "tst")

                        @blk.vector
                        def _(v):
                            S.wait(v, "tst")
                            S.use(v, "logf")
                            vv = S.inc(v.tensor_copy(logf_bf[:, :], lgf[:, :]),
                                       "vmi")
                            S.release("lgf", "vmi", vv)

                        @blk.tensor
                        def _(t, tl=tl):
                            S.wait(t, "vmi")
                            S.use(t, "rtp")
                            for i in range(TC):
                                ii = t.transpose(rtp[:, i * 256:i * 256 + NE],
                                                 logf_bf[:, i * 128:(i + 1) * 128],
                                                 idn[0:NE, 0:NE])
                            tok = S.inc(ii, "tst")
                            S.release("logf", "tst", tok)

                        @blk.vector
                        def _(v, tl=tl):
                            S.wait(v, "tst")
                            dst = bass.AP(ltok, tl * TC * NE,
                                          [[LC * NE, 128], [NE, TC], [1, NE]])
                            srcp = bass.AP(rtp, 0,
                                           [[4 * 256, 128], [256, TC], [1, NE]])
                            vv = S.inc(v.tensor_copy(dst, srcp), "vmi")
                            S.release("rtp", "vmi", vv)

                    @blk.vector
                    def _(v):
                        S.wait(v, "vmi")
                        full = ltok[:, :, :]
                        mb = bass.AP(mcol, 0, [[LC, 128], [1, LC], [0, NE]])
                        vv = S.inc(v.reduce_max(mcol[:, :, :], full, axis=AX.X),
                                   "vmi")
                        S.wait(v, "vmi", vv)
                        S.inc(v.tensor_tensor(wrk[:, :, :], full, mb, ALU.subtract),
                              "vmi")

                    @blk.scalar
                    def _(sc):
                        S.wait(sc, "vmi")
                        S.inc(sc.activation(ltok[:, :, :], wrk[:, :, :], AF.Exp),
                              "sst")

                    @blk.vector
                    def _(v):
                        S.wait(v, "sst")
                        full = ltok[:, :, :]
                        mb = bass.AP(mcol, 0, [[LC, 128], [1, LC], [0, NE]])
                        vv = S.inc(v.tensor_copy(wrk[:, :, :], full), "vmi")
                        for it in range(c.TOPK - 1):
                            S.wait(v, "vmi", vv)
                            vv = S.inc(v.reduce_max(mcol[:, :, :], wrk[:, :, :],
                                                    axis=AX.X), "vmi")
                            S.wait(v, "vmi", vv)
                            vv = S.inc(v.tensor_tensor(msk[:, :, :], wrk[:, :, :],
                                                       mb, ALU.is_ge), "vmi")
                            S.wait(v, "vmi", vv)
                            vv = S.inc(v.scalar_tensor_tensor(wrk[:, :, :],
                                                              msk[:, :, :], -BIG,
                                                              wrk[:, :, :],
                                                              ALU.mult, ALU.add),
                                       "vmi")
                        S.wait(v, "vmi", vv)
                        vv = S.inc(v.reduce_max(mcol[:, :, :], wrk[:, :, :],
                                                axis=AX.X), "vmi")
                        S.wait(v, "vmi", vv)
                        vv = S.inc(v.tensor_tensor(msk[:, :, :], full, mb,
                                                   ALU.is_ge), "vmi")
                        S.wait(v, "vmi", vv)
                        vv = S.inc(v.tensor_tensor(wrk[:, :, :], full, msk[:, :, :],
                                                   ALU.mult), "vmi")
                        S.wait(v, "vmi", vv)
                        vv = S.inc(v.reduce_sum(mcol[:, :, :], wrk[:, :, :],
                                                axis=AX.X), "vmi")
                        S.wait(v, "vmi", vv)
                        vv = S.inc(v.reciprocal(rden[:, :, :], mcol[:, :, :]), "vmi")
                        S.wait(v, "vmi", vv)
                        rb = bass.AP(rden, 0, [[LC, 128], [1, LC], [0, NE]])
                        vv = S.inc(v.tensor_tensor(stok_bf[:, :, :], wrk[:, :, :],
                                                   rb, ALU.mult), "vmi")

                    for lc in range(LC):
                        @blk.tensor
                        def _(t, lc=lc):
                            if lc == 0:
                                S.wait(t, "vmi")
                            S.use(t, ("stp", lc % 2))
                            S.inc(t.transpose(
                                stp[lc % 2][:, 0:128],
                                stok_bf[:, lc, :], idn[:, :]), "tst")

                        @blk.vector
                        def _(v, lc=lc):
                            S.wait(v, "tst")
                            vv = S.inc(v.tensor_copy(
                                scoresF[:, lc * 128:(lc + 1) * 128],
                                stp[lc % 2][:, 0:128]), "vtp")
                            S.release(("stp", lc % 2), "vtp", vv)


                A = [psn(f"A{i}", [128, TT]) for i in range(4)]
                C = [psn(f"C{i}", [128, TT]) for i in range(4)]

                @blk.tensor
                def _(t):
                    S.wait(t, "tst", S.val("tst"))
                    S.wait(t, "vln", S.val("vln"))
                    S.wait(t, "vtp", S.val("vtp"))
                    S.wait(t, "vmi", S.val("vmi"))
                    S.wait(t, "vrow", S.val("vrow"))

                def emit_wdma(e):
                    gc_now = S.val("gc")

                    @blk.sync
                    def _(s, e=e, gc_now=gc_now):
                        hcnt = 0
                        for p, (srcd, wid) in enumerate([(w1d, KC * DI),
                                                         (w3d, KC * DI),
                                                         (w2d, IC * D)]):
                            nch = srcd.shape[1] // 128
                            hch = nch // 2
                            wper = wid // nch
                            for hh in range(2):
                                S.wait(s, "gc", gc_now + hcnt)
                                S.use(s, ("wbuf", e % 2))
                                ap = srcd[e][hh * hch * 128:(hh + 1) * hch * 128, :] \
                                    .rearrange("(q p) n -> p q n", p=128)
                                dst = bass.AP(wland, 0,
                                              [[LSZ // 2, 128], [wper, hch],
                                               [1, wper]])
                                S.inc(s.dma_start(dst, ap), "dwq", 16)
                                hcnt += 1

                WPARTS = [(OW1, KC * DI), (OW3, KC * DI), (OW2, IC * D)]
                CAST_SLICES = {0: [0, 1], 1: [2, 3], 2: [4], 3: [5]}

                def emit_one_cast(v, e, hcnt):
                    p, hh = hcnt // 2, hcnt % 2
                    off, wid = WPARTS[p]
                    dwq_need = DWQ_BASE[e] + 16 * (hcnt + 1)
                    v.wait_ge(S.sem["dwq"], dwq_need)
                    S.inc(v.tensor_copy(
                        wbf[e % 2][:, off + hh * (wid // 2):
                                   off + (hh + 1) * (wid // 2)],
                        wland[:, 0:wid // 2]), "gc")

                def emit_wcast_slice(e, tl, gcb):
                    pass

                DWQ_BASE = {}

                def emit_wdma_tracked(e):
                    DWQ_BASE[e] = S.val("dwq")
                    emit_wdma(e)

                def emit_wcasts_now(e):
                    @blk.vector
                    def _(v, e=e):
                        for hcnt in range(6):
                            emit_one_cast(v, e, hcnt)

                def emit_w2_full(ctx):
                    (e, tl, t0, t1, g_tok2, gc_done) = ctx
                    nq = KC // 2

                    def w2_mms(t, q, ks):
                        pair = q % 2
                        ms = (2 * q, 2 * q + 1)
                        ks = list(ks)
                        S.wait(t, "gact", g_tok2[ks[-1]])
                        mm = None
                        for k in ks:
                            for m in ms:
                                mm = t.matmul(C[2 * pair + (m % 2)][:, :],
                                              w_t(e, 2, k, m),
                                              act_s[:, k, :],
                                              start=(k == 0),
                                              stop=(k == IC - 1))
                        return mm

                    def w2_fin(q, tok):
                        if q == nq - 1:
                            S.release(("wbuf", e % 2), "tw2", tok)
                            for ci in range(IC):
                                S.release(("act", ci), "tw2", tok)
                            if tl == NT - 1:
                                S.release(("scbc",), "tw2", tok)
                        S.rel[("q", q)] = [("tw2", tok)]

                    def emit_rupd(q):
                        pair = q % 2

                        @blk.vector
                        def _(v, q=q, pair=pair):
                            S.use(v, ("q", q))
                            vv = 0
                            for m in (2 * q, 2 * q + 1):
                                S.use(v, ("Rs", m, tl))
                                ii = v.scalar_tensor_tensor(
                                    R[:, m, t0:t1], C[2 * pair + (m % 2)][:, :],
                                    0.5, R[:, m, t0:t1], ALU.mult, ALU.add)
                                vv = S.inc(ii, "vev")
                                S.release(("Rs", m, tl), "vev", vv)
                            S.release(("Cp", pair), "vev", vv)

                    if nq >= 2:
                        @blk.tensor
                        def _(t):
                            if tl == 0:
                                S.wait(t, "gc", gc_done)
                            S.use(t, ("Cp", 0))
                            w2_mms(t, 0, range(IC - 1))
                            S.use(t, ("Cp", 1))
                            w2_mms(t, 1, range(IC - 1))
                            tok0 = S.inc(w2_mms(t, 0, [IC - 1]), "tw2")
                            w2_fin(0, tok0)
                            tok1 = S.inc(w2_mms(t, 1, [IC - 1]), "tw2")
                            w2_fin(1, tok1)
                        emit_rupd(0)
                        emit_rupd(1)
                        for q in range(2, nq):
                            @blk.tensor
                            def _(t, q=q):
                                S.use(t, ("Cp", q % 2))
                                tok = S.inc(w2_mms(t, q, range(IC)), "tw2")
                                w2_fin(q, tok)
                            emit_rupd(q)
                    else:
                        @blk.tensor
                        def _(t):
                            if tl == 0:
                                S.wait(t, "gc", gc_done)
                            S.use(t, ("Cp", 0))
                            tok = S.inc(w2_mms(t, 0, range(IC)), "tw2")
                            w2_fin(0, tok)
                        emit_rupd(0)

                gc_base = {0: S.val("gc")}
                emit_wdma_tracked(0)
                emit_wcasts_now(0)
                gcd = {0: S.val("gc")}
                pend = None
                for e in range(NE):
                    @blk.sync
                    def _(s, e=e):
                        if e == 0:
                            S.wait(s, "vtp", S.val("vtp"))
                        sl = 64 * (e % 2)
                        S.use(s, ("sc0", e % 2))
                        S.inc(s.dma_start(scores0[sl:sl + 1, :],
                                          scoresF[e:e + 1, :]),
                              "dq", 16)
                    sc0_tok = S.val("dq")

                    if e + 1 < NE:
                        gc_base[e + 1] = S.val("gc")
                        emit_wdma_tracked(e + 1)
                        gcd[e + 1] = gc_base[e + 1] + 6
                    gc_done = gcd[e]
                    sc_ev_tok = {}

                    for tl in range(NT):
                        t0, t1 = tl * TT, (tl + 1) * TT
                        gu_tok = {}
                        vau_b = S.val("vau")

                        @blk.tensor
                        def _(t, e=e, tl=tl, t0=t0, t1=t1, gu_tok=gu_tok,
                              vau_b=vau_b, gc_done=gc_done):
                            if tl == 0:
                                S.wait(t, "gc", gc_done - 2)
                                if e == 0:
                                    S.wait(t, "vln", S.val("vln"))
                                    S.wait(t, "vev", S.val("vev"))
                            S.wait(t, "vau", vau_b)
                            for ci in range(IC):
                                pair = ci % 2
                                if ci >= 2:
                                    S.wait(t, "vau", vau_b + ci - 1)
                                for k in range(KC):
                                    mm = t.matmul(A[2 * pair][:, :], w_t(e, 0, k, ci),
                                                  xhat[:, k, t0:t1],
                                                  start=(k == 0), stop=(k == KC - 1))
                                g_tok = S.inc(mm, "tgu")
                                for k in range(KC):
                                    mm = t.matmul(A[2 * pair + 1][:, :],
                                                  w_t(e, 1, k, ci),
                                                  xhat[:, k, t0:t1],
                                                  start=(k == 0), stop=(k == KC - 1))
                                u_tok = S.inc(mm, "tgu")
                                gu_tok[ci] = (g_tok, u_tok)

                        s_tok = {}

                        @blk.scalar
                        def _(sc, gu_tok=gu_tok, s_tok=s_tok, vau_b=vau_b):
                            S.wait(sc, "vau", vau_b)
                            for ci in range(IC):
                                S.wait(sc, "tgu", gu_tok[ci][0])
                                ii = sc.activation(silu_s[:, ci, :],
                                                   A[2 * (ci % 2)][:, :], AF.Silu)
                                s_tok[ci] = S.inc(ii, "sact")

                        v_tok = {}
                        gact_b = S.val("gact")

                        @blk.vector
                        def _(v, e=e, tl=tl, gu_tok=gu_tok, s_tok=s_tok,
                              v_tok=v_tok, gact_b=gact_b):
                            S.wait(v, "gact", gact_b)
                            for ci in range(IC):
                                S.wait(v, "sact", s_tok[ci])
                                S.wait(v, "tgu", gu_tok[ci][1])
                                ii = v.tensor_tensor(tmp_bf[:, ci, :],
                                                     silu_s[:, ci, :],
                                                     A[2 * (ci % 2) + 1][:, :],
                                                     ALU.mult)
                                v_tok[ci] = S.inc(ii, "vau")

                        if tl == 0:
                            if pend is not None:
                                emit_w2_full(pend)
                                pend = None
                            for stl in range(NT):
                                @blk.tensor
                                def _(t, e=e, stl=stl, sc0_tok=sc0_tok):
                                    if stl < 2:
                                        S.use(t, ("Cp", 0))
                                    else:
                                        S.use(t, ("Cb", stl % 2))
                                    if stl == 0:
                                        S.wait(t, "dq", sc0_tok)
                                    sl = 64 * (e % 2)
                                    ii = t.matmul(
                                        C[stl % 2][:, :], oner2[sl:sl + 1, :],
                                        scores0[sl:sl + 1,
                                                stl * TT:(stl + 1) * TT],
                                        start=True, stop=True)
                                    tok = S.inc(ii, "tsc")
                                    if stl == NT - 1:
                                        S.release(("sc0", e % 2), "tsc", tok)

                                @blk.vector
                                def _(v, e=e, stl=stl):
                                    S.wait(v, "tsc", S.val("tsc"))
                                    S.use(v, ("scbc",))
                                    vv = S.inc(v.tensor_copy(scbc[:, stl, :],
                                                             C[stl % 2][:, :]),
                                               "vev")
                                    sc_ev_tok[stl] = vv
                                    S.release(("Cb", stl % 2), "vev", vv)
                                    if stl == NT - 1:
                                        S.release(("Cp", 0), "vev", vv)
                        else:
                            emit_w2_full(pend)
                            pend = None

                        @blk.vector
                        def _(v, e=e, tl=tl):
                            if e + 1 < NE:
                                for hcnt in CAST_SLICES[tl % 4] if NT == 4 \
                                        else ([0, 1, 2] if tl % 2 == 0 else [3, 4, 5]):
                                    emit_one_cast(v, e + 1, hcnt)

                        g_tok2 = {}

                        @blk.gpsimd
                        def _(g, e=e, tl=tl, v_tok=v_tok, g_tok2=g_tok2,
                              sc_ev_tok=sc_ev_tok):
                            for ci in range(IC):
                                S.use(g, ("act", ci))
                                if ci == 0:
                                    g.wait_ge(S.sem["vev"], sc_ev_tok[tl])
                                g.wait_ge(S.sem["vau"], v_tok[ci])
                                ii = g.tensor_tensor(act_s[:, ci, :],
                                                     tmp_bf[:, ci, :],
                                                     scbc[:, tl, :], ALU.mult)
                                g_tok2[ci] = S.inc(ii, "gact")

                        pend = (e, tl, t0, t1, g_tok2, gc_done)

                emit_w2_full(pend)
                pend = None
                if call == 1:
                    @blk.sync
                    def _(s):
                        for kc in range(KC):
                            for tl in range(NT):
                                S.use(s, ("Rs", kc, tl))
                            S.inc(s.dma_start(
                                outd.rearrange("(c p) l -> c p l", p=128)[kc],
                                R[:, kc, :]), "oq", 16)
            S.barrier_clear()

        def emit_attn():
            with ExitStack() as mem, nc.Block() as blk:
                psn = lambda name, shp, dt=F32: mem.enter_context(
                    nc.psum_tensor(f"{name}_at", shp, dt))
                st0 = psn("st0", [1, TT])
                st1 = psn("st1", [1, TT])
                st2 = psn("st2", [1, TT])
                st3 = psn("st3", [1, TT])
                bc0 = psn("bc0", [128, TT])
                bc1 = psn("bc1", [128, TT])
                x2ps = [psn(f"x2ps{i}", [128, TT]) for i in range(2)]

                sba = lambda name, shp, dt: mem.enter_context(
                    nc.sbuf_tensor(name, shp, dt))
                winbf = sba("winbf", [128, KC, D], BF16)
                wland = sba("wland_at", [128, LSZ // 2], F32)
                rows = sba("rows_at", [1, 7, TT], F32)
                x1t = sba("x1t", [128, 2, KC, TT], BF16)
                x2t = sba("x2t", [128, KC, TT], F32)
                x3t = sba("x3t", [128, KC, TT], F32)
                PAD = 16
                assert NTAP <= PAD
                x3bf = sba("x3bf", [128, KC, PAD + TT], BF16)
                cst = sba("cst", [128, TT], F32)
                xbf_q = sba("xbf_q", [128, 2, TT], BF16)
                xsq_q = sba("xsq_q", [128, 2, TT], BF16)
                carry = sba("carry", [128, KC], F32)
                diagall = sba("diagall", [128, KC, NTAP, 128], BF16)

                QN = max(1, D // (LSZ // 2 // KC)) if KC else 1
                qw = D // QN
                gc_now = S.val("gc")
                dwq_now = S.val("dwq")

                @blk.sync
                def _(s, gc_now=gc_now):
                    for qt in range(QN):
                        S.wait(s, "gc", gc_now + qt)
                        ap = winT[:, qt * qw:(qt + 1) * qw] \
                            .rearrange("(q p) n -> p q n", p=128)
                        dst = bass.AP(wland, 0,
                                      [[LSZ // 2, 128], [qw, KC], [1, qw]])
                        S.inc(s.dma_start(dst, ap), "dwq", 16)

                @blk.scalar
                def _(sc, dwq_now=dwq_now):
                    for qt in range(QN):
                        sc.wait_ge(S.sem["dwq"], dwq_now + 16 * (qt + 1))
                        srcw = bass.AP(wland, 0,
                                       [[LSZ // 2, 128], [qw, KC], [1, qw]])
                        S.inc(sc.activation(winbf[:, :, qt * qw:(qt + 1) * qw],
                                            srcw, AF.Copy), "gc")
                gc_win = S.val("gc")

                @blk.vector
                def _(v):
                    S.inc(v.memset(carry[:, :], 0.0), "vsc")
                    S.inc(v.memset(x3bf[:, :, 0:PAD], 0.0), "vsc")
                    prevv = None
                    for cc in range(KC):
                        for j in range(NTAP):
                            kcol = k0sb[:, cc, j:j + 1]
                            kbc = bass.AP(k0sb, kcol.offset,
                                          [list(kcol.ap[0]), [0, 128]])
                            prevv = v.tensor_tensor(diagall[:, cc, j, :],
                                                    idn[:, :], kbc, ALU.mult)
                    S.inc(prevv, "vtp")
                dg_tok = S.val("vtp")

                ln1_tok = {0: ln_stats_part1(blk, (lambda cc:
                                                   R[:, cc, 0:TT]),
                                             st2, st3, "stq", 1,
                                             bufs=(xbf_q, xsq_q))}
                apply_tok = {}

                def emit_ln1(tl):
                    ln_stats_part2(blk, rows, st2, st3, "stq", ln1_tok[tl])
                    ln_rows(blk, rows)
                    ln_bcast_tile(blk, rows, bc0, bc1)

                    @blk.vector
                    def _(v, tl=tl):
                        S.use(v, ("x1t", tl % 2))

                    ln_apply_tile(blk, (lambda cc, tl=tl:
                                        R[:, cc, tl * TT:(tl + 1) * TT]),
                                  (lambda cc, tl=tl: x1t[:, tl % 2, cc, :]),
                                  bc0, bc1)
                    apply_tok[tl] = S.val("vln")

                emit_ln1(0)
                for tl in range(NT):
                    t0, t1 = tl * TT, (tl + 1) * TT
                    vln_tile = apply_tok[tl]

                    x2tok = {}
                    for m in range(KC):
                        @blk.tensor
                        def _(t, tl=tl, m=m, vln_tile=vln_tile):
                            if m == 0:
                                S.wait(t, "vln", vln_tile)
                                if tl == 0:
                                    S.wait(t, "gc", gc_win)
                            S.use(t, ("x2ps", m % 2))
                            mm = None
                            for k in range(KC):
                                mm = t.matmul(x2ps[m % 2][:, :],
                                              winbf[:, k, m * 128:(m + 1) * 128],
                                              x1t[:, tl % 2, k, :],
                                              start=(k == 0), stop=(k == KC - 1))
                            tok = S.inc(mm, "tst")
                            if m == KC - 1:
                                S.release(("x1t", tl % 2), "tst", tok)
                            S.rel[("x2m", m)] = [("tst", tok)]

                        @blk.vector
                        def _(v, m=m, x2tok=x2tok):
                            S.use(v, ("x2m", m))
                            S.use(v, ("x2t", m))
                            ii = v.tensor_scalar(x2t[:, m, :], x2ps[m % 2][:, :],
                                                 prm[:, m:m + 1], None, ALU.add)
                            vv = S.inc(ii, "vmi")
                            x2tok[m] = vv
                            S.release(("x2ps", m % 2), "vmi", vv)

                    ln_stats_tile(blk, (lambda cc: x2t[:, cc, :]),
                                  rows, st0, st1,
                                  src_tok=(lambda cc, x2tok=x2tok:
                                           [("vmi", x2tok[cc])]))
                    ln_rows(blk, rows)
                    if tl + 1 < NT:
                        ln1_tok[tl + 1] = ln_stats_part1(
                            blk, (lambda cc, tl=tl:
                                  R[:, cc, (tl + 1) * TT:(tl + 2) * TT]),
                            st2, st3, "stq", 1, bufs=(xbf_q, xsq_q))
                    ln_bcast_tile(blk, rows, bc0, bc1)

                    @blk.vector
                    def _(v):
                        S.use(v, ("x3t",))

                    ln_apply_tile(blk, (lambda cc: x2t[:, cc, :]),
                                  (lambda cc: x3t[:, cc, :]),
                                  bc0, bc1, gb=True)
                    vln_x3 = S.val("vln")
                    for m in range(KC):
                        S.release(("x2t", m), "vln", vln_x3)

                    gmi_b = S.val("gmi")

                    @blk.gpsimd
                    def _(g, tl=tl, vln_x3=vln_x3, gmi_b=gmi_b):
                        g.wait_ge(S.sem["vln"], vln_x3)
                        if tl > 0:
                            g.wait_ge(S.sem["gmi"], gmi_b)
                        for cc in range(KC):
                            S.inc(g.tensor_copy(x3bf[:, cc, PAD:PAD + TT],
                                                x3t[:, cc, :]), "gmi")

                    if tl + 1 < NT:
                        emit_ln1(tl + 1)

                    x3rel = []
                    tap_tok = {}

                    def emit_evac(v, ec, tl=tl, t0=t0, t1=t1):
                        S.use(v, ("ytq", ec, tl))
                        S.use(v, ("Rs2", ec, tl))
                        ii = v.tensor_tensor(R[:, ec, t0:t1], R[:, ec, t0:t1],
                                             x2ps[ec % 2][:, :], ALU.add)
                        rv = S.inc(ii, "vev")
                        S.release(("x2ps", ec % 2), "vev", rv)
                        S.release(("Rs2", ec, tl), "vev", rv)

                    for cc in range(KC):
                        @blk.vector
                        def _(v, cc=cc, tl=tl, t0=t0, t1=t1, vln_x3=vln_x3,
                              x3rel=x3rel):
                            if cc == 0:
                                S.wait(v, "vln", vln_x3)
                            S.use(v, ("cst",))
                            S.use(v, ("carry", cc))
                            ii = v.tensor_tensor_scan(cst[:, :],
                                                      ones_t[:, :],
                                                      x3t[:, cc, :],
                                                      carry[:, cc:cc + 1],
                                                      ALU.mult, ALU.add)
                            vv = S.inc(ii, "vsc")
                            S.wait(v, "vsc", vv)
                            ii = v.tensor_copy(carry[:, cc:cc + 1],
                                               cst[:, TT - 1:TT])
                            cv = S.inc(ii, "vsc")
                            S.release(("carry", cc), "vsc", cv)
                            dcol = prm[:, 3 * KC + cc:3 * KC + cc + 1]
                            S.use(v, ("Rs2", cc, tl))
                            ii = v.scalar_tensor_tensor(R[:, cc, t0:t1],
                                                        cst[:, :], dcol,
                                                        R[:, cc, t0:t1],
                                                        ALU.mult, ALU.add)
                            rv = S.inc(ii, "vsc")
                            S.release(("cst",), "vsc", rv)
                            S.wait(v, "vsc", rv)
                            ii = v.tensor_tensor(R[:, cc, t0:t1], R[:, cc, t0:t1],
                                                 x3t[:, cc, :], ALU.add)
                            rv2 = S.inc(ii, "vsc")
                            S.release(("Rs2", cc, tl), "vsc", rv2)
                            x3rel.append(("vsc", rv2))
                            if cc >= 2:
                                emit_evac(v, cc - 2)

                        @blk.tensor
                        def _(t, cc=cc, tl=tl, t0=t0, t1=t1, gmi_b=gmi_b,
                              tap_tok=tap_tok):
                            if tl == 0 and cc == 0:
                                S.wait(t, "vtp", dg_tok)
                            S.wait(t, "gmi", gmi_b + cc + 1)
                            S.use(t, ("x2ps", cc % 2))
                            mm = None
                            for j in range(NTAP):
                                o0 = max(t0, j)
                                mm = t.matmul(x2ps[cc % 2][:, o0 - t0:TT],
                                              diagall[:, cc, j, :],
                                              x3bf[:, cc, PAD + o0 - t0 - j:
                                                   PAD + TT - j],
                                              start=(j == 0),
                                              stop=(j == NTAP - 1))
                            tok = S.inc(mm, "tw2")
                            tap_tok[cc] = tok
                            S.rel[("ytq", cc, tl)] = [("tw2", tok)]

                    @blk.vector
                    def _(v, tl=tl, t0=t0, t1=t1):
                        emit_evac(v, KC - 2)
                        emit_evac(v, KC - 1)

                    S.rel[("x3t",)] = x3rel + [("gmi", gmi_b + KC)]

                    if tl + 1 < NT:
                        @blk.gpsimd
                        def _(g, tap_tok=tap_tok):
                            g.wait_ge(S.sem["tw2"], tap_tok[KC - 1])
                            S.inc(g.tensor_copy(x3bf[:, :, 0:PAD],
                                                x3bf[:, :, TT:TT + PAD]), "gmi")
            S.barrier_clear()

        def dump(pairs):
            with nc.Block() as blk:
                @blk.sync
                def _(s):
                    for dst, srcb in pairs:
                        S.inc(s.dma_start(dst, srcb), "dq", 16)
                    S.wait(s, "dq")
            S.barrier_clear()

        emit_ffn(0)
        if c.dbg:
            dump([(d_R1, R[:, :, :]), (d_k0, k0sb[:, :, :])])
        emit_attn()
        if c.dbg:
            dump([(d_R2, R[:, :, :])])
        emit_ffn(1)

        with nc.Block() as blk:
            @blk.sync
            def _(s):
                S.wait(s, "oq")

    return nc


def _col_layout(v, KC):
    return np.ascontiguousarray(np.asarray(v, dtype=np.float32).reshape(KC, 128).T)


def make_core_inputs(c, x_b, W_in, b_in, gamma, beta, A_re, A_im, B_re, B_im,
                     C_re, C_im, D_re, gate_w, w1, w3, w2):
    KC = c.KC
    f32 = lambda a: np.ascontiguousarray(np.asarray(a, dtype=np.float32))
    return {
        "xT": f32(x_b.T),
        "w1": f32(w1), "w3": f32(w3), "w2": f32(w2),
        "gateT": f32(np.asarray(gate_w).T),
        "winT": f32(np.asarray(W_in).T),
        "bin_c": _col_layout(b_in, KC),
        "gam_c": _col_layout(gamma, KC),
        "bet_c": _col_layout(beta, KC),
        "dre_c": _col_layout(D_re, KC),
        "abcol": f32(np.stack([np.asarray(A_re), np.asarray(A_im),
                               np.asarray(B_re), np.asarray(B_im)], axis=1)),
        "ctre": f32(np.asarray(C_re).T),
        "ctim": f32(np.asarray(C_im).T),
        "onec_bf": np.ones((128, 1), dtype=ml_dtypes.bfloat16),
        "oner_bf": np.ones((1, 128), dtype=ml_dtypes.bfloat16),
        "oner_f": np.ones((1, 128), dtype=np.float32),
        "identd": np.eye(128, dtype=ml_dtypes.bfloat16),
    }


_NC_CACHE = {}


def kernel(L0_attn_x, L0_attn_W_in, L0_attn_b_in, L0_attn_gamma, L0_attn_beta,
           L0_attn_A_re, L0_attn_A_im, L0_attn_B_re, L0_attn_B_im,
           L0_attn_C_re, L0_attn_C_im, L0_attn_D_re, L0_attn_D_im,
           L0_ffn_gate_w, L0_ffn_w1, L0_ffn_w3, L0_ffn_w2):
    c = Cfg()
    if "nc" not in _NC_CACHE:
        _NC_CACHE["nc"] = build_nc(c)
    nc = _NC_CACHE["nc"]
    x = np.asarray(L0_attn_x, dtype=np.float32)
    nb = x.shape[0]
    base = make_core_inputs(c, x[0], L0_attn_W_in, L0_attn_b_in, L0_attn_gamma,
                            L0_attn_beta, L0_attn_A_re, L0_attn_A_im,
                            L0_attn_B_re, L0_attn_B_im, L0_attn_C_re,
                            L0_attn_C_im, L0_attn_D_re, L0_ffn_gate_w,
                            L0_ffn_w1, L0_ffn_w3, L0_ffn_w2)
    in_maps = []
    for b in range(nb):
        m = dict(base)
        m["xT"] = np.ascontiguousarray(x[b].T)
        in_maps.append(m)
    res = run_bass_kernel_spmd(nc, in_maps, core_ids=list(range(nb)))
    out = np.stack([np.asarray(res.results[b]["out"], dtype=np.float32).T
                    for b in range(nb)], axis=0)
    return np.ascontiguousarray(out)
